# version 13
# baseline (speedup 1.0000x reference)
import numpy as np
import concourse.bass as bass
import concourse.mybir as mybir
from concourse.bass_utils import run_bass_kernel_spmd

F32 = mybir.dt.float32
BF16 = mybir.dt.bfloat16
I32 = mybir.dt.int32
A = mybir.AluOpType
ACT = mybir.ActivationFunctionType
AX = mybir.AxisListType.X

C_RND = float(np.float32(12582912.0))
T = 4096
D = 1024
I3 = 3072
TTOK = 512
NT = T // TTOK
G = TTOK // 128
KC = D // 128
MC = I3 // 128
NCORES = 8
LN_EPS = 1e-5
QA = 32767.0


class Ch:
    def __init__(self, stack, nc, name):
        self.sem = stack.enter_context(nc.semaphore(name))
        self.n = 0
        self.rn = 0
        self.prefix = [(0, 0, -1)]

    def note(self, amt, kind, gvidx):
        self.n += amt
        c, r, g = self.prefix[-1]
        if kind == "r":
            self.rn += amt
            r = self.rn
        else:
            g = max(g, gvidx)
        self.prefix.append((self.n, r, g))

    def targets(self, val):
        import bisect
        i = bisect.bisect_left([p[0] for p in self.prefix], val)
        if i >= len(self.prefix):
            i = len(self.prefix) - 1
        c, r, g = self.prefix[i]
        return r, (g + 1 if g >= 0 else 0)


class B_:
    def __init__(self, nc, stack):
        self.nc = nc
        self.stack = stack
        self.ops = {k: [] for k in ("sync", "scalar", "vector", "tensor", "gpsimd")}
        self.gv = stack.enter_context(nc.semaphore("c_gv"))
        self.gv_n = 0

    def ch(self, name):
        return Ch(self.stack, self.nc, name)

    def em(self, eng, f, inc=None, amt=1):
        if eng == "vector":
            idx = self.gv_n
            self.gv_n += 1
            gsem = self.gv
            if inc is not None:
                inc.note(amt, "v", idx)

            def g(e, f=f, idx=idx, gsem=gsem):
                if idx > 0:
                    e.wait_ge(gsem, idx)
                f(e).then_inc(gsem, 1)
            self.ops[eng].append(g)
            return
        if inc is not None:
            inc.note(amt, "r", None)
            self.ops[eng].append(lambda e, f=f, s=inc.sem, a=amt: f(e).then_inc(s, a))
        else:
            self.ops[eng].append(f)

    def wait(self, eng, ch, val=None):
        v = ch.n if val is None else val
        if v <= 0:
            return
        rt, gt = ch.targets(v)
        if rt > 0:
            self.ops[eng].append(lambda e, s=ch.sem, v=rt: e.wait_ge(s, v))
        if gt > 0:
            self.ops[eng].append(lambda e, s=self.gv, v=gt: e.wait_ge(s, v))


def build_program(debug=False):
    from contextlib import ExitStack
    nc = bass.Bass("TRN2", target_bir_lowering=False, debug=False)
    gstack = ExitStack()
    b = B_(nc, gstack)

    def sb(st, name, shape, dt=F32):
        return st.enter_context(nc.sbuf_tensor(name, shape, dt))

    scratch_kind = {"kind": "ExternalOutput"} if debug else {}

    x_in = nc.dram_tensor("x", [T, D], F32, kind="ExternalInput").ap()
    w_rkv = nc.dram_tensor("w_rkv", [D, 3 * D], F32, kind="ExternalInput").ap()
    b_rkv = nc.dram_tensor("b_rkv", [3 * D], F32, kind="ExternalInput").ap()
    w_o = nc.dram_tensor("w_o", [D, D], F32, kind="ExternalInput").ap()
    b_o = nc.dram_tensor("b_o", [D], F32, kind="ExternalInput").ap()
    nu_log = nc.dram_tensor("nu_log", [D], F32, kind="ExternalInput").ap()
    gamma_log = nc.dram_tensor("gamma_log", [D], F32, kind="ExternalInput").ap()
    w_gate = nc.dram_tensor("w_gate", [I3, D], F32, kind="ExternalInput").ap()
    w_up = nc.dram_tensor("w_up", [I3, D], F32, kind="ExternalInput").ap()
    w_down = nc.dram_tensor("w_down", [D, I3], F32, kind="ExternalInput").ap()
    out = nc.dram_tensor("out", [T, D], F32, kind="ExternalOutput").ap()

    x1_hbm = nc.dram_tensor("x1_hbm", [T, D], F32, **scratch_kind).ap()
    up_hbm = nc.dram_tensor("up_hbm", [I3, T], F32, **scratch_kind).ap()
    gt_hbm = nc.dram_tensor("gt_hbm", [I3, T], F32, **scratch_kind).ap()
    g2_hbm = nc.dram_tensor("g2_hbm", [I3, T], F32, **scratch_kind).ap()
    o_hbm = nc.dram_tensor("o_hbm", [I3, T], F32, **scratch_kind).ap()
    wup_bf = nc.dram_tensor("wup_bf", [I3, D], BF16).ap()
    wgt_bf = nc.dram_tensor("wgt_bf", [I3, D], BF16).ap()
    wdn_bf = nc.dram_tensor("wdn_bf", [D, I3], BF16).ap()
    cc_bufs = {}
    for nm, n in [("cc1", 128), ("cc2", 256), ("cc3", 128), ("cc4", 128)]:
        cc_bufs[nm] = (nc.dram_tensor(nm + "_in", [1, n], F32).ap(),
                       nc.dram_tensor(nm + "_out", [1, n], F32).ap())
    scl_hbm = nc.dram_tensor("scl_hbm", [1, 8], F32).ap()

    ident = sb(gstack, "ident", [128, 128], BF16)
    iotat = sb(gstack, "iotat", [128, 128], F32)
    ones_bf = sb(gstack, "ones_bf", [1, 128], BF16)
    bo_bf = sb(gstack, "bo_bf", [1, D], BF16)
    brkv_t = sb(gstack, "brkv_t", [128, 3 * KC], F32)
    nul_t = sb(gstack, "nul_t", [128, KC], F32)
    gml_t = sb(gstack, "gml_t", [128, KC], F32)
    lam_t = sb(gstack, "lam_t", [128, KC], F32)
    gam_t = sb(gstack, "gam_t", [128, KC], F32)
    brkv_h = sb(gstack, "brkv_h", [128, KC], F32)

    x_tm = [sb(gstack, f"x_tm{i}", [128, G * D], F32) for i in range(2)]
    yn_g = [sb(gstack, f"yn{i}", [128, TTOK], F32) for i in range(2)]
    xn2_g = sb(gstack, "xn2_g", [128, D], F32)
    xn2_g2 = sb(gstack, "xn2_g2", [128, D], F32)

    bnst = sb(gstack, "bnst", [128, 48], F32)
    stat_m = sb(gstack, "stat_m", [128, G], F32)
    stat_t0 = sb(gstack, "stat_t0", [128, G], F32)
    stat_v = sb(gstack, "stat_v", [128, G], F32)
    stat_sq = sb(gstack, "stat_sq", [128, 16], F32)
    stat_ss = sb(gstack, "stat_ss", [128, G], F32)
    stat_std = sb(gstack, "stat_std", [128, G], F32)
    rv1 = sb(gstack, "rv1", [128, G], F32)
    bb1 = sb(gstack, "bb1", [128, G], F32)
    rv2 = sb(gstack, "rv2", [128, 1], F32)
    bb2 = sb(gstack, "bb2", [128, 1], F32)
    rv4 = sb(gstack, "rv4", [128, 1], F32)
    bb4 = sb(gstack, "bb4", [128, 1], F32)
    rv3_all = sb(gstack, "rv3_all", [128, NT * G], F32)
    b3_all = sb(gstack, "b3_all", [128, NT * G], F32)

    sxacc = sb(gstack, "sxacc", [128, 1], F32)
    sx_slots = sb(gstack, "sx_slots", [128, NT * G], F32)
    acc_slots = [sb(gstack, f"accsl{i}", [128, MC * NT], F32) for i in range(2)]
    uacc = sb(gstack, "uacc", [128, 1], F32)
    gacc = sb(gstack, "gacc", [128, 1], F32)
    g2acc = sb(gstack, "g2acc", [128, 1], F32)
    oacc = sb(gstack, "oacc", [128, 1], F32)
    redt = sb(gstack, "redt", [128, 1], F32)
    sx_b = sb(gstack, "sx_b", [128, 2], F32)
    su_b = sb(gstack, "su_b", [128, 2], F32)
    sg_b = sb(gstack, "sg_b", [128, 2], F32)
    sg2_b = sb(gstack, "sg2_b", [128, 2], F32)
    so_b = sb(gstack, "so_b", [128, 2], F32)
    ccs = sb(gstack, "ccs", [1, 256], F32)
    svals = sb(gstack, "svals", [1, 4], F32)
    gx1 = sb(gstack, "gx1", [1, 1], F32)
    gx2 = sb(gstack, "gx2", [1, 1], F32)
    fence_v = sb(gstack, "fence_v", [1, 2], F32)
    fence_s = sb(gstack, "fence_s", [1, 2], F32)
    eps_t = sb(gstack, "eps_t", [128, 1], F32)

    WPC = 512
    WPD = 4
    wp_in = [sb(gstack, f"wp_in{i}", [128, WPC], F32) for i in range(WPD)]
    wp_bmax = sb(gstack, "wp_bmax", [128, 16], F32)
    wp_sb = sb(gstack, "wp_sb", [128, 16], F32)
    wp_isb = sb(gstack, "wp_isb", [128, 16], F32)
    wp_t1 = sb(gstack, "wp_t1", [128, WPC], F32)
    wp_i32 = sb(gstack, "wp_i32", [128, WPC], I32)
    wp_out = [sb(gstack, f"wp_out{i}", [128, WPC], BF16) for i in range(2)]

    ptp = [gstack.enter_context(nc.psum_tensor("ptp_a", [128, 1024], BF16)),
           gstack.enter_context(nc.psum_tensor("ptp_b", [128, 1024], BF16))]
    pmm = [gstack.enter_context(nc.psum_tensor(f"pmm{i}", [128, TTOK], F32)) for i in range(6)]

    names = ("c_setup c_cast c_tiny c_xload c_xfree c_ln1 c_xn c_tp c_tpe c_rkv "
             "c_kev c_rev c_vev c_xat c_wo c_ln2 c_yn c_x1 c_x1s c_ln3 c_xn2 c_sx "
             "c_cc c_bdma c_bv c_wpl c_wpg c_wpv c_wps c_wt c_xq c_mm c_mev c_mred "
             "c_mst c_fl c_f1 c_f2 c_f3 c_fsg c_uxf c_uq c_oq c_oql c_dmm c_ln4 "
             "c_yn4 c_xo c_out c_vg c_iota c_x1s2 c_mst2 c_f32 c_out2 c_xload2 c_fl2 c_oql2 c_wpl2 c_wps2 c_woh "
             "c_wpl3 c_wpl4 c_wt2 c_mst3 c_mst4").split()
    C = {nm: b.ch(nm) for nm in names}
    (c_setup, c_cast, c_tiny, c_xload, c_xfree, c_ln1, c_xn, c_tp, c_tpe, c_rkv,
     c_kev, c_rev, c_vev, c_xat, c_wo, c_ln2, c_yn, c_x1, c_x1s, c_ln3, c_xn2, c_sx,
     c_cc, c_bdma, c_bv, c_wpl, c_wpg, c_wpv, c_wps, c_wt, c_xq, c_mm, c_mev, c_mred,
     c_mst, c_fl, c_f1, c_f2, c_f3, c_fsg, c_uxf, c_uq, c_oq, c_oql, c_dmm, c_ln4,
     c_yn4, c_xo, c_out, c_vg, c_iota, c_x1s2, c_mst2, c_f32, c_out2, c_xload2, c_fl2, c_oql2, c_wpl2, c_wps2, c_woh,
     c_wpl3, c_wpl4, c_wt2, c_mst3, c_mst4) = (C[nm] for nm in names)

    c_dbgp = b.ch("c_dbgp")
    c_dbgs = b.ch("c_dbgs")
    if debug:
        dbg1 = nc.dram_tensor("dbg1", [128, 4096], F32, kind="ExternalOutput").ap()
        dbgb = nc.dram_tensor("dbgb", [128, 2048], BF16, kind="ExternalOutput").ap()
        dbgst1 = sb(gstack, "dbgst1", [128, 4096], F32)
        dbgstb = sb(gstack, "dbgstb", [128, 2048], BF16)

    def dsnap(eng, src_fn, dstname, col, w, wait_ch=None):
        if not debug:
            return
        stg = dbgst1 if dstname == "dbg1" else dbgstb
        if wait_ch is not None:
            b.wait("vector", wait_ch)
        b.em("vector", lambda e: e.tensor_copy(stg[:, col:col + w], src_fn()), inc=c_dbgp)

    def dsnap_finish():
        if not debug:
            return
        b.wait("sync", c_dbgp)
        b.em("sync", lambda e: e.dma_start(out=dbg1[:, :], in_=dbgst1[:, :]), inc=c_dbgs, amt=16)
        b.em("sync", lambda e: e.dma_start(out=dbgb[:, :], in_=dbgstb[:, :]), inc=c_dbgs, amt=16)
        b.wait("sync", c_dbgs)

    b.em("gpsimd", lambda e: e.iota(iotat[:, :], [[1, 128]], base=0, channel_multiplier=-1,
                                    allow_small_or_imprecise_dtypes=True), inc=c_iota, amt=1)
    b.em("sync", lambda e: e.dma_start(out=brkv_t[:, :],
                                       in_=b_rkv.rearrange("(c p) -> p c", p=128), allow_slow_non_contiguous=True), inc=c_setup, amt=16)
    b.em("sync", lambda e: e.dma_start(out=nul_t[:, :],
                                       in_=nu_log.rearrange("(c p) -> p c", p=128), allow_slow_non_contiguous=True), inc=c_setup, amt=16)
    b.em("sync", lambda e: e.dma_start(out=gml_t[:, :],
                                       in_=gamma_log.rearrange("(c p) -> p c", p=128), allow_slow_non_contiguous=True), inc=c_setup, amt=16)
    SETUP_DONE = c_setup.n

    b.wait("vector", c_iota, 1)
    b.em("vector", lambda e: e.tensor_scalar(ident[:, :], iotat[:, :], 0.0, None, A.is_equal))
    b.em("vector", lambda e: e.memset(ones_bf[0:1, :], 1.0))
    b.em("vector", lambda e: e.memset(eps_t[:, :], float(LN_EPS)))
    b.em("vector", lambda e: e.memset(fence_v[0:1, :], 0.0))
    b.em("vector", lambda e: e.memset(fence_s[0:1, :], 0.0))
    if debug:
        b.em("vector", lambda e: e.memset(dbgst1[:, :], 0.0))
        b.em("vector", lambda e: e.memset(dbgstb[:, :], 0.0))
    for z in (sxacc, uacc, gacc, g2acc, oacc):
        b.em("vector", lambda e, z=z: e.memset(z[:, :], 0.0))
    b.em("vector", lambda e: e.memset(redt[:, :], 0.0), inc=c_tiny)
    b.wait("scalar", c_setup, SETUP_DONE)
    b.wait("scalar", c_tiny, 1)
    b.em("scalar", lambda e: e.activation(lam_t[:, :], nul_t[:, :], ACT.Exp), inc=c_tiny)
    b.wait("vector", c_tiny, 2)
    b.em("vector", lambda e: e.tensor_tensor(gam_t[:, :], nul_t[:, :], gml_t[:, :], A.add), inc=c_tiny)
    b.wait("scalar", c_tiny, 3)
    b.em("scalar", lambda e: e.activation(gam_t[:, :], gam_t[:, :], ACT.Exp), inc=c_tiny)
    b.em("scalar", lambda e: e.activation(lam_t[:, :], lam_t[:, :], ACT.Exp, scale=-1.0), inc=c_tiny)
    b.wait("vector", c_tiny, 5)
    b.em("vector", lambda e: e.tensor_scalar(gam_t[:, :], gam_t[:, :], -1.0, None, A.add))
    b.wait("vector", c_setup, SETUP_DONE)
    b.em("vector", lambda e: e.tensor_scalar(brkv_h[:, :], brkv_t[:, 0:KC], 0.5, None, A.mult), inc=c_tiny)
    TINY_DONE = c_tiny.n

    def ln_stats(chan, srcs, rv_dst, b_dst, col0=0):
        ng = len(srcs)
        kk = 4
        for g, halves in enumerate(srcs):
            for h, (tt_, cc_) in enumerate(halves):
                last = (g == ng - 1) and (h == 1)
                b.em("vector", lambda e, g=g, h=h, tt_=tt_, cc_=cc_: e.bn_stats(
                    bnst[:, 12 * g + 6 * h:12 * g + 6 * h + 6], tt_[:, cc_:cc_ + 512]),
                    inc=(chan if last else None))
        b.wait("vector", chan)
        b.em("vector", lambda e, ng=ng: e.tensor_reduce(
            stat_m[:, 0:ng], bnst[:, 1:12 * ng:3].rearrange("p (g k) -> p g k", k=kk), AX, A.add))
        b.em("vector", lambda e, ng=ng: e.tensor_reduce(
            stat_t0[:, 0:ng], bnst[:, 2:12 * ng:3].rearrange("p (g k) -> p g k", k=kk), AX, A.add))
        b.em("vector", lambda e, ng=ng: e.tensor_tensor(
            stat_sq[:, 0:kk * ng], bnst[:, 1:12 * ng:3], bnst[:, 1:12 * ng:3], A.mult))
        b.em("vector", lambda e, ng=ng: e.tensor_reduce(
            stat_ss[:, 0:ng], stat_sq[:, 0:kk * ng].rearrange("p (g k) -> p g k", k=kk), AX, A.add),
            inc=chan)
        b.wait("vector", chan)
        b.em("vector", lambda e, ng=ng: e.tensor_scalar(
            stat_m[:, 0:ng], stat_m[:, 0:ng], 0.25, None, A.mult))
        b.em("vector", lambda e, ng=ng: e.tensor_scalar(
            stat_v[:, 0:ng], stat_t0[:, 0:ng], 1.0 / 1024.0, None, A.mult))
        b.em("vector", lambda e, ng=ng: e.scalar_tensor_tensor(
            stat_v[:, 0:ng], stat_ss[:, 0:ng], 0.25, stat_v[:, 0:ng], A.mult, A.add))
        b.em("vector", lambda e, ng=ng: e.tensor_tensor(
            stat_sq[:, 0:ng], stat_m[:, 0:ng], stat_m[:, 0:ng], A.mult))
        b.em("vector", lambda e, ng=ng: e.tensor_tensor(
            stat_v[:, 0:ng], stat_v[:, 0:ng], stat_sq[:, 0:ng], A.subtract), inc=chan)
        b.wait("scalar", chan)
        b.em("scalar", lambda e, ng=ng: e.activation(
            stat_std[:, 0:ng], stat_v[:, 0:ng], ACT.Sqrt, bias=eps_t[:, 0:1]), inc=chan)
        b.wait("vector", chan)
        b.em("vector", lambda e, ng=ng: e.reciprocal(stat_t0[:, 0:ng], stat_std[:, 0:ng]),
             inc=chan)
        b.wait("vector", chan)
        b.em("vector", lambda e, ng=ng: e.tensor_tensor(
            stat_sq[:, 0:ng], stat_t0[:, 0:ng], stat_t0[:, 0:ng], A.mult))
        b.em("vector", lambda e, ng=ng: e.tensor_scalar(
            stat_ss[:, 0:ng], stat_v[:, 0:ng], 0.5, float(0.5 * LN_EPS), A.mult, A.add))
        b.em("vector", lambda e, ng=ng: e.tensor_tensor(
            stat_ss[:, 0:ng], stat_ss[:, 0:ng], stat_sq[:, 0:ng], A.mult))
        b.em("vector", lambda e, ng=ng: e.tensor_scalar(
            stat_ss[:, 0:ng], stat_ss[:, 0:ng], -1.0, 1.5, A.mult, A.add))
        b.em("vector", lambda e, ng=ng, rv_dst=rv_dst, col0=col0: e.tensor_tensor(
            rv_dst[:, col0:col0 + ng], stat_t0[:, 0:ng], stat_ss[:, 0:ng], A.mult))
        b.em("vector", lambda e, ng=ng, rv_dst=rv_dst, b_dst=b_dst, col0=col0: e.scalar_tensor_tensor(
            b_dst[:, col0:col0 + ng], stat_m[:, 0:ng], -1.0,
            rv_dst[:, col0:col0 + ng], A.mult, A.mult), inc=chan)

    def barrier(cc_name, accs, out_tiles, qmaxs):
        n = len(accs)
        cin, cout = cc_bufs[cc_name]
        for i, acc in enumerate(accs):
            b.em("sync", lambda e, i=i, acc=acc, cin=cin: e.dma_start(
                out=cin[0:1, 128 * i:128 * (i + 1)], in_=acc[:, 0:1]), inc=c_bdma, amt=16)
        b.wait("gpsimd", c_bdma)
        b.em("gpsimd", lambda e, cin=cin, cout=cout: e.collective_compute(
            "AllReduce", A.max, replica_groups=[list(range(NCORES))],
            ins=[cin[:, :]], outs=[cout[:, :]]), inc=c_cc)
        b.wait("sync", c_cc)
        b.em("sync", lambda e, cout=cout, n=n: e.dma_start(
            out=ccs[0:1, 0:128 * n], in_=cout[:, :]), inc=c_bdma, amt=16)
        b.wait("vector", c_bdma)
        for i in range(n):
            q = qmaxs[i]
            b.em("vector", lambda e, i=i: e.tensor_reduce(
                gx1[0:1, 0:1], ccs[0:1, 128 * i:128 * (i + 1)], AX, A.max), inc=c_bv)
            b.wait("vector", c_bv)
            b.em("vector", lambda e, q=q: e.tensor_scalar(
                gx2[0:1, 0:1], gx1[0:1, 0:1], 1.0 / q, 1e-12, A.mult, A.max))
            b.em("vector", lambda e, i=i: e.tensor_copy(svals[0:1, 2 * i:2 * i + 1], gx2[0:1, 0:1]))
            b.em("vector", lambda e, i=i: e.reciprocal(svals[0:1, 2 * i + 1:2 * i + 2], gx2[0:1, 0:1]))
        b.em("vector", lambda e: e.tensor_copy(fence_v[0:1, 0:1], fence_v[0:1, 1:2]), inc=c_bv)
        b.wait("sync", c_bv)
        b.em("sync", lambda e, n=n: e.dma_start(out=scl_hbm[0:1, 0:2 * n],
                                                in_=svals[0:1, 0:2 * n]), inc=c_bdma, amt=16)
        b.wait("sync", c_bdma)
        for i in range(n):
            b.em("sync", lambda e, i=i, ot=out_tiles[i]: e.dma_start(
                out=ot[:, :], in_=scl_hbm[0:1, 2 * i:2 * i + 2].broadcast_to((128, 2))),
                inc=c_bdma, amt=16)

    NB = WPC // 32
    wp_vend = {}

    def make_prep(w_ap, whbm_bf, r, col0, idx):
        wch = (c_wpl, c_wpl2, c_wpl3, c_wpl4)[idx % WPD]

        def load():
            if idx >= WPD:
                b.wait("sync", c_wpv, wp_vend[idx - WPD])
            b.em("sync", lambda e: e.dma_start(
                out=wp_in[idx % WPD][:, :], in_=w_ap[r * 128:(r + 1) * 128, col0:col0 + WPC]),
                inc=wch, amt=16)

        def proc():
            win = wp_in[idx % WPD]
            wout = wp_out[idx % 2]
            b.wait("vector", wch)
            b.em("vector", lambda e, win=win: e.tensor_reduce(
                wp_bmax[:, 0:NB], win[:, :].rearrange("p (n k) -> p n k", k=32),
                AX, A.max, apply_absolute_value=True))
            b.em("vector", lambda e: e.tensor_scalar(
                wp_sb[:, 0:NB], wp_bmax[:, 0:NB], 1.0 / 7.0, 1e-12, A.mult, A.max))
            b.em("vector", lambda e: e.reciprocal(wp_isb[:, 0:NB], wp_sb[:, 0:NB]))
            b.em("vector", lambda e, win=win: e.tensor_tensor(
                wp_t1[:, :].rearrange("p (n k) -> p n k", k=32),
                win[:, :].rearrange("p (n k) -> p n k", k=32),
                wp_isb[:, 0:NB].rearrange("p (n o) -> p n o", o=1).broadcast_to((128, NB, 32)),
                A.mult))
            b.em("vector", lambda e: e.tensor_scalar(
                wp_t1[:, :], wp_t1[:, :], C_RND, -C_RND, A.add, A.add))
            b.em("vector", lambda e: e.tensor_copy(wp_i32[:, :], wp_t1[:, :]))
            b.em("vector", lambda e: e.tensor_copy(wp_t1[:, :], wp_i32[:, :]))
            if idx >= 2:
                b.wait("vector", c_wps if idx % 2 == 0 else c_wps2)
            b.em("vector", lambda e, wout=wout: e.tensor_tensor(
                wout[:, :].rearrange("p (n k) -> p n k", k=32),
                wp_t1[:, :].rearrange("p (n k) -> p n k", k=32),
                wp_sb[:, 0:NB].rearrange("p (n o) -> p n o", o=1).broadcast_to((128, NB, 32)),
                A.mult), inc=c_wpv)
            wp_vend[idx] = c_wpv.n
            b.wait("sync", c_wpv, wp_vend[idx])
            b.em("sync", lambda e, wout=wout: e.dma_start(
                out=whbm_bf[r * 128:(r + 1) * 128, col0:col0 + WPC], in_=wout[:, :]),
                inc=(c_wps if idx % 2 == 0 else c_wps2), amt=16)
        return load, proc

    prep_steps = []
    _idx = 0
    for w_ap, whbm in ((w_up, wup_bf), (w_gate, wgt_bf)):
        for r in range(MC):
            for cchunk in range(D // WPC):
                prep_steps.append(make_prep(w_ap, whbm, r, cchunk * WPC, _idx))
                _idx += 1
    WPS_UP = 16 * (MC * (D // WPC)) // 2
    WPS_GT = 2 * WPS_UP
    for r in range(KC):
        for cchunk in range(I3 // WPC):
            prep_steps.append(make_prep(w_down, wdn_bf, r, cchunk * WPC, _idx))
            _idx += 1
    WPS_DN = 16 * _idx // 2
    N_PREP = _idx
    prep_pos = [0]

    def do_prep(n):
        for _ in range(n):
            s = prep_pos[0]
            if s > N_PREP:
                return
            if s == 0:
                prep_steps[0][0]()
            if s + 1 < N_PREP:
                prep_steps[s + 1][0]()
            if s >= 1:
                prep_steps[s - 1][1]()
            prep_pos[0] = s + 1

    astack = ExitStack()
    wrkv_sb = [sb(astack, f"wrkv{k}", [128, 3 * D], BF16) for k in range(KC)]
    wo_sb = [sb(astack, f"wo{k}", [128, D], BF16) for k in range(KC)]
    xn_tm = [sb(astack, f"xn_tm{g}", [128, D], BF16) for g in range(G)]
    xn_fm = [sb(astack, f"xn_fm{f}", [128, TTOK], BF16) for f in range(KC)]
    k_c = [sb(astack, f"k_c{i}", [128, TTOK], F32) for i in range(2)]
    r_c = [sb(astack, f"r_c{i}", [128, TTOK], F32) for i in range(2)]
    kv_c = [sb(astack, f"kv_c{i}", [128, TTOK], F32) for i in range(2)]
    y1_c = sb(astack, "y1_c", [128, TTOK], F32)
    y2_c = sb(astack, "y2_c", [128, TTOK], F32)
    num_c = sb(astack, "num_c", [128, TTOK], F32)
    den_c = sb(astack, "den_c", [128, TTOK], F32)
    rec_c = sb(astack, "rec_c", [128, TTOK], F32)
    xatt_fm = [sb(astack, f"xatt{f}", [128, TTOK], BF16) for f in range(KC)]
    st1 = sb(astack, "st1", [128, KC], F32)
    st2 = sb(astack, "st2", [128, KC], F32)
    x1_g = [sb(astack, f"x1g{i}", [128, D], F32) for i in range(2)]

    for k in range(KC):
        b.em("gpsimd", lambda e, k=k: e.dma_start(out=wrkv_sb[k][:, :],
                                                  in_=w_rkv[k * 128:(k + 1) * 128, :]), inc=c_cast, amt=16)
        b.em("gpsimd", lambda e, k=k: e.dma_start(out=wo_sb[k][:, :],
                                                  in_=w_o[k * 128:(k + 1) * 128, :]), inc=c_cast, amt=16)
    b.em("gpsimd", lambda e: e.dma_start(out=bo_bf[0:1, :], in_=b_o[None, :]), inc=c_cast, amt=16)
    CAST_DONE = c_cast.n

    b.wait("vector", c_cast, CAST_DONE)
    for k in range(KC):
        b.em("vector", lambda e, k=k: e.tensor_scalar(
            wo_sb[k][:, :], wo_sb[k][:, :], 0.5, None, A.mult),
            inc=(c_woh if k == KC - 1 else None))

    b.wait("vector", c_tiny, 1)
    b.em("vector", lambda e: e.memset(st1[:, :], 0.0))
    b.em("vector", lambda e: e.memset(st2[:, :], 0.0), inc=c_tiny)
    TINY_DONE = c_tiny.n

    b.wait("tensor", c_cast, CAST_DONE)
    b.wait("tensor", c_woh)
    b.wait("tensor", c_tiny, 1)
    b.wait("vector", c_tiny, TINY_DONE)
    b.wait("scalar", c_tiny, TINY_DONE)
    b.wait("scalar", c_setup, SETUP_DONE)

    for t in range(NT):
        xb = x_tm[t % 2]
        if t >= 2:
            b.wait("sync", c_xfree, t - 1)
        for g in range(G):
            b.em("sync", lambda e, t=t, g=g, xb=xb: e.dma_start(
                out=xb[:, g * D:(g + 1) * D],
                in_=x_in[t * TTOK + g * 128: t * TTOK + (g + 1) * 128, :]),
                inc=(c_xload if t % 2 == 0 else c_xload2), amt=16)
        b.wait("vector", c_xload if t % 2 == 0 else c_xload2)
        ln_stats(c_ln1, [[(xb, g * D), (xb, g * D + 512)] for g in range(G)], rv1, bb1)
        do_prep(6)
        if t == 0:
            dsnap("vector", lambda: stat_v[:, 0:4], "dbg1", 0, 4)
            dsnap("vector", lambda: rv1[:, 0:4], "dbg1", 4, 4)
            dsnap("vector", lambda: bb1[:, 0:4], "dbg1", 8, 4)
        b.wait("scalar", c_ln1)
        if t >= 1:
            b.wait("scalar", c_tp)
        for g in range(G):
            b.em("scalar", lambda e, g=g, xb=xb: e.activation(
                xn_tm[g][:, :], xb[:, g * D:(g + 1) * D], ACT.Identity,
                bias=bb1[:, g:g + 1], scale=rv1[:, g:g + 1]), inc=c_xn)
        if t == 0:
            dsnap("vector", lambda: xn_tm[0][:, :], "dbgb", 0, 1024, wait_ch=c_xn)
        b.wait("tensor", c_xn)
        for f in range(KC):
            if c_tp.n >= 2:
                b.wait("tensor", c_tpe, c_tp.n - 1)
            for g in range(G):
                b.em("tensor", lambda e, f=f, g=g: e.transpose(
                    ptp[f % 2][:, g * 128:(g + 1) * 128], xn_tm[g][:, f * 128:(f + 1) * 128],
                    ident[:, :]), inc=(c_tp if g == G - 1 else None))
            b.wait("scalar", c_tp)
            if t >= 1 and f == 0:
                b.wait("scalar", c_rkv)
            b.em("scalar", lambda e, f=f: e.activation(
                xn_fm[f][:, :], ptp[f % 2][:, 0:TTOK], ACT.Identity), inc=c_tpe)
        if t == 0:
            dsnap("vector", lambda: xn_fm[0][:, :], "dbgb", 1024, 512, wait_ch=c_tpe)
        b.wait("tensor", c_tpe)
        if t >= 1:
            b.wait("vector", c_wo)
            b.wait("tensor", c_yn)
        for c in range(KC):
            kb, rb, vb = k_c[c % 2], r_c[c % 2], kv_c[c % 2]
            pk, pv, pr = pmm[c % 2], pmm[2 + c % 2], pmm[4 + c % 2]
            b.wait("tensor", c_kev, c_kev.n - 1)
            b.wait("tensor", c_vev, c_vev.n - 1)
            b.wait("tensor", c_rev, c_rev.n - 1)
            for kk in range(KC):
                b.em("tensor", lambda e, c=c, kk=kk, pk=pk: e.matmul(
                    pk[:, :], wrkv_sb[kk][:, (8 + c) * 128:(9 + c) * 128], xn_fm[kk][:, :],
                    start=(kk == 0), stop=(kk == KC - 1)), inc=(c_rkv if kk == KC - 1 else None))
            for kk in range(KC):
                b.em("tensor", lambda e, c=c, kk=kk, pv=pv: e.matmul(
                    pv[:, :], wrkv_sb[kk][:, (16 + c) * 128:(17 + c) * 128], xn_fm[kk][:, :],
                    start=(kk == 0), stop=(kk == KC - 1)), inc=(c_rkv if kk == KC - 1 else None))
            for kk in range(KC):
                b.em("tensor", lambda e, c=c, kk=kk, pr=pr: e.matmul(
                    pr[:, :], wrkv_sb[kk][:, c * 128:(c + 1) * 128], xn_fm[kk][:, :],
                    start=(kk == 0), stop=(kk == KC - 1)), inc=(c_rkv if kk == KC - 1 else None))
            b.wait("scalar", c_rkv, c_rkv.n - 2)
            if t * KC + c >= 2:
                b.wait("scalar", c_xat, c_xat.n - 1)
            b.em("scalar", lambda e, c=c, kb=kb, pk=pk: e.activation(
                kb[:, :], pk[:, :], ACT.Exp, bias=brkv_t[:, 8 + c:9 + c]), inc=c_kev)
            b.wait("scalar", c_rkv, c_rkv.n)
            b.em("scalar", lambda e, c=c, rb=rb, pr=pr: e.activation(
                rb[:, :], pr[:, :], ACT.Tanh, scale=0.5, bias=brkv_h[:, c:c + 1]), inc=c_rev)
            if t == 0 and c == 0:
                dsnap("vector", lambda: k_c[0][:, :], "dbg1", 512, 512, wait_ch=c_kev)
                dsnap("vector", lambda: r_c[0][:, :], "dbg1", 1024, 512, wait_ch=c_rev)
            b.wait("vector", c_rkv, c_rkv.n - 1)
            b.wait("vector", c_kev)
            b.em("vector", lambda e, c=c, kb=kb, vb=vb, pv=pv: e.scalar_tensor_tensor(
                vb[:, :], pv[:, :], brkv_t[:, 16 + c:17 + c], kb[:, :], A.add, A.mult), inc=c_vev)
            if t == 0 and c == 0:
                dsnap("vector", lambda: kv_c[0][:, :], "dbg1", 1536, 512)
            b.em("vector", lambda e, c=c, vb=vb: e.tensor_tensor_scan(
                y1_c[:, :], lam_t[:, c:c + 1].broadcast_to((128, TTOK)), vb[:, :],
                st1[:, c:c + 1], A.mult, A.add))
            b.em("vector", lambda e, c=c, kb=kb: e.tensor_tensor_scan(
                y2_c[:, :], lam_t[:, c:c + 1].broadcast_to((128, TTOK)), kb[:, :],
                st2[:, c:c + 1], A.mult, A.add), inc=c_vg)
            b.wait("vector", c_vg)
            b.em("vector", lambda e, c=c: e.tensor_copy(st1[:, c:c + 1], y1_c[:, TTOK - 1:TTOK]))
            b.em("vector", lambda e, c=c: e.tensor_copy(st2[:, c:c + 1], y2_c[:, TTOK - 1:TTOK]))
            if t == 0 and c == 0:
                dsnap("vector", lambda: y1_c[:, :], "dbg1", 2048, 512)
                dsnap("vector", lambda: y2_c[:, :], "dbg1", 2560, 512)
                dsnap("vector", lambda: lam_t[:, 0:8], "dbg1", 3584, 8)
                dsnap("vector", lambda: gam_t[:, 0:8], "dbg1", 3592, 8)
                dsnap("vector", lambda: brkv_t[:, 0:24], "dbg1", 3600, 24, wait_ch=c_setup)
            b.em("vector", lambda e, c=c, vb=vb: e.scalar_tensor_tensor(
                num_c[:, :], vb[:, :], gam_t[:, c:c + 1], y1_c[:, :], A.mult, A.add))
            b.em("vector", lambda e, c=c, kb=kb: e.scalar_tensor_tensor(
                den_c[:, :], kb[:, :], gam_t[:, c:c + 1], y2_c[:, :], A.mult, A.add))
            b.em("vector", lambda e: e.tensor_scalar(den_c[:, :], den_c[:, :], 1e-6, None, A.add))
            b.em("vector", lambda e: e.reciprocal(rec_c[:, :], den_c[:, :]))
            if t == 0 and c == 0:
                dsnap("vector", lambda: den_c[:, :], "dbg1", 3072, 512)
            b.em("vector", lambda e: e.tensor_tensor(num_c[:, :], num_c[:, :], rec_c[:, :], A.mult))
            b.wait("vector", c_rev)
            b.em("vector", lambda e, c=c, rb=rb: e.scalar_tensor_tensor(
                xatt_fm[c][:, :], rb[:, :], 1.0, num_c[:, :], A.add, A.mult), inc=c_xat)
            if t == 0 and c == 0:
                dsnap("vector", lambda: xatt_fm[0][:, :], "dbgb", 1536, 512)
            if c >= 2:
                do_prep(1)
        b.wait("tensor", c_xat)
        for g in range(G):
            pw0, pw1 = pmm[(2 * g) % 4], pmm[(2 * g + 1) % 4]
            for h in range(2):
                pw = (pw0, pw1)[h]
                b.wait("tensor", c_yn, c_wo.n - 3)
                for c in range(KC):
                    b.em("tensor", lambda e, c=c, g=g, h=h, pw=pw: e.matmul(
                        pw[:, :], xatt_fm[c][:, g * 128:(g + 1) * 128],
                        wo_sb[c][:, h * TTOK:(h + 1) * TTOK], start=(c == 0), stop=False))
                b.em("tensor", lambda e, g=g, h=h, pw=pw: e.matmul(
                    pw[:, :], ones_bf[0:1, :], bo_bf[0:1, h * TTOK:(h + 1) * TTOK],
                    start=False, stop=True), inc=c_wo)
            b.wait("vector", c_wo)
            ln_stats(c_ln2, [[(pw0, 0), (pw1, 0)]], rv2, bb2)
            b.wait("scalar", c_ln2)
            x1b = x1_g[g % 2]
            if t * G + g >= 2:
                b.wait("vector", c_xn2, c_xn2.n - 1)
                b.wait("vector", c_x1s if g % 2 == 0 else c_x1s2)
            for h in range(2):
                b.em("scalar", lambda e, h=h, pw=(pw0, pw1)[h]: e.activation(
                    yn_g[h][:, :], pw[:, :], ACT.Identity,
                    bias=bb2[:, 0:1], scale=rv2[:, 0:1]), inc=c_yn)
                if t == 0 and g == 0 and h == 0:
                    dsnap("vector", lambda: yn_g[0][:, 0:400], "dbg1", 3624, 400, wait_ch=c_yn)
                    dsnap("vector", lambda: rv2[:, 0:1], "dbg1", 4094, 1)
                    dsnap("vector", lambda: bb2[:, 0:1], "dbg1", 4095, 1)
                b.wait("vector", c_yn)
                b.em("vector", lambda e, g=g, h=h, xb=xb, x1b=x1b: e.tensor_tensor(
                    x1b[:, h * 512:(h + 1) * 512], xb[:, g * D + h * 512:g * D + (h + 1) * 512],
                    yn_g[h][:, :], A.add), inc=(c_x1 if h == 1 else None))
            if g == G - 1:
                b.em("vector", lambda e: e.tensor_copy(fence_v[0:1, 0:1], fence_v[0:1, 1:2]), inc=c_xfree)
            b.wait("sync", c_x1)
            b.em("sync", lambda e, t=t, g=g, x1b=x1b: e.dma_start(
                out=x1_hbm[t * TTOK + g * 128:t * TTOK + (g + 1) * 128, :], in_=x1b[:, :]),
                inc=(c_x1s if g % 2 == 0 else c_x1s2), amt=16)
            ln_stats(c_ln3, [[(x1b, 0), (x1b, 512)]], rv3_all, b3_all, col0=t * G + g)
            b.wait("scalar", c_ln3)
            if t * G + g >= 1:
                b.wait("scalar", c_sx)
            b.em("scalar", lambda e, t=t, g=g, x1b=x1b: e.activation(
                xn2_g[:, :], x1b[:, :], ACT.Identity,
                bias=b3_all[:, t * G + g:t * G + g + 1],
                scale=rv3_all[:, t * G + g:t * G + g + 1]), inc=c_xn2)
            b.wait("vector", c_xn2)
            b.em("vector", lambda e, t=t, g=g: e.tensor_reduce(
                sx_slots[:, t * G + g:t * G + g + 1], xn2_g[:, :], AX, A.max,
                apply_absolute_value=True), inc=c_sx)

    b.wait("vector", c_sx)
    b.em("vector", lambda e: e.tensor_reduce(sxacc[:, 0:1], sx_slots[:, :], AX, A.max), inc=c_sx)
    b.wait("sync", c_sx)
    barrier("cc1", [sxacc], [sx_b], [QA])
    B1_DMA = c_bdma.n
    astack.close()

    ustack = ExitStack()
    xq_tm = [sb(ustack, f"xq_tm{g}", [128, D], BF16) for g in range(G)]
    qt_g = sb(ustack, "qt_g", [128, D], F32)
    xqf = [sb(ustack, f"xqf{f}", [128, T // 2], BF16) for f in range(KC)]
    wT = [sb(ustack, f"wT{k}", [128, I3], BF16) for k in range(KC)]
    mm_stage = [sb(ustack, f"mmst{i}", [128, TTOK], F32) for i in range(8)]
    uq_hist = []

    b.wait("vector", c_bdma, B1_DMA)
    b.wait("scalar", c_ln3)
    b.wait("sync", c_x1s)
    b.wait("sync", c_x1s2)

    def xq_front(t, tloc):
        xb = x_tm[t % 2]
        if t >= 2:
            b.wait("sync", c_uxf, t - 1)
        for g in range(G):
            b.em("sync", lambda e, t=t, g=g, xb=xb: e.dma_start(
                out=xb[:, g * D:(g + 1) * D],
                in_=x1_hbm[t * TTOK + g * 128:t * TTOK + (g + 1) * 128, :]),
                inc=(c_xload if t % 2 == 0 else c_xload2), amt=16)
        b.wait("scalar", c_xload if t % 2 == 0 else c_xload2)
        b.wait("vector", c_tp)
        for g in range(G):
            xng = (xn2_g, xn2_g2)[g % 2]
            if len(uq_hist) >= 2:
                b.wait("scalar", c_uq, uq_hist[-2])
            b.em("scalar", lambda e, t=t, g=g, xb=xb, xng=xng: e.activation(
                xng[:, :], xb[:, g * D:(g + 1) * D], ACT.Identity,
                bias=b3_all[:, t * G + g:t * G + g + 1],
                scale=rv3_all[:, t * G + g:t * G + g + 1]), inc=c_xn2)
            b.wait("vector", c_xn2)
            b.em("vector", lambda e, xng=xng: e.tensor_scalar(
                qt_g[:, :], xng[:, :], sx_b[:, 1:2], C_RND, A.mult, A.add))
            b.em("vector", lambda e, g=g: e.tensor_scalar(
                xq_tm[g][:, :], qt_g[:, :], -C_RND, sx_b[:, 0:1], A.add, A.mult), inc=c_uq)
            uq_hist.append(c_uq.n)
        b.em("vector", lambda e: e.tensor_copy(fence_v[0:1, 0:1], fence_v[0:1, 1:2]), inc=c_uxf)
        do_prep(6)

    def xq_back(t, tloc, mm_wait=True):
        if mm_wait:
            b.wait("scalar", c_mm)
        b.wait("tensor", c_uq)
        for f in range(KC):
            if c_tp.n >= 2:
                b.wait("tensor", c_tpe, c_tp.n - 1)
            for g in range(G):
                b.em("tensor", lambda e, f=f, g=g: e.transpose(
                    ptp[f % 2][:, g * 128:(g + 1) * 128], xq_tm[g][:, f * 128:(f + 1) * 128],
                    ident[:, :]), inc=(c_tp if g == G - 1 else None))
            b.wait("scalar", c_tp)
            b.em("scalar", lambda e, f=f, tloc=tloc: e.activation(
                xqf[f][:, tloc * TTOK:(tloc + 1) * TTOK], ptp[f % 2][:, 0:TTOK], ACT.Identity),
                inc=c_tpe)
        b.em("scalar", lambda e: e.activation(fence_s[0:1, 0:1], fence_s[0:1, 1:2], ACT.Identity), inc=c_xq)

    mst_chs = (c_mst, c_mst2, c_mst3, c_mst4)

    def emit_ffn_mm(tglobal, tloc, dest_hbm, slots, mrange=None):
        for m in (mrange if mrange is not None else range(MC)):
            b.wait("tensor", c_mev, c_mm.n - 5)
            for kk in range(KC):
                b.em("tensor", lambda e, m=m, kk=kk, tloc=tloc: e.matmul(
                    pmm[m % 6][:, :], wT[kk][:, m * 128:(m + 1) * 128],
                    xqf[kk][:, tloc * TTOK:(tloc + 1) * TTOK],
                    start=(kk == 0), stop=(kk == KC - 1)), inc=(c_mm if kk == KC - 1 else None))
            b.wait("scalar", c_mm)
            stg = mm_stage[m % 8]
            b.wait("scalar", c_mred, c_mred.n - 7)
            b.wait("scalar", mst_chs[m % 4])
            b.em("scalar", lambda e, m=m, stg=stg: e.activation(
                stg[:, :], pmm[m % 6][:, :], ACT.Identity), inc=c_mev)
            b.wait("vector", c_mev)
            b.em("vector", lambda e, stg=stg, m=m, tglobal=tglobal: e.tensor_reduce(
                slots[:, tglobal * MC + m:tglobal * MC + m + 1], stg[:, :], AX, A.max,
                apply_absolute_value=True), inc=c_mred)
            b.wait("sync", c_mev)
            b.em("sync", lambda e, m=m, tglobal=tglobal, stg=stg, dest_hbm=dest_hbm: e.dma_start(
                out=dest_hbm[m * 128:(m + 1) * 128, tglobal * TTOK:(tglobal + 1) * TTOK],
                in_=stg[:, :]), inc=mst_chs[m % 4], amt=16)

    def load_wT(whbm, wps_val):
        b.wait("sync", c_wps, wps_val)
        b.wait("sync", c_wps2, wps_val)
        b.wait("sync", c_mm)
        for kk in range(KC):
            b.em("sync", lambda e, kk=kk, whbm=whbm: e.dma_start_transpose(
                wT[kk][:, 0:512], whbm[0:512, kk * 128:(kk + 1) * 128]), inc=c_wt, amt=16)
        for kk in range(KC):
            b.em("sync", lambda e, kk=kk, whbm=whbm: e.dma_start_transpose(
                wT[kk][:, 512:I3], whbm[512:I3, kk * 128:(kk + 1) * 128]), inc=c_wt2, amt=16)

    xq_front(0, 0)
    load_wT(wup_bf, WPS_UP)
    xq_back(0, 0, mm_wait=False)
    for half in range(2):
        for tloc in range(4):
            t = half * 4 + tloc
            b.wait("tensor", c_wt)
            b.wait("tensor", c_xq)
            emit_ffn_mm(t, tloc, up_hbm, acc_slots[0], range(0, 4))
            b.wait("tensor", c_wt2)
            if half == 0 and t + 1 <= 3:
                xq_front(t + 1, t + 1)
            emit_ffn_mm(t, tloc, up_hbm, acc_slots[0], range(4, MC))
            if half == 0 and t + 1 <= 3:
                xq_back(t + 1, t + 1, mm_wait=False)
        load_wT(wgt_bf, WPS_GT)
        for tloc in range(4):
            t = half * 4 + tloc
            b.wait("tensor", c_wt)
            emit_ffn_mm(t, tloc, gt_hbm, acc_slots[1], range(0, 4))
            b.wait("tensor", c_wt2)
            if half == 0:
                xq_front(4 + tloc, tloc)
            emit_ffn_mm(t, tloc, gt_hbm, acc_slots[1], range(4, MC))
            if half == 0:
                xq_back(4 + tloc, tloc, mm_wait=True)
        if half == 0:
            load_wT(wup_bf, WPS_UP)

    b.wait("vector", c_mred)
    b.em("vector", lambda e: e.tensor_reduce(uacc[:, 0:1], acc_slots[0][:, :], AX, A.max))
    b.em("vector", lambda e: e.tensor_reduce(gacc[:, 0:1], acc_slots[1][:, :], AX, A.max), inc=c_mred)
    b.wait("sync", c_mred)
    barrier("cc2", [uacc, gacc], [su_b, sg_b], [QA, QA])
    B2_DMA = c_bdma.n
    MEV_FINAL = c_mev.n
    ustack.close()

    fstack = ExitStack()
    FW = 2048
    NFW = (I3 * T) // (128 * FW)
    f_in = [sb(fstack, f"fin{i}", [128, FW], F32) for i in range(2)]
    f_in2 = [sb(fstack, f"fin2{i}", [128, FW], F32) for i in range(2)]
    f_t = sb(fstack, "f_t", [128, FW], F32)
    f_qp = [sb(fstack, f"f_qp{i}", [128, FW], F32) for i in range(2)]
    f_sigp = [sb(fstack, f"f_sigp{i}", [128, FW], F32) for i in range(2)]
    f_t2 = sb(fstack, "f_t2", [128, FW], F32)
    f_u16 = sb(fstack, "f_u16", [128, FW], F32)
    f_q2 = sb(fstack, "f_q2", [128, FW], F32)
    f_out = [sb(fstack, f"fout{i}", [128, FW], F32) for i in range(2)]
    f1_at, f2_at, fsg_at = {}, {}, {}

    b.wait("vector", c_bdma, B2_DMA)
    b.wait("sync", c_mst)
    b.wait("sync", c_mst2)
    b.wait("sync", c_mst3)
    b.wait("sync", c_mst4)

    def s2_load(i):
        m, tq = i // (T // FW), i % (T // FW)
        fin = f_in[i % 2]
        if i >= 2:
            b.wait("sync", c_f1, f1_at[i - 2])
        b.em("sync", lambda e, m=m, tq=tq, fin=fin: e.dma_start(
            out=fin[:, :], in_=gt_hbm[m * 128:(m + 1) * 128, tq * FW:(tq + 1) * FW]),
            inc=(c_fl if i % 2 == 0 else c_fl2), amt=16)

    def s2_front(i):
        fin, fq = f_in[i % 2], f_qp[i % 2]
        b.wait("vector", c_fl if i % 2 == 0 else c_fl2)
        b.em("vector", lambda e, fin=fin: e.tensor_scalar(
            f_t[:, :], fin[:, :], sg_b[:, 1:2], C_RND, A.mult, A.add))
        b.em("vector", lambda e, fq=fq: e.tensor_scalar(
            fq[:, :], f_t[:, :], -C_RND, sg_b[:, 0:1], A.add, A.mult), inc=c_f1)
        f1_at[i] = c_f1.n

    def s2_sig(i):
        b.wait("scalar", c_f1, f1_at[i])
        if i >= 2:
            b.wait("scalar", c_f2, f2_at[i - 2])
        b.em("scalar", lambda e, i=i: e.activation(
            f_sigp[i % 2][:, :], f_qp[i % 2][:, :], ACT.Sigmoid), inc=c_fsg)
        fsg_at[i] = c_fsg.n

    def s2_tail(i):
        fo = f_out[i % 2]
        b.wait("vector", c_fsg, fsg_at[i])
        b.em("vector", lambda e, i=i: e.tensor_scalar(
            f_t2[:, :], f_sigp[i % 2][:, :], 65536.0, C_RND, A.mult, A.add))
        b.em("vector", lambda e: e.tensor_scalar(
            f_u16[:, :], f_t2[:, :], -C_RND, 65535.0, A.add, A.min))
        if i >= 2:
            b.wait("vector", c_f3 if i % 2 == 0 else c_f32)
        b.em("vector", lambda e, fo=fo, i=i: e.scalar_tensor_tensor(
            fo[:, :], f_u16[:, :], float(2.0 ** -16), f_qp[i % 2][:, :], A.mult, A.mult))
        b.em("vector", lambda e, fo=fo, i=i: e.tensor_reduce(
            acc_slots[0][:, i:i + 1], fo[:, :], AX, A.max, apply_absolute_value=True),
            inc=c_f2)
        f2_at[i] = c_f2.n
        m, tq = i // (T // FW), i % (T // FW)
        b.wait("sync", c_f2, f2_at[i])
        b.em("sync", lambda e, m=m, tq=tq, fo=fo: e.dma_start(
            out=g2_hbm[m * 128:(m + 1) * 128, tq * FW:(tq + 1) * FW], in_=fo[:, :]),
            inc=(c_f3 if i % 2 == 0 else c_f32), amt=16)

    s2_load(0)
    s2_load(1)
    for i in range(NFW):
        if i % 6 == 0:
            do_prep(1)
        s2_front(i)
        if i + 2 < NFW:
            s2_load(i + 2)
        s2_sig(i)
        if i >= 1:
            s2_tail(i - 1)
    s2_tail(NFW - 1)

    b.wait("vector", c_f2)
    b.em("vector", lambda e: e.tensor_reduce(g2acc[:, 0:1], acc_slots[0][:, 0:NFW], AX, A.max), inc=c_f2)
    b.wait("sync", c_f2)
    barrier("cc3", [g2acc], [sg2_b], [QA])
    B3_DMA = c_bdma.n

    b.wait("vector", c_bdma, B3_DMA)
    b.wait("sync", c_f3)
    b.wait("sync", c_f32)
    f1o_at, f2o_at = {}, {}

    def o_load(i):
        m, tq = i // (T // FW), i % (T // FW)
        fin, fin2 = f_in[i % 2], f_in2[i % 2]
        if i >= 2:
            b.wait("sync", c_f1, f1o_at[i - 2])
        b.em("sync", lambda e, m=m, tq=tq, fin=fin: e.dma_start(
            out=fin[:, :], in_=g2_hbm[m * 128:(m + 1) * 128, tq * FW:(tq + 1) * FW]),
            inc=(c_fl if i % 2 == 0 else c_fl2), amt=16)
        b.em("sync", lambda e, m=m, tq=tq, fin2=fin2: e.dma_start(
            out=fin2[:, :], in_=up_hbm[m * 128:(m + 1) * 128, tq * FW:(tq + 1) * FW]),
            inc=(c_fl if i % 2 == 0 else c_fl2), amt=16)

    def o_body(i):
        m, tq = i // (T // FW), i % (T // FW)
        fin, fin2 = f_in[i % 2], f_in2[i % 2]
        fq = f_qp[0]
        b.wait("vector", c_fl if i % 2 == 0 else c_fl2)
        b.em("vector", lambda e, fin=fin: e.tensor_scalar(
            f_t[:, :], fin[:, :], sg2_b[:, 1:2], C_RND, A.mult, A.add))
        b.em("vector", lambda e, fq=fq: e.tensor_scalar(
            fq[:, :], f_t[:, :], -C_RND, sg2_b[:, 0:1], A.add, A.mult))
        b.em("vector", lambda e, fin2=fin2: e.tensor_scalar(
            f_t2[:, :], fin2[:, :], su_b[:, 1:2], C_RND, A.mult, A.add), inc=c_f1)
        f1o_at[i] = c_f1.n
        b.em("vector", lambda e: e.tensor_scalar(
            f_q2[:, :], f_t2[:, :], -C_RND, su_b[:, 0:1], A.add, A.mult))
        fo = f_out[i % 2]
        if i >= 2:
            b.wait("vector", c_f3 if i % 2 == 0 else c_f32)
        b.em("vector", lambda e, fo=fo, fq=fq: e.tensor_tensor(
            fo[:, :], fq[:, :], f_q2[:, :], A.mult))
        b.em("vector", lambda e, fo=fo, i=i: e.tensor_reduce(
            acc_slots[1][:, i:i + 1], fo[:, :], AX, A.max, apply_absolute_value=True),
            inc=c_f2)
        f2o_at[i] = c_f2.n
        b.wait("sync", c_f2, f2o_at[i])
        b.em("sync", lambda e, m=m, tq=tq, fo=fo: e.dma_start(
            out=o_hbm[m * 128:(m + 1) * 128, tq * FW:(tq + 1) * FW], in_=fo[:, :]),
            inc=(c_f3 if i % 2 == 0 else c_f32), amt=16)

    o_load(0)
    o_load(1)
    for i in range(NFW):
        o_body(i)
        if i + 2 < NFW:
            o_load(i + 2)

    b.wait("vector", c_f2)
    b.em("vector", lambda e: e.tensor_reduce(oacc[:, 0:1], acc_slots[1][:, 0:NFW], AX, A.max), inc=c_f2)
    b.wait("sync", c_f2)
    barrier("cc4", [oacc], [so_b], [QA])
    B4_DMA = c_bdma.n

    fstack.close()
    dstack = ExitStack()
    d_in = [sb(dstack, f"d_in{i}", [128, 512], F32) for i in range(4)]
    d_t = sb(dstack, "d_t", [128, 512], F32)
    oqf2 = [[sb(dstack, f"oqf{p}_{m}", [128, 512], BF16) for m in range(MC)]
            for p in range(2)]
    wdT = [sb(dstack, f"wdT{m}", [128, D], BF16) for m in range(MC)]
    xout_g = [sb(dstack, f"xoutg{i}", [128, D], F32) for i in range(2)]

    b.wait("vector", c_bdma, B4_DMA)
    b.wait("sync", c_f3)
    b.wait("sync", c_f32)
    b.wait("sync", c_x1s)
    b.wait("sync", c_x1s2)
    b.wait("sync", c_wps, WPS_DN)
    b.wait("sync", c_wps2, WPS_DN)
    for m in range(MC):
        b.em("sync", lambda e, m=m: e.dma_start_transpose(
            wdT[m][:, :], wdn_bf[:, m * 128:(m + 1) * 128]), inc=c_wt, amt=16)
    WT_DN = c_wt.n
    b.wait("tensor", c_mev, MEV_FINAL)

    dq_at, dl_at, dmm_end = {}, {}, {}

    def build_oqf_part(t, mlo, mhi):
        buf = oqf2[t % 2]
        if mlo == 0 and t >= 2:
            b.wait("vector", c_dmm, dmm_end[t - 2])
        for m in range(mlo, mhi):
            mi = t * MC + m
            if mi >= 4:
                b.wait("sync", c_oq, dl_at[mi - 4])
            b.em("sync", lambda e, m=m, t=t, fin=d_in[m % 4]: e.dma_start(
                out=fin[:, :], in_=o_hbm[m * 128:(m + 1) * 128, t * TTOK:(t + 1) * TTOK]),
                inc=(c_oql if m % 2 == 0 else c_oql2), amt=16)
        for m in range(mlo, mhi):
            fin = d_in[m % 4]
            mi = t * MC + m
            b.wait("vector", c_oql if m % 2 == 0 else c_oql2)
            b.em("vector", lambda e, fin=fin: e.tensor_scalar(
                d_t[:, :], fin[:, :], so_b[:, 1:2], C_RND, A.mult, A.add))
            b.em("vector", lambda e, m=m, buf=buf: e.tensor_scalar(
                buf[m][:, :], d_t[:, :], -C_RND, so_b[:, 0:1], A.add, A.mult), inc=c_oq)
            dl_at[mi] = c_oq.n
        if mhi == MC:
            dq_at[t] = c_oq.n

    for p in range(6):
        build_oqf_part(0, 4 * p, 4 * p + 4)
    for t in range(NT):
        xb = x_tm[t % 2]
        nxt = t + 1
        if t >= 2:
            b.wait("sync", c_uxf, c_uxf.n - 1)
        for g in range(G):
            b.em("sync", lambda e, t=t, g=g, xb=xb: e.dma_start(
                out=xb[:, g * D:(g + 1) * D],
                in_=x1_hbm[t * TTOK + g * 128:t * TTOK + (g + 1) * 128, :]),
                inc=(c_xload if t % 2 == 0 else c_xload2), amt=16)
        if t == 0:
            b.wait("tensor", c_wt, WT_DN)
        b.wait("tensor", c_oq, dq_at[t])
        if nxt < NT:
            build_oqf_part(nxt, 0, 4)
        for g in range(G):
            pw0, pw1 = pmm[(2 * g) % 6], pmm[(2 * g + 1) % 6]
            for h in range(2):
                pw = (pw0, pw1)[h]
                b.wait("tensor", c_yn4, c_dmm.n - 5)
                for m in range(MC):
                    b.em("tensor", lambda e, m=m, g=g, h=h, pw=pw, t=t: e.matmul(
                        pw[:, :], oqf2[t % 2][m][:, g * 128:(g + 1) * 128],
                        wdT[m][:, h * TTOK:(h + 1) * TTOK],
                        start=(m == 0), stop=(m == MC - 1)), inc=(c_dmm if m == MC - 1 else None))
            b.wait("vector", c_dmm)
            ln_stats(c_ln4, [[(pw0, 0), (pw1, 0)]], rv4, bb4)
            b.wait("scalar", c_ln4)
            b.wait("vector", c_xload if t % 2 == 0 else c_xload2)
            xob = xout_g[g % 2]
            if t * G + g >= 2:
                b.wait("vector", c_out if g % 2 == 0 else c_out2)
            for h in range(2):
                b.em("scalar", lambda e, h=h, pw=(pw0, pw1)[h]: e.activation(
                    yn_g[h][:, :], pw[:, :], ACT.Identity,
                    bias=bb4[:, 0:1], scale=rv4[:, 0:1]), inc=c_yn4)
                b.wait("vector", c_yn4)
                b.em("vector", lambda e, g=g, h=h, xb=xb, xob=xob: e.tensor_tensor(
                    xob[:, h * 512:(h + 1) * 512], xb[:, g * D + h * 512:g * D + (h + 1) * 512],
                    yn_g[h][:, :], A.add), inc=(c_xo if h == 1 else None))
            if g == G - 1:
                b.em("vector", lambda e: e.tensor_copy(fence_v[0:1, 0:1], fence_v[0:1, 1:2]), inc=c_uxf)
            b.wait("sync", c_xo)
            b.em("sync", lambda e, t=t, g=g, xob=xob: e.dma_start(
                out=out[t * TTOK + g * 128:t * TTOK + (g + 1) * 128, :], in_=xob[:, :]),
                inc=(c_out if g % 2 == 0 else c_out2), amt=16)
            if nxt < NT:
                if g < G - 1:
                    build_oqf_part(nxt, 4 * (g + 1), 4 * (g + 2))
                else:
                    build_oqf_part(nxt, 16, 20)
                    build_oqf_part(nxt, 20, 24)
        dmm_end[t] = c_dmm.n

    dsnap_finish()
    b.wait("sync", c_out, c_out.n)
    b.wait("sync", c_out2, c_out2.n)
    dstack.close()

    with nc.Block() as block:
        @block.sync
        def _(e):
            for f in b.ops["sync"]:
                f(e)

        @block.scalar
        def _(e):
            for f in b.ops["scalar"]:
                f(e)

        @block.vector
        def _(e):
            for f in b.ops["vector"]:
                f(e)

        @block.tensor
        def _(e):
            for f in b.ops["tensor"]:
                f(e)

        @block.gpsimd
        def _(e):
            for f in b.ops["gpsimd"]:
                f(e)

    gstack.close()
    return nc


_NC_CACHE = {}


def kernel(L0_attn_inputs, L0_attn_W_rkv, L0_attn_b_rkv, L0_attn_W_o, L0_attn_b_o,
           L0_attn_nu_log, L0_attn_gamma_log, L0_ffn_w_gate, L0_ffn_w_up, L0_ffn_w_down):
    if "main" not in _NC_CACHE:
        _NC_CACHE["main"] = build_program(debug=False)
    nc = _NC_CACHE["main"]
    x = np.ascontiguousarray(np.asarray(L0_attn_inputs, dtype=np.float32))
    shared = {
        "w_rkv": np.ascontiguousarray(np.asarray(L0_attn_W_rkv, np.float32)),
        "b_rkv": np.ascontiguousarray(np.asarray(L0_attn_b_rkv, np.float32)),
        "w_o": np.ascontiguousarray(np.asarray(L0_attn_W_o, np.float32)),
        "b_o": np.ascontiguousarray(np.asarray(L0_attn_b_o, np.float32)),
        "nu_log": np.ascontiguousarray(np.asarray(L0_attn_nu_log, np.float32)),
        "gamma_log": np.ascontiguousarray(np.asarray(L0_attn_gamma_log, np.float32)),
        "w_gate": np.ascontiguousarray(np.asarray(L0_ffn_w_gate, np.float32)),
        "w_up": np.ascontiguousarray(np.asarray(L0_ffn_w_up, np.float32)),
        "w_down": np.ascontiguousarray(np.asarray(L0_ffn_w_down, np.float32)),
    }
    in_maps = [dict(shared, x=x[i]) for i in range(NCORES)]
    res = run_bass_kernel_spmd(nc, in_maps, core_ids=list(range(NCORES)))
    return np.stack([np.asarray(res.results[i]["out"], dtype=np.float32)
                     for i in range(NCORES)], axis=0)


# version 14
# speedup vs baseline: 1.0201x; 1.0201x over previous
import numpy as np
import concourse.bass as bass
import concourse.mybir as mybir
from concourse.bass_utils import run_bass_kernel_spmd

F32 = mybir.dt.float32
BF16 = mybir.dt.bfloat16
I32 = mybir.dt.int32
A = mybir.AluOpType
ACT = mybir.ActivationFunctionType
AX = mybir.AxisListType.X

C_RND = float(np.float32(12582912.0))
T = 4096
D = 1024
I3 = 3072
TTOK = 512
NT = T // TTOK
G = TTOK // 128
KC = D // 128
MC = I3 // 128
NCORES = 8
LN_EPS = 1e-5
QA = 32767.0


class Ch:
    def __init__(self, stack, nc, name):
        self.sem = stack.enter_context(nc.semaphore(name))
        self.n = 0
        self.rn = 0
        self.prefix = [(0, 0, -1)]

    def note(self, amt, kind, gvidx):
        self.n += amt
        c, r, g = self.prefix[-1]
        if kind == "r":
            self.rn += amt
            r = self.rn
        else:
            g = max(g, gvidx)
        self.prefix.append((self.n, r, g))

    def targets(self, val):
        import bisect
        i = bisect.bisect_left([p[0] for p in self.prefix], val)
        if i >= len(self.prefix):
            i = len(self.prefix) - 1
        c, r, g = self.prefix[i]
        return r, (g + 1 if g >= 0 else 0)


class B_:
    def __init__(self, nc, stack):
        self.nc = nc
        self.stack = stack
        self.ops = {k: [] for k in ("sync", "scalar", "vector", "tensor", "gpsimd")}
        self.gv = stack.enter_context(nc.semaphore("c_gv"))
        self.gv_n = 0

    def ch(self, name):
        return Ch(self.stack, self.nc, name)

    def em(self, eng, f, inc=None, amt=1):
        if eng == "vector":
            idx = self.gv_n
            self.gv_n += 1
            gsem = self.gv
            if inc is not None:
                inc.note(amt, "v", idx)

            def g(e, f=f, idx=idx, gsem=gsem):
                if idx > 0:
                    e.wait_ge(gsem, idx)
                f(e).then_inc(gsem, 1)
            self.ops[eng].append(g)
            return
        if inc is not None:
            inc.note(amt, "r", None)
            self.ops[eng].append(lambda e, f=f, s=inc.sem, a=amt: f(e).then_inc(s, a))
        else:
            self.ops[eng].append(f)

    def wait(self, eng, ch, val=None):
        v = ch.n if val is None else val
        if v <= 0:
            return
        rt, gt = ch.targets(v)
        if rt > 0:
            self.ops[eng].append(lambda e, s=ch.sem, v=rt: e.wait_ge(s, v))
        if gt > 0:
            self.ops[eng].append(lambda e, s=self.gv, v=gt: e.wait_ge(s, v))


def build_program(debug=False):
    from contextlib import ExitStack
    nc = bass.Bass("TRN2", target_bir_lowering=False, debug=False)
    gstack = ExitStack()
    b = B_(nc, gstack)

    def sb(st, name, shape, dt=F32):
        return st.enter_context(nc.sbuf_tensor(name, shape, dt))

    scratch_kind = {"kind": "ExternalOutput"} if debug else {}

    x_in = nc.dram_tensor("x", [T, D], F32, kind="ExternalInput").ap()
    w_rkv = nc.dram_tensor("w_rkv", [D, 3 * D], F32, kind="ExternalInput").ap()
    b_rkv = nc.dram_tensor("b_rkv", [3 * D], F32, kind="ExternalInput").ap()
    w_o = nc.dram_tensor("w_o", [D, D], F32, kind="ExternalInput").ap()
    b_o = nc.dram_tensor("b_o", [D], F32, kind="ExternalInput").ap()
    nu_log = nc.dram_tensor("nu_log", [D], F32, kind="ExternalInput").ap()
    gamma_log = nc.dram_tensor("gamma_log", [D], F32, kind="ExternalInput").ap()
    w_gate = nc.dram_tensor("w_gate", [I3, D], F32, kind="ExternalInput").ap()
    w_up = nc.dram_tensor("w_up", [I3, D], F32, kind="ExternalInput").ap()
    w_down = nc.dram_tensor("w_down", [D, I3], F32, kind="ExternalInput").ap()
    out = nc.dram_tensor("out", [T, D], F32, kind="ExternalOutput").ap()

    x1_hbm = nc.dram_tensor("x1_hbm", [T, D], F32, **scratch_kind).ap()
    up_hbm = nc.dram_tensor("up_hbm", [I3, T], F32, **scratch_kind).ap()
    gt_hbm = nc.dram_tensor("gt_hbm", [I3, T], F32, **scratch_kind).ap()
    g2_hbm = nc.dram_tensor("g2_hbm", [I3, T], F32, **scratch_kind).ap()
    o_hbm = nc.dram_tensor("o_hbm", [I3, T], F32, **scratch_kind).ap()
    wup_bf = nc.dram_tensor("wup_bf", [I3, D], BF16).ap()
    wgt_bf = nc.dram_tensor("wgt_bf", [I3, D], BF16).ap()
    wdn_bf = nc.dram_tensor("wdn_bf", [D, I3], BF16).ap()
    cc_bufs = {}
    for nm, n in [("cc1", 128), ("cc2", 256), ("cc3", 128), ("cc4", 128)]:
        cc_bufs[nm] = (nc.dram_tensor(nm + "_in", [1, n], F32).ap(),
                       nc.dram_tensor(nm + "_out", [1, n], F32).ap())
    scl_hbm = nc.dram_tensor("scl_hbm", [1, 8], F32).ap()

    ident = sb(gstack, "ident", [128, 128], BF16)
    iotat = sb(gstack, "iotat", [128, 128], F32)
    ones_bf = sb(gstack, "ones_bf", [1, 128], BF16)
    bo_bf = sb(gstack, "bo_bf", [1, D], BF16)
    brkv_t = sb(gstack, "brkv_t", [128, 3 * KC], F32)
    nul_t = sb(gstack, "nul_t", [128, KC], F32)
    gml_t = sb(gstack, "gml_t", [128, KC], F32)
    lam_t = sb(gstack, "lam_t", [128, KC], F32)
    gam_t = sb(gstack, "gam_t", [128, KC], F32)
    brkv_h = sb(gstack, "brkv_h", [128, KC], F32)

    x_tm = [sb(gstack, f"x_tm{i}", [128, G * D], F32) for i in range(2)]
    yn_g = [sb(gstack, f"yn{i}", [128, TTOK], F32) for i in range(2)]
    xn2_g = sb(gstack, "xn2_g", [128, D], F32)
    xn2_g2 = sb(gstack, "xn2_g2", [128, D], F32)

    bnst = sb(gstack, "bnst", [128, 48], F32)
    stat_m = sb(gstack, "stat_m", [128, G], F32)
    stat_t0 = sb(gstack, "stat_t0", [128, G], F32)
    stat_v = sb(gstack, "stat_v", [128, G], F32)
    stat_sq = sb(gstack, "stat_sq", [128, 16], F32)
    stat_ss = sb(gstack, "stat_ss", [128, G], F32)
    stat_std = sb(gstack, "stat_std", [128, G], F32)
    rv1 = sb(gstack, "rv1", [128, G], F32)
    bb1 = sb(gstack, "bb1", [128, G], F32)
    rv2 = sb(gstack, "rv2", [128, 1], F32)
    bb2 = sb(gstack, "bb2", [128, 1], F32)
    rv4 = sb(gstack, "rv4", [128, 1], F32)
    bb4 = sb(gstack, "bb4", [128, 1], F32)
    rv3_all = sb(gstack, "rv3_all", [128, NT * G], F32)
    b3_all = sb(gstack, "b3_all", [128, NT * G], F32)

    sxacc = sb(gstack, "sxacc", [128, 1], F32)
    sx_slots = sb(gstack, "sx_slots", [128, NT * G], F32)
    acc_slots = [sb(gstack, f"accsl{i}", [128, MC * NT], F32) for i in range(2)]
    uacc = sb(gstack, "uacc", [128, 1], F32)
    gacc = sb(gstack, "gacc", [128, 1], F32)
    g2acc = sb(gstack, "g2acc", [128, 1], F32)
    oacc = sb(gstack, "oacc", [128, 1], F32)
    redt = sb(gstack, "redt", [128, 1], F32)
    sx_b = sb(gstack, "sx_b", [128, 2], F32)
    su_b = sb(gstack, "su_b", [128, 2], F32)
    sg_b = sb(gstack, "sg_b", [128, 2], F32)
    sg2_b = sb(gstack, "sg2_b", [128, 2], F32)
    so_b = sb(gstack, "so_b", [128, 2], F32)
    ccs = sb(gstack, "ccs", [1, 256], F32)
    svals = sb(gstack, "svals", [1, 4], F32)
    gx1 = sb(gstack, "gx1", [1, 1], F32)
    gx2 = sb(gstack, "gx2", [1, 1], F32)
    fence_v = sb(gstack, "fence_v", [1, 2], F32)
    fence_s = sb(gstack, "fence_s", [1, 2], F32)
    eps_t = sb(gstack, "eps_t", [128, 1], F32)

    WPC = 512
    WPD = 4
    wp_in = [sb(gstack, f"wp_in{i}", [128, WPC], F32) for i in range(WPD)]
    wp_bmax = sb(gstack, "wp_bmax", [128, 16], F32)
    wp_sb = sb(gstack, "wp_sb", [128, 16], F32)
    wp_isb = sb(gstack, "wp_isb", [128, 16], F32)
    wp_t1 = sb(gstack, "wp_t1", [128, WPC], F32)
    wp_i32 = sb(gstack, "wp_i32", [128, WPC], I32)
    wp_out = [sb(gstack, f"wp_out{i}", [128, WPC], BF16) for i in range(2)]

    ptp = [gstack.enter_context(nc.psum_tensor("ptp_a", [128, 1024], BF16)),
           gstack.enter_context(nc.psum_tensor("ptp_b", [128, 1024], BF16))]
    pmm = [gstack.enter_context(nc.psum_tensor(f"pmm{i}", [128, TTOK], F32)) for i in range(6)]

    names = ("c_setup c_cast c_tiny c_xload c_xfree c_ln1 c_xn c_tp c_tpe c_rkv "
             "c_kev c_rev c_vev c_xat c_wo c_ln2 c_yn c_x1 c_x1s c_ln3 c_xn2 c_sx "
             "c_cc c_bdma c_bv c_wpl c_wpg c_wpv c_wps c_wt c_xq c_mm c_mev c_mred "
             "c_mst c_fl c_f1 c_f2 c_f3 c_fsg c_uxf c_uq c_oq c_oql c_dmm c_ln4 "
             "c_yn4 c_xo c_out c_vg c_iota c_x1s2 c_mst2 c_f32 c_out2 c_xload2 c_fl2 c_oql2 c_wpl2 c_wps2 c_woh "
             "c_wpl3 c_wpl4 c_wt2 c_mst3 c_mst4 c_oql3 c_oql4 c_oql5 c_oql6 c_oql7 c_oql8").split()
    C = {nm: b.ch(nm) for nm in names}
    (c_setup, c_cast, c_tiny, c_xload, c_xfree, c_ln1, c_xn, c_tp, c_tpe, c_rkv,
     c_kev, c_rev, c_vev, c_xat, c_wo, c_ln2, c_yn, c_x1, c_x1s, c_ln3, c_xn2, c_sx,
     c_cc, c_bdma, c_bv, c_wpl, c_wpg, c_wpv, c_wps, c_wt, c_xq, c_mm, c_mev, c_mred,
     c_mst, c_fl, c_f1, c_f2, c_f3, c_fsg, c_uxf, c_uq, c_oq, c_oql, c_dmm, c_ln4,
     c_yn4, c_xo, c_out, c_vg, c_iota, c_x1s2, c_mst2, c_f32, c_out2, c_xload2, c_fl2, c_oql2, c_wpl2, c_wps2, c_woh,
     c_wpl3, c_wpl4, c_wt2, c_mst3, c_mst4, c_oql3, c_oql4, c_oql5, c_oql6, c_oql7, c_oql8) = (C[nm] for nm in names)

    c_dbgp = b.ch("c_dbgp")
    c_dbgs = b.ch("c_dbgs")
    if debug:
        dbg1 = nc.dram_tensor("dbg1", [128, 4096], F32, kind="ExternalOutput").ap()
        dbgb = nc.dram_tensor("dbgb", [128, 2048], BF16, kind="ExternalOutput").ap()
        dbgst1 = sb(gstack, "dbgst1", [128, 4096], F32)
        dbgstb = sb(gstack, "dbgstb", [128, 2048], BF16)

    def dsnap(eng, src_fn, dstname, col, w, wait_ch=None):
        if not debug:
            return
        stg = dbgst1 if dstname == "dbg1" else dbgstb
        if wait_ch is not None:
            b.wait("vector", wait_ch)
        b.em("vector", lambda e: e.tensor_copy(stg[:, col:col + w], src_fn()), inc=c_dbgp)

    def dsnap_finish():
        if not debug:
            return
        b.wait("sync", c_dbgp)
        b.em("sync", lambda e: e.dma_start(out=dbg1[:, :], in_=dbgst1[:, :]), inc=c_dbgs, amt=16)
        b.em("sync", lambda e: e.dma_start(out=dbgb[:, :], in_=dbgstb[:, :]), inc=c_dbgs, amt=16)
        b.wait("sync", c_dbgs)

    b.em("gpsimd", lambda e: e.iota(iotat[:, :], [[1, 128]], base=0, channel_multiplier=-1,
                                    allow_small_or_imprecise_dtypes=True), inc=c_iota, amt=1)
    b.em("sync", lambda e: e.dma_start(out=brkv_t[:, :],
                                       in_=b_rkv.rearrange("(c p) -> p c", p=128), allow_slow_non_contiguous=True), inc=c_setup, amt=16)
    b.em("sync", lambda e: e.dma_start(out=nul_t[:, :],
                                       in_=nu_log.rearrange("(c p) -> p c", p=128), allow_slow_non_contiguous=True), inc=c_setup, amt=16)
    b.em("sync", lambda e: e.dma_start(out=gml_t[:, :],
                                       in_=gamma_log.rearrange("(c p) -> p c", p=128), allow_slow_non_contiguous=True), inc=c_setup, amt=16)
    SETUP_DONE = c_setup.n

    b.wait("vector", c_iota, 1)
    b.em("vector", lambda e: e.tensor_scalar(ident[:, :], iotat[:, :], 0.0, None, A.is_equal))
    b.em("vector", lambda e: e.memset(ones_bf[0:1, :], 1.0))
    b.em("vector", lambda e: e.memset(eps_t[:, :], float(LN_EPS)))
    b.em("vector", lambda e: e.memset(fence_v[0:1, :], 0.0))
    b.em("vector", lambda e: e.memset(fence_s[0:1, :], 0.0))
    if debug:
        b.em("vector", lambda e: e.memset(dbgst1[:, :], 0.0))
        b.em("vector", lambda e: e.memset(dbgstb[:, :], 0.0))
    for z in (sxacc, uacc, gacc, g2acc, oacc):
        b.em("vector", lambda e, z=z: e.memset(z[:, :], 0.0))
    b.em("vector", lambda e: e.memset(redt[:, :], 0.0), inc=c_tiny)
    b.wait("scalar", c_setup, SETUP_DONE)
    b.wait("scalar", c_tiny, 1)
    b.em("scalar", lambda e: e.activation(lam_t[:, :], nul_t[:, :], ACT.Exp), inc=c_tiny)
    b.wait("vector", c_tiny, 2)
    b.em("vector", lambda e: e.tensor_tensor(gam_t[:, :], nul_t[:, :], gml_t[:, :], A.add), inc=c_tiny)
    b.wait("scalar", c_tiny, 3)
    b.em("scalar", lambda e: e.activation(gam_t[:, :], gam_t[:, :], ACT.Exp), inc=c_tiny)
    b.em("scalar", lambda e: e.activation(lam_t[:, :], lam_t[:, :], ACT.Exp, scale=-1.0), inc=c_tiny)
    b.wait("vector", c_tiny, 5)
    b.em("vector", lambda e: e.tensor_scalar(gam_t[:, :], gam_t[:, :], -1.0, None, A.add))
    b.wait("vector", c_setup, SETUP_DONE)
    b.em("vector", lambda e: e.tensor_scalar(brkv_h[:, :], brkv_t[:, 0:KC], 0.5, None, A.mult), inc=c_tiny)
    TINY_DONE = c_tiny.n

    def ln_stats(chan, srcs, rv_dst, b_dst, col0=0):
        ng = len(srcs)
        kk = 4
        for g, halves in enumerate(srcs):
            for h, (tt_, cc_) in enumerate(halves):
                last = (g == ng - 1) and (h == 1)
                b.em("vector", lambda e, g=g, h=h, tt_=tt_, cc_=cc_: e.bn_stats(
                    bnst[:, 12 * g + 6 * h:12 * g + 6 * h + 6], tt_[:, cc_:cc_ + 512]),
                    inc=(chan if last else None))
        b.wait("vector", chan)
        b.em("vector", lambda e, ng=ng: e.tensor_reduce(
            stat_m[:, 0:ng], bnst[:, 1:12 * ng:3].rearrange("p (g k) -> p g k", k=kk), AX, A.add))
        b.em("vector", lambda e, ng=ng: e.tensor_reduce(
            stat_t0[:, 0:ng], bnst[:, 2:12 * ng:3].rearrange("p (g k) -> p g k", k=kk), AX, A.add))
        b.em("vector", lambda e, ng=ng: e.tensor_tensor(
            stat_sq[:, 0:kk * ng], bnst[:, 1:12 * ng:3], bnst[:, 1:12 * ng:3], A.mult))
        b.em("vector", lambda e, ng=ng: e.tensor_reduce(
            stat_ss[:, 0:ng], stat_sq[:, 0:kk * ng].rearrange("p (g k) -> p g k", k=kk), AX, A.add),
            inc=chan)
        b.wait("vector", chan)
        b.em("vector", lambda e, ng=ng: e.tensor_scalar(
            stat_m[:, 0:ng], stat_m[:, 0:ng], 0.25, None, A.mult))
        b.em("vector", lambda e, ng=ng: e.tensor_scalar(
            stat_v[:, 0:ng], stat_t0[:, 0:ng], 1.0 / 1024.0, None, A.mult))
        b.em("vector", lambda e, ng=ng: e.scalar_tensor_tensor(
            stat_v[:, 0:ng], stat_ss[:, 0:ng], 0.25, stat_v[:, 0:ng], A.mult, A.add))
        b.em("vector", lambda e, ng=ng: e.tensor_tensor(
            stat_sq[:, 0:ng], stat_m[:, 0:ng], stat_m[:, 0:ng], A.mult))
        b.em("vector", lambda e, ng=ng: e.tensor_tensor(
            stat_v[:, 0:ng], stat_v[:, 0:ng], stat_sq[:, 0:ng], A.subtract), inc=chan)
        b.wait("scalar", chan)
        b.em("scalar", lambda e, ng=ng: e.activation(
            stat_std[:, 0:ng], stat_v[:, 0:ng], ACT.Sqrt, bias=eps_t[:, 0:1]), inc=chan)
        b.wait("vector", chan)
        b.em("vector", lambda e, ng=ng: e.reciprocal(stat_t0[:, 0:ng], stat_std[:, 0:ng]),
             inc=chan)
        b.wait("vector", chan)
        b.em("vector", lambda e, ng=ng: e.tensor_tensor(
            stat_sq[:, 0:ng], stat_t0[:, 0:ng], stat_t0[:, 0:ng], A.mult))
        b.em("vector", lambda e, ng=ng: e.tensor_scalar(
            stat_ss[:, 0:ng], stat_v[:, 0:ng], 0.5, float(0.5 * LN_EPS), A.mult, A.add))
        b.em("vector", lambda e, ng=ng: e.tensor_tensor(
            stat_ss[:, 0:ng], stat_ss[:, 0:ng], stat_sq[:, 0:ng], A.mult))
        b.em("vector", lambda e, ng=ng: e.tensor_scalar(
            stat_ss[:, 0:ng], stat_ss[:, 0:ng], -1.0, 1.5, A.mult, A.add))
        b.em("vector", lambda e, ng=ng, rv_dst=rv_dst, col0=col0: e.tensor_tensor(
            rv_dst[:, col0:col0 + ng], stat_t0[:, 0:ng], stat_ss[:, 0:ng], A.mult))
        b.em("vector", lambda e, ng=ng, rv_dst=rv_dst, b_dst=b_dst, col0=col0: e.scalar_tensor_tensor(
            b_dst[:, col0:col0 + ng], stat_m[:, 0:ng], -1.0,
            rv_dst[:, col0:col0 + ng], A.mult, A.mult), inc=chan)

    def barrier(cc_name, accs, out_tiles, qmaxs):
        n = len(accs)
        cin, cout = cc_bufs[cc_name]
        for i, acc in enumerate(accs):
            b.em("sync", lambda e, i=i, acc=acc, cin=cin: e.dma_start(
                out=cin[0:1, 128 * i:128 * (i + 1)], in_=acc[:, 0:1]), inc=c_bdma, amt=16)
        b.wait("gpsimd", c_bdma)
        b.em("gpsimd", lambda e, cin=cin, cout=cout: e.collective_compute(
            "AllReduce", A.max, replica_groups=[list(range(NCORES))],
            ins=[cin[:, :]], outs=[cout[:, :]]), inc=c_cc)
        b.wait("sync", c_cc)
        b.em("sync", lambda e, cout=cout, n=n: e.dma_start(
            out=ccs[0:1, 0:128 * n], in_=cout[:, :]), inc=c_bdma, amt=16)
        b.wait("vector", c_bdma)
        for i in range(n):
            q = qmaxs[i]
            b.em("vector", lambda e, i=i: e.tensor_reduce(
                gx1[0:1, 0:1], ccs[0:1, 128 * i:128 * (i + 1)], AX, A.max), inc=c_bv)
            b.wait("vector", c_bv)
            b.em("vector", lambda e, q=q: e.tensor_scalar(
                gx2[0:1, 0:1], gx1[0:1, 0:1], 1.0 / q, 1e-12, A.mult, A.max))
            b.em("vector", lambda e, i=i: e.tensor_copy(svals[0:1, 2 * i:2 * i + 1], gx2[0:1, 0:1]))
            b.em("vector", lambda e, i=i: e.reciprocal(svals[0:1, 2 * i + 1:2 * i + 2], gx2[0:1, 0:1]))
        b.em("vector", lambda e: e.tensor_copy(fence_v[0:1, 0:1], fence_v[0:1, 1:2]), inc=c_bv)
        b.wait("sync", c_bv)
        b.em("sync", lambda e, n=n: e.dma_start(out=scl_hbm[0:1, 0:2 * n],
                                                in_=svals[0:1, 0:2 * n]), inc=c_bdma, amt=16)
        b.wait("sync", c_bdma)
        for i in range(n):
            b.em("sync", lambda e, i=i, ot=out_tiles[i]: e.dma_start(
                out=ot[:, :], in_=scl_hbm[0:1, 2 * i:2 * i + 2].broadcast_to((128, 2))),
                inc=c_bdma, amt=16)

    NB = WPC // 32
    wp_vend = {}

    def make_prep(w_ap, whbm_bf, r, col0, idx):
        wch = (c_wpl, c_wpl2, c_wpl3, c_wpl4)[idx % WPD]

        def load():
            if idx >= WPD:
                b.wait("sync", c_wpv, wp_vend[idx - WPD])
            b.em("sync", lambda e: e.dma_start(
                out=wp_in[idx % WPD][:, :], in_=w_ap[r * 128:(r + 1) * 128, col0:col0 + WPC]),
                inc=wch, amt=16)

        def proc():
            win = wp_in[idx % WPD]
            wout = wp_out[idx % 2]
            b.wait("vector", wch)
            b.em("vector", lambda e, win=win: e.tensor_reduce(
                wp_bmax[:, 0:NB], win[:, :].rearrange("p (n k) -> p n k", k=32),
                AX, A.max, apply_absolute_value=True))
            b.em("vector", lambda e: e.tensor_scalar(
                wp_sb[:, 0:NB], wp_bmax[:, 0:NB], 1.0 / 7.0, 1e-12, A.mult, A.max))
            b.em("vector", lambda e: e.reciprocal(wp_isb[:, 0:NB], wp_sb[:, 0:NB]))
            b.em("vector", lambda e, win=win: e.tensor_tensor(
                wp_t1[:, :].rearrange("p (n k) -> p n k", k=32),
                win[:, :].rearrange("p (n k) -> p n k", k=32),
                wp_isb[:, 0:NB].rearrange("p (n o) -> p n o", o=1).broadcast_to((128, NB, 32)),
                A.mult))
            b.em("vector", lambda e: e.tensor_scalar(
                wp_t1[:, :], wp_t1[:, :], C_RND, -C_RND, A.add, A.add))
            b.em("vector", lambda e: e.tensor_copy(wp_i32[:, :], wp_t1[:, :]))
            b.em("vector", lambda e: e.tensor_copy(wp_t1[:, :], wp_i32[:, :]))
            if idx >= 2:
                b.wait("vector", c_wps if idx % 2 == 0 else c_wps2)
            b.em("vector", lambda e, wout=wout: e.tensor_tensor(
                wout[:, :].rearrange("p (n k) -> p n k", k=32),
                wp_t1[:, :].rearrange("p (n k) -> p n k", k=32),
                wp_sb[:, 0:NB].rearrange("p (n o) -> p n o", o=1).broadcast_to((128, NB, 32)),
                A.mult), inc=c_wpv)
            wp_vend[idx] = c_wpv.n
            b.wait("sync", c_wpv, wp_vend[idx])
            b.em("sync", lambda e, wout=wout: e.dma_start(
                out=whbm_bf[r * 128:(r + 1) * 128, col0:col0 + WPC], in_=wout[:, :]),
                inc=(c_wps if idx % 2 == 0 else c_wps2), amt=16)
        return load, proc

    prep_steps = []
    _idx = 0
    for w_ap, whbm in ((w_up, wup_bf), (w_gate, wgt_bf)):
        for r in range(MC):
            for cchunk in range(D // WPC):
                prep_steps.append(make_prep(w_ap, whbm, r, cchunk * WPC, _idx))
                _idx += 1
    WPS_UP = 16 * (MC * (D // WPC)) // 2
    WPS_GT = 2 * WPS_UP
    for r in range(KC):
        for cchunk in range(I3 // WPC):
            prep_steps.append(make_prep(w_down, wdn_bf, r, cchunk * WPC, _idx))
            _idx += 1
    WPS_DN = 16 * _idx // 2
    N_PREP = _idx
    prep_pos = [0]

    def do_prep(n):
        for _ in range(n):
            s = prep_pos[0]
            if s > N_PREP:
                return
            if s == 0:
                prep_steps[0][0]()
            if s + 1 < N_PREP:
                prep_steps[s + 1][0]()
            if s >= 1:
                prep_steps[s - 1][1]()
            prep_pos[0] = s + 1

    astack = ExitStack()
    wrkv_sb = [sb(astack, f"wrkv{k}", [128, 3 * D], BF16) for k in range(KC)]
    wo_sb = [sb(astack, f"wo{k}", [128, D], BF16) for k in range(KC)]
    xn_tm = [sb(astack, f"xn_tm{g}", [128, D], BF16) for g in range(G)]
    xn_fm = [sb(astack, f"xn_fm{f}", [128, TTOK], BF16) for f in range(KC)]
    k_c = [sb(astack, f"k_c{i}", [128, TTOK], F32) for i in range(2)]
    r_c = [sb(astack, f"r_c{i}", [128, TTOK], F32) for i in range(2)]
    kv_c = [sb(astack, f"kv_c{i}", [128, TTOK], F32) for i in range(2)]
    y1_c = sb(astack, "y1_c", [128, TTOK], F32)
    y2_c = sb(astack, "y2_c", [128, TTOK], F32)
    num_c = sb(astack, "num_c", [128, TTOK], F32)
    den_c = sb(astack, "den_c", [128, TTOK], F32)
    rec_c = sb(astack, "rec_c", [128, TTOK], F32)
    xatt_fm = [sb(astack, f"xatt{f}", [128, TTOK], BF16) for f in range(KC)]
    st1 = sb(astack, "st1", [128, KC], F32)
    st2 = sb(astack, "st2", [128, KC], F32)
    x1_g = [sb(astack, f"x1g{i}", [128, D], F32) for i in range(2)]

    for k in range(KC):
        b.em("gpsimd", lambda e, k=k: e.dma_start(out=wrkv_sb[k][:, :],
                                                  in_=w_rkv[k * 128:(k + 1) * 128, :]), inc=c_cast, amt=16)
        b.em("gpsimd", lambda e, k=k: e.dma_start(out=wo_sb[k][:, :],
                                                  in_=w_o[k * 128:(k + 1) * 128, :]), inc=c_cast, amt=16)
    b.em("gpsimd", lambda e: e.dma_start(out=bo_bf[0:1, :], in_=b_o[None, :]), inc=c_cast, amt=16)
    CAST_DONE = c_cast.n

    b.wait("vector", c_cast, CAST_DONE)
    for k in range(KC):
        b.em("vector", lambda e, k=k: e.tensor_scalar(
            wo_sb[k][:, :], wo_sb[k][:, :], 0.5, None, A.mult),
            inc=(c_woh if k == KC - 1 else None))

    b.wait("vector", c_tiny, 1)
    b.em("vector", lambda e: e.memset(st1[:, :], 0.0))
    b.em("vector", lambda e: e.memset(st2[:, :], 0.0), inc=c_tiny)
    TINY_DONE = c_tiny.n

    b.wait("tensor", c_cast, CAST_DONE)
    b.wait("tensor", c_woh)
    b.wait("tensor", c_tiny, 1)
    b.wait("vector", c_tiny, TINY_DONE)
    b.wait("scalar", c_tiny, TINY_DONE)
    b.wait("scalar", c_setup, SETUP_DONE)

    for t in range(NT):
        xb = x_tm[t % 2]
        if t >= 2:
            b.wait("sync", c_xfree, t - 1)
        for g in range(G):
            b.em("sync", lambda e, t=t, g=g, xb=xb: e.dma_start(
                out=xb[:, g * D:(g + 1) * D],
                in_=x_in[t * TTOK + g * 128: t * TTOK + (g + 1) * 128, :]),
                inc=(c_xload if t % 2 == 0 else c_xload2), amt=16)
        b.wait("vector", c_xload if t % 2 == 0 else c_xload2)
        ln_stats(c_ln1, [[(xb, g * D), (xb, g * D + 512)] for g in range(G)], rv1, bb1)
        do_prep(6)
        if t == 0:
            dsnap("vector", lambda: stat_v[:, 0:4], "dbg1", 0, 4)
            dsnap("vector", lambda: rv1[:, 0:4], "dbg1", 4, 4)
            dsnap("vector", lambda: bb1[:, 0:4], "dbg1", 8, 4)
        b.wait("scalar", c_ln1)
        if t >= 1:
            b.wait("scalar", c_tp)
        for g in range(G):
            b.em("scalar", lambda e, g=g, xb=xb: e.activation(
                xn_tm[g][:, :], xb[:, g * D:(g + 1) * D], ACT.Identity,
                bias=bb1[:, g:g + 1], scale=rv1[:, g:g + 1]), inc=c_xn)
        if t == 0:
            dsnap("vector", lambda: xn_tm[0][:, :], "dbgb", 0, 1024, wait_ch=c_xn)
        b.wait("tensor", c_xn)
        for f in range(KC):
            if c_tp.n >= 2:
                b.wait("tensor", c_tpe, c_tp.n - 1)
            for g in range(G):
                b.em("tensor", lambda e, f=f, g=g: e.transpose(
                    ptp[f % 2][:, g * 128:(g + 1) * 128], xn_tm[g][:, f * 128:(f + 1) * 128],
                    ident[:, :]), inc=(c_tp if g == G - 1 else None))
            b.wait("scalar", c_tp)
            if t >= 1 and f == 0:
                b.wait("scalar", c_rkv)
            b.em("scalar", lambda e, f=f: e.activation(
                xn_fm[f][:, :], ptp[f % 2][:, 0:TTOK], ACT.Identity), inc=c_tpe)
        if t == 0:
            dsnap("vector", lambda: xn_fm[0][:, :], "dbgb", 1024, 512, wait_ch=c_tpe)
        b.wait("tensor", c_tpe)
        if t >= 1:
            b.wait("vector", c_wo)
            b.wait("tensor", c_yn)
        for c in range(KC):
            kb, rb, vb = k_c[c % 2], r_c[c % 2], kv_c[c % 2]
            pk, pv, pr = pmm[c % 2], pmm[2 + c % 2], pmm[4 + c % 2]
            b.wait("tensor", c_kev, c_kev.n - 1)
            b.wait("tensor", c_vev, c_vev.n - 1)
            b.wait("tensor", c_rev, c_rev.n - 1)
            for kk in range(KC):
                b.em("tensor", lambda e, c=c, kk=kk, pk=pk: e.matmul(
                    pk[:, :], wrkv_sb[kk][:, (8 + c) * 128:(9 + c) * 128], xn_fm[kk][:, :],
                    start=(kk == 0), stop=(kk == KC - 1)), inc=(c_rkv if kk == KC - 1 else None))
            for kk in range(KC):
                b.em("tensor", lambda e, c=c, kk=kk, pv=pv: e.matmul(
                    pv[:, :], wrkv_sb[kk][:, (16 + c) * 128:(17 + c) * 128], xn_fm[kk][:, :],
                    start=(kk == 0), stop=(kk == KC - 1)), inc=(c_rkv if kk == KC - 1 else None))
            for kk in range(KC):
                b.em("tensor", lambda e, c=c, kk=kk, pr=pr: e.matmul(
                    pr[:, :], wrkv_sb[kk][:, c * 128:(c + 1) * 128], xn_fm[kk][:, :],
                    start=(kk == 0), stop=(kk == KC - 1)), inc=(c_rkv if kk == KC - 1 else None))
            b.wait("scalar", c_rkv, c_rkv.n - 2)
            if t * KC + c >= 2:
                b.wait("scalar", c_xat, c_xat.n - 1)
            b.em("scalar", lambda e, c=c, kb=kb, pk=pk: e.activation(
                kb[:, :], pk[:, :], ACT.Exp, bias=brkv_t[:, 8 + c:9 + c]), inc=c_kev)
            b.wait("scalar", c_rkv, c_rkv.n)
            b.em("scalar", lambda e, c=c, rb=rb, pr=pr: e.activation(
                rb[:, :], pr[:, :], ACT.Tanh, scale=0.5, bias=brkv_h[:, c:c + 1]), inc=c_rev)
            if t == 0 and c == 0:
                dsnap("vector", lambda: k_c[0][:, :], "dbg1", 512, 512, wait_ch=c_kev)
                dsnap("vector", lambda: r_c[0][:, :], "dbg1", 1024, 512, wait_ch=c_rev)
            b.wait("vector", c_rkv, c_rkv.n - 1)
            b.wait("vector", c_kev)
            b.em("vector", lambda e, c=c, kb=kb, vb=vb, pv=pv: e.scalar_tensor_tensor(
                vb[:, :], pv[:, :], brkv_t[:, 16 + c:17 + c], kb[:, :], A.add, A.mult), inc=c_vev)
            if t == 0 and c == 0:
                dsnap("vector", lambda: kv_c[0][:, :], "dbg1", 1536, 512)
            b.em("vector", lambda e, c=c, vb=vb: e.tensor_tensor_scan(
                y1_c[:, :], lam_t[:, c:c + 1].broadcast_to((128, TTOK)), vb[:, :],
                st1[:, c:c + 1], A.mult, A.add))
            b.em("vector", lambda e, c=c, kb=kb: e.tensor_tensor_scan(
                y2_c[:, :], lam_t[:, c:c + 1].broadcast_to((128, TTOK)), kb[:, :],
                st2[:, c:c + 1], A.mult, A.add), inc=c_vg)
            b.wait("vector", c_vg)
            b.em("vector", lambda e, c=c: e.tensor_copy(st1[:, c:c + 1], y1_c[:, TTOK - 1:TTOK]))
            b.em("vector", lambda e, c=c: e.tensor_copy(st2[:, c:c + 1], y2_c[:, TTOK - 1:TTOK]))
            if t == 0 and c == 0:
                dsnap("vector", lambda: y1_c[:, :], "dbg1", 2048, 512)
                dsnap("vector", lambda: y2_c[:, :], "dbg1", 2560, 512)
                dsnap("vector", lambda: lam_t[:, 0:8], "dbg1", 3584, 8)
                dsnap("vector", lambda: gam_t[:, 0:8], "dbg1", 3592, 8)
                dsnap("vector", lambda: brkv_t[:, 0:24], "dbg1", 3600, 24, wait_ch=c_setup)
            b.em("vector", lambda e, c=c, vb=vb: e.scalar_tensor_tensor(
                num_c[:, :], vb[:, :], gam_t[:, c:c + 1], y1_c[:, :], A.mult, A.add))
            b.em("vector", lambda e, c=c, kb=kb: e.scalar_tensor_tensor(
                den_c[:, :], kb[:, :], gam_t[:, c:c + 1], y2_c[:, :], A.mult, A.add))
            b.em("vector", lambda e: e.tensor_scalar(den_c[:, :], den_c[:, :], 1e-6, None, A.add))
            b.em("vector", lambda e: e.reciprocal(rec_c[:, :], den_c[:, :]))
            if t == 0 and c == 0:
                dsnap("vector", lambda: den_c[:, :], "dbg1", 3072, 512)
            b.em("vector", lambda e: e.tensor_tensor(num_c[:, :], num_c[:, :], rec_c[:, :], A.mult))
            b.wait("vector", c_rev)
            b.em("vector", lambda e, c=c, rb=rb: e.scalar_tensor_tensor(
                xatt_fm[c][:, :], rb[:, :], 1.0, num_c[:, :], A.add, A.mult), inc=c_xat)
            if t == 0 and c == 0:
                dsnap("vector", lambda: xatt_fm[0][:, :], "dbgb", 1536, 512)
            if c >= 2:
                do_prep(1)
        b.wait("tensor", c_xat)
        for g in range(G):
            pw0, pw1 = pmm[(2 * g) % 4], pmm[(2 * g + 1) % 4]
            for h in range(2):
                pw = (pw0, pw1)[h]
                b.wait("tensor", c_yn, c_wo.n - 3)
                for c in range(KC):
                    b.em("tensor", lambda e, c=c, g=g, h=h, pw=pw: e.matmul(
                        pw[:, :], xatt_fm[c][:, g * 128:(g + 1) * 128],
                        wo_sb[c][:, h * TTOK:(h + 1) * TTOK], start=(c == 0), stop=False))
                b.em("tensor", lambda e, g=g, h=h, pw=pw: e.matmul(
                    pw[:, :], ones_bf[0:1, :], bo_bf[0:1, h * TTOK:(h + 1) * TTOK],
                    start=False, stop=True), inc=c_wo)
            b.wait("vector", c_wo)
            ln_stats(c_ln2, [[(pw0, 0), (pw1, 0)]], rv2, bb2)
            b.wait("scalar", c_ln2)
            x1b = x1_g[g % 2]
            if t * G + g >= 2:
                b.wait("vector", c_xn2, c_xn2.n - 1)
                b.wait("vector", c_x1s if g % 2 == 0 else c_x1s2)
            for h in range(2):
                b.em("scalar", lambda e, h=h, pw=(pw0, pw1)[h]: e.activation(
                    yn_g[h][:, :], pw[:, :], ACT.Identity,
                    bias=bb2[:, 0:1], scale=rv2[:, 0:1]), inc=c_yn)
                if t == 0 and g == 0 and h == 0:
                    dsnap("vector", lambda: yn_g[0][:, 0:400], "dbg1", 3624, 400, wait_ch=c_yn)
                    dsnap("vector", lambda: rv2[:, 0:1], "dbg1", 4094, 1)
                    dsnap("vector", lambda: bb2[:, 0:1], "dbg1", 4095, 1)
                b.wait("vector", c_yn)
                b.em("vector", lambda e, g=g, h=h, xb=xb, x1b=x1b: e.tensor_tensor(
                    x1b[:, h * 512:(h + 1) * 512], xb[:, g * D + h * 512:g * D + (h + 1) * 512],
                    yn_g[h][:, :], A.add), inc=(c_x1 if h == 1 else None))
            if g == G - 1:
                b.em("vector", lambda e: e.tensor_copy(fence_v[0:1, 0:1], fence_v[0:1, 1:2]), inc=c_xfree)
            b.wait("sync", c_x1)
            b.em("sync", lambda e, t=t, g=g, x1b=x1b: e.dma_start(
                out=x1_hbm[t * TTOK + g * 128:t * TTOK + (g + 1) * 128, :], in_=x1b[:, :]),
                inc=(c_x1s if g % 2 == 0 else c_x1s2), amt=16)
            ln_stats(c_ln3, [[(x1b, 0), (x1b, 512)]], rv3_all, b3_all, col0=t * G + g)
            b.wait("scalar", c_ln3)
            if t * G + g >= 1:
                b.wait("scalar", c_sx)
            b.em("scalar", lambda e, t=t, g=g, x1b=x1b: e.activation(
                xn2_g[:, :], x1b[:, :], ACT.Identity,
                bias=b3_all[:, t * G + g:t * G + g + 1],
                scale=rv3_all[:, t * G + g:t * G + g + 1]), inc=c_xn2)
            b.wait("vector", c_xn2)
            b.em("vector", lambda e, t=t, g=g: e.tensor_reduce(
                sx_slots[:, t * G + g:t * G + g + 1], xn2_g[:, :], AX, A.max,
                apply_absolute_value=True), inc=c_sx)

    b.wait("vector", c_sx)
    b.em("vector", lambda e: e.tensor_reduce(sxacc[:, 0:1], sx_slots[:, :], AX, A.max), inc=c_sx)
    b.wait("sync", c_sx)
    barrier("cc1", [sxacc], [sx_b], [QA])
    B1_DMA = c_bdma.n
    astack.close()

    ustack = ExitStack()
    xq_tm = [sb(ustack, f"xq_tm{g}", [128, D], BF16) for g in range(G)]
    qt_g = sb(ustack, "qt_g", [128, D], F32)
    xqf = [sb(ustack, f"xqf{f}", [128, T // 2], BF16) for f in range(KC)]
    wT = [sb(ustack, f"wT{k}", [128, I3], BF16) for k in range(KC)]
    mm_stage = [sb(ustack, f"mmst{i}", [128, TTOK], F32) for i in range(8)]
    uq_hist = []

    b.wait("vector", c_bdma, B1_DMA)
    b.wait("scalar", c_ln3)
    b.wait("sync", c_x1s)
    b.wait("sync", c_x1s2)

    def xq_front(t, tloc):
        xb = x_tm[t % 2]
        if t >= 2:
            b.wait("sync", c_uxf, t - 1)
        for g in range(G):
            b.em("sync", lambda e, t=t, g=g, xb=xb: e.dma_start(
                out=xb[:, g * D:(g + 1) * D],
                in_=x1_hbm[t * TTOK + g * 128:t * TTOK + (g + 1) * 128, :]),
                inc=(c_xload if t % 2 == 0 else c_xload2), amt=16)
        b.wait("scalar", c_xload if t % 2 == 0 else c_xload2)
        b.wait("vector", c_tp)
        for g in range(G):
            xng = (xn2_g, xn2_g2)[g % 2]
            if len(uq_hist) >= 2:
                b.wait("scalar", c_uq, uq_hist[-2])
            b.em("scalar", lambda e, t=t, g=g, xb=xb, xng=xng: e.activation(
                xng[:, :], xb[:, g * D:(g + 1) * D], ACT.Identity,
                bias=b3_all[:, t * G + g:t * G + g + 1],
                scale=rv3_all[:, t * G + g:t * G + g + 1]), inc=c_xn2)
            b.wait("vector", c_xn2)
            b.em("vector", lambda e, xng=xng: e.tensor_scalar(
                qt_g[:, :], xng[:, :], sx_b[:, 1:2], C_RND, A.mult, A.add))
            b.em("vector", lambda e, g=g: e.tensor_scalar(
                xq_tm[g][:, :], qt_g[:, :], -C_RND, sx_b[:, 0:1], A.add, A.mult), inc=c_uq)
            uq_hist.append(c_uq.n)
        b.em("vector", lambda e: e.tensor_copy(fence_v[0:1, 0:1], fence_v[0:1, 1:2]), inc=c_uxf)
        do_prep(6)

    def xq_back(t, tloc, mm_wait=True):
        if mm_wait:
            b.wait("scalar", c_mm)
        b.wait("tensor", c_uq)
        for f in range(KC):
            if c_tp.n >= 2:
                b.wait("tensor", c_tpe, c_tp.n - 1)
            for g in range(G):
                b.em("tensor", lambda e, f=f, g=g: e.transpose(
                    ptp[f % 2][:, g * 128:(g + 1) * 128], xq_tm[g][:, f * 128:(f + 1) * 128],
                    ident[:, :]), inc=(c_tp if g == G - 1 else None))
            b.wait("scalar", c_tp)
            b.em("scalar", lambda e, f=f, tloc=tloc: e.activation(
                xqf[f][:, tloc * TTOK:(tloc + 1) * TTOK], ptp[f % 2][:, 0:TTOK], ACT.Identity),
                inc=c_tpe)
        b.em("scalar", lambda e: e.activation(fence_s[0:1, 0:1], fence_s[0:1, 1:2], ACT.Identity), inc=c_xq)

    mst_chs = (c_mst, c_mst2, c_mst3, c_mst4)

    def emit_ffn_mm(tglobal, tloc, dest_hbm, slots, mrange=None):
        for m in (mrange if mrange is not None else range(MC)):
            b.wait("tensor", c_mev, c_mm.n - 5)
            for kk in range(KC):
                b.em("tensor", lambda e, m=m, kk=kk, tloc=tloc: e.matmul(
                    pmm[m % 6][:, :], wT[kk][:, m * 128:(m + 1) * 128],
                    xqf[kk][:, tloc * TTOK:(tloc + 1) * TTOK],
                    start=(kk == 0), stop=(kk == KC - 1)), inc=(c_mm if kk == KC - 1 else None))
            b.wait("scalar", c_mm)
            stg = mm_stage[m % 8]
            b.wait("scalar", c_mred, c_mred.n - 7)
            b.wait("scalar", mst_chs[m % 4])
            b.em("scalar", lambda e, m=m, stg=stg: e.activation(
                stg[:, :], pmm[m % 6][:, :], ACT.Identity), inc=c_mev)
            b.wait("vector", c_mev)
            b.em("vector", lambda e, stg=stg, m=m, tglobal=tglobal: e.tensor_reduce(
                slots[:, tglobal * MC + m:tglobal * MC + m + 1], stg[:, :], AX, A.max,
                apply_absolute_value=True), inc=c_mred)
            b.wait("sync", c_mev)
            b.em("sync", lambda e, m=m, tglobal=tglobal, stg=stg, dest_hbm=dest_hbm: e.dma_start(
                out=dest_hbm[m * 128:(m + 1) * 128, tglobal * TTOK:(tglobal + 1) * TTOK],
                in_=stg[:, :]), inc=mst_chs[m % 4], amt=16)

    def load_wT(whbm, wps_val):
        b.wait("sync", c_wps, wps_val)
        b.wait("sync", c_wps2, wps_val)
        b.wait("sync", c_mm)
        for kk in range(KC):
            b.em("sync", lambda e, kk=kk, whbm=whbm: e.dma_start_transpose(
                wT[kk][:, 0:512], whbm[0:512, kk * 128:(kk + 1) * 128]), inc=c_wt, amt=16)
        for kk in range(KC):
            b.em("sync", lambda e, kk=kk, whbm=whbm: e.dma_start_transpose(
                wT[kk][:, 512:I3], whbm[512:I3, kk * 128:(kk + 1) * 128]), inc=c_wt2, amt=16)

    xq_front(0, 0)
    load_wT(wup_bf, WPS_UP)
    xq_back(0, 0, mm_wait=False)
    for half in range(2):
        for tloc in range(4):
            t = half * 4 + tloc
            b.wait("tensor", c_wt)
            b.wait("tensor", c_xq)
            emit_ffn_mm(t, tloc, up_hbm, acc_slots[0], range(0, 4))
            b.wait("tensor", c_wt2)
            if half == 0 and t + 1 <= 3:
                xq_front(t + 1, t + 1)
            emit_ffn_mm(t, tloc, up_hbm, acc_slots[0], range(4, MC))
            if half == 0 and t + 1 <= 3:
                xq_back(t + 1, t + 1, mm_wait=False)
        load_wT(wgt_bf, WPS_GT)
        for tloc in range(4):
            t = half * 4 + tloc
            b.wait("tensor", c_wt)
            emit_ffn_mm(t, tloc, gt_hbm, acc_slots[1], range(0, 4))
            b.wait("tensor", c_wt2)
            if half == 0:
                xq_front(4 + tloc, tloc)
            emit_ffn_mm(t, tloc, gt_hbm, acc_slots[1], range(4, MC))
            if half == 0:
                xq_back(4 + tloc, tloc, mm_wait=True)
        if half == 0:
            load_wT(wup_bf, WPS_UP)

    b.wait("vector", c_mred)
    b.em("vector", lambda e: e.tensor_reduce(uacc[:, 0:1], acc_slots[0][:, :], AX, A.max))
    b.em("vector", lambda e: e.tensor_reduce(gacc[:, 0:1], acc_slots[1][:, :], AX, A.max), inc=c_mred)
    b.wait("sync", c_mred)
    barrier("cc2", [uacc, gacc], [su_b, sg_b], [QA, QA])
    B2_DMA = c_bdma.n
    MEV_FINAL = c_mev.n
    ustack.close()

    fstack = ExitStack()
    FW = 2048
    NFW = (I3 * T) // (128 * FW)
    f_in = [sb(fstack, f"fin{i}", [128, FW], F32) for i in range(2)]
    f_in2 = [sb(fstack, f"fin2{i}", [128, FW], F32) for i in range(2)]
    f_t = sb(fstack, "f_t", [128, FW], F32)
    f_qp = [sb(fstack, f"f_qp{i}", [128, FW], F32) for i in range(2)]
    f_sigp = [sb(fstack, f"f_sigp{i}", [128, FW], F32) for i in range(2)]
    f_t2 = sb(fstack, "f_t2", [128, FW], F32)
    f_u16 = sb(fstack, "f_u16", [128, FW], F32)
    f_q2 = sb(fstack, "f_q2", [128, FW], F32)
    f_out = [sb(fstack, f"fout{i}", [128, FW], F32) for i in range(2)]
    f1_at, f2_at, fsg_at = {}, {}, {}

    b.wait("vector", c_bdma, B2_DMA)
    b.wait("sync", c_mst)
    b.wait("sync", c_mst2)
    b.wait("sync", c_mst3)
    b.wait("sync", c_mst4)

    def s2_load(i):
        m, tq = i // (T // FW), i % (T // FW)
        fin = f_in[i % 2]
        if i >= 2:
            b.wait("sync", c_f1, f1_at[i - 2])
        b.em("sync", lambda e, m=m, tq=tq, fin=fin: e.dma_start(
            out=fin[:, :], in_=gt_hbm[m * 128:(m + 1) * 128, tq * FW:(tq + 1) * FW]),
            inc=(c_fl if i % 2 == 0 else c_fl2), amt=16)

    def s2_front(i):
        fin, fq = f_in[i % 2], f_qp[i % 2]
        b.wait("vector", c_fl if i % 2 == 0 else c_fl2)
        b.em("vector", lambda e, fin=fin: e.tensor_scalar(
            f_t[:, :], fin[:, :], sg_b[:, 1:2], C_RND, A.mult, A.add))
        b.em("vector", lambda e, fq=fq: e.tensor_scalar(
            fq[:, :], f_t[:, :], -C_RND, sg_b[:, 0:1], A.add, A.mult), inc=c_f1)
        f1_at[i] = c_f1.n

    def s2_sig(i):
        b.wait("scalar", c_f1, f1_at[i])
        if i >= 2:
            b.wait("scalar", c_f2, f2_at[i - 2])
        b.em("scalar", lambda e, i=i: e.activation(
            f_sigp[i % 2][:, :], f_qp[i % 2][:, :], ACT.Sigmoid), inc=c_fsg)
        fsg_at[i] = c_fsg.n

    def s2_tail(i):
        fo = f_out[i % 2]
        b.wait("vector", c_fsg, fsg_at[i])
        b.em("vector", lambda e, i=i: e.tensor_scalar(
            f_t2[:, :], f_sigp[i % 2][:, :], 65536.0, C_RND, A.mult, A.add))
        b.em("vector", lambda e: e.tensor_scalar(
            f_u16[:, :], f_t2[:, :], -C_RND, 65535.0, A.add, A.min))
        if i >= 2:
            b.wait("vector", c_f3 if i % 2 == 0 else c_f32)
        b.em("vector", lambda e, fo=fo, i=i: e.scalar_tensor_tensor(
            fo[:, :], f_u16[:, :], float(2.0 ** -16), f_qp[i % 2][:, :], A.mult, A.mult))
        b.em("vector", lambda e, fo=fo, i=i: e.tensor_reduce(
            acc_slots[0][:, i:i + 1], fo[:, :], AX, A.max, apply_absolute_value=True),
            inc=c_f2)
        f2_at[i] = c_f2.n
        m, tq = i // (T // FW), i % (T // FW)
        b.wait("sync", c_f2, f2_at[i])
        b.em("sync", lambda e, m=m, tq=tq, fo=fo: e.dma_start(
            out=g2_hbm[m * 128:(m + 1) * 128, tq * FW:(tq + 1) * FW], in_=fo[:, :]),
            inc=(c_f3 if i % 2 == 0 else c_f32), amt=16)

    s2_load(0)
    s2_load(1)
    for i in range(NFW):
        if i % 6 == 0:
            do_prep(1)
        s2_front(i)
        if i + 2 < NFW:
            s2_load(i + 2)
        s2_sig(i)
        if i >= 1:
            s2_tail(i - 1)
    s2_tail(NFW - 1)

    b.wait("vector", c_f2)
    b.em("vector", lambda e: e.tensor_reduce(g2acc[:, 0:1], acc_slots[0][:, 0:NFW], AX, A.max), inc=c_f2)
    b.wait("sync", c_f2)
    barrier("cc3", [g2acc], [sg2_b], [QA])
    B3_DMA = c_bdma.n

    b.wait("vector", c_bdma, B3_DMA)
    b.wait("sync", c_f3)
    b.wait("sync", c_f32)
    f1o_at, f2o_at = {}, {}

    def o_load(i):
        m, tq = i // (T // FW), i % (T // FW)
        fin, fin2 = f_in[i % 2], f_in2[i % 2]
        if i >= 2:
            b.wait("sync", c_f1, f1o_at[i - 2])
        b.em("sync", lambda e, m=m, tq=tq, fin=fin: e.dma_start(
            out=fin[:, :], in_=g2_hbm[m * 128:(m + 1) * 128, tq * FW:(tq + 1) * FW]),
            inc=(c_fl if i % 2 == 0 else c_fl2), amt=16)
        b.em("sync", lambda e, m=m, tq=tq, fin2=fin2: e.dma_start(
            out=fin2[:, :], in_=up_hbm[m * 128:(m + 1) * 128, tq * FW:(tq + 1) * FW]),
            inc=(c_fl if i % 2 == 0 else c_fl2), amt=16)

    def o_body(i):
        m, tq = i // (T // FW), i % (T // FW)
        fin, fin2 = f_in[i % 2], f_in2[i % 2]
        fq = f_qp[0]
        b.wait("vector", c_fl if i % 2 == 0 else c_fl2)
        b.em("vector", lambda e, fin=fin: e.tensor_scalar(
            f_t[:, :], fin[:, :], sg2_b[:, 1:2], C_RND, A.mult, A.add))
        b.em("vector", lambda e, fq=fq: e.tensor_scalar(
            fq[:, :], f_t[:, :], -C_RND, sg2_b[:, 0:1], A.add, A.mult))
        b.em("vector", lambda e, fin2=fin2: e.tensor_scalar(
            f_t2[:, :], fin2[:, :], su_b[:, 1:2], C_RND, A.mult, A.add), inc=c_f1)
        f1o_at[i] = c_f1.n
        b.em("vector", lambda e: e.tensor_scalar(
            f_q2[:, :], f_t2[:, :], -C_RND, su_b[:, 0:1], A.add, A.mult))
        fo = f_out[i % 2]
        if i >= 2:
            b.wait("vector", c_f3 if i % 2 == 0 else c_f32)
        b.em("vector", lambda e, fo=fo, fq=fq: e.tensor_tensor(
            fo[:, :], fq[:, :], f_q2[:, :], A.mult))
        b.em("vector", lambda e, fo=fo, i=i: e.tensor_reduce(
            acc_slots[1][:, i:i + 1], fo[:, :], AX, A.max, apply_absolute_value=True),
            inc=c_f2)
        f2o_at[i] = c_f2.n
        b.wait("sync", c_f2, f2o_at[i])
        b.em("sync", lambda e, m=m, tq=tq, fo=fo: e.dma_start(
            out=o_hbm[m * 128:(m + 1) * 128, tq * FW:(tq + 1) * FW], in_=fo[:, :]),
            inc=(c_f3 if i % 2 == 0 else c_f32), amt=16)

    o_load(0)
    o_load(1)
    for i in range(NFW):
        o_body(i)
        if i + 2 < NFW:
            o_load(i + 2)

    b.wait("vector", c_f2)
    b.em("vector", lambda e: e.tensor_reduce(oacc[:, 0:1], acc_slots[1][:, 0:NFW], AX, A.max), inc=c_f2)
    b.wait("sync", c_f2)
    barrier("cc4", [oacc], [so_b], [QA])
    B4_DMA = c_bdma.n

    fstack.close()
    dstack = ExitStack()
    d_in = [sb(dstack, f"d_in{i}", [128, 512], F32) for i in range(8)]
    d_t = sb(dstack, "d_t", [128, 512], F32)
    oqf2 = [[sb(dstack, f"oqf{p}_{m}", [128, 512], BF16) for m in range(MC)]
            for p in range(2)]
    wdT = [sb(dstack, f"wdT{m}", [128, D], BF16) for m in range(MC)]
    xout_g = [sb(dstack, f"xoutg{i}", [128, D], F32) for i in range(2)]

    b.wait("vector", c_bdma, B4_DMA)
    b.wait("sync", c_f3)
    b.wait("sync", c_f32)
    b.wait("sync", c_x1s)
    b.wait("sync", c_x1s2)
    b.wait("sync", c_wps, WPS_DN)
    b.wait("sync", c_wps2, WPS_DN)
    for m in range(MC):
        b.em("sync", lambda e, m=m: e.dma_start_transpose(
            wdT[m][:, :], wdn_bf[:, m * 128:(m + 1) * 128]), inc=c_wt, amt=16)
    WT_DN = c_wt.n
    b.wait("tensor", c_mev, MEV_FINAL)

    dq_at, dl_at, dmm_end = {}, {}, {}

    def build_oqf_part(t, mlo, mhi):
        buf = oqf2[t % 2]
        oql_chs = (c_oql, c_oql2, c_oql3, c_oql4, c_oql5, c_oql6, c_oql7, c_oql8)
        if mlo == 0 and t >= 2:
            b.wait("vector", c_dmm, dmm_end[t - 2])
        for m in range(mlo, mhi):
            mi = t * MC + m
            if mi >= 8:
                b.wait("sync", c_oq, dl_at[mi - 8])
            b.em("sync", lambda e, m=m, t=t, fin=d_in[m % 8]: e.dma_start(
                out=fin[:, :], in_=o_hbm[m * 128:(m + 1) * 128, t * TTOK:(t + 1) * TTOK]),
                inc=oql_chs[m % 8], amt=16)
        for m in range(mlo, mhi):
            fin = d_in[m % 8]
            mi = t * MC + m
            b.wait("vector", oql_chs[m % 8])
            b.em("vector", lambda e, fin=fin: e.tensor_scalar(
                d_t[:, :], fin[:, :], so_b[:, 1:2], C_RND, A.mult, A.add))
            b.em("vector", lambda e, m=m, buf=buf: e.tensor_scalar(
                buf[m][:, :], d_t[:, :], -C_RND, so_b[:, 0:1], A.add, A.mult), inc=c_oq)
            dl_at[mi] = c_oq.n
        if mhi == MC:
            dq_at[t] = c_oq.n

    for p in range(6):
        build_oqf_part(0, 4 * p, 4 * p + 4)
    for t in range(NT):
        xb = x_tm[t % 2]
        nxt = t + 1
        if t >= 2:
            b.wait("sync", c_uxf, c_uxf.n - 1)
        for g in range(G):
            b.em("sync", lambda e, t=t, g=g, xb=xb: e.dma_start(
                out=xb[:, g * D:(g + 1) * D],
                in_=x1_hbm[t * TTOK + g * 128:t * TTOK + (g + 1) * 128, :]),
                inc=(c_xload if t % 2 == 0 else c_xload2), amt=16)
        if t == 0:
            b.wait("tensor", c_wt, WT_DN)
        b.wait("tensor", c_oq, dq_at[t])
        if nxt < NT:
            build_oqf_part(nxt, 0, 4)
            build_oqf_part(nxt, 4, 8)
        for g in range(G):
            s0 = 8 * t + 2 * g
            pw0, pw1 = pmm[s0 % 6], pmm[(s0 + 1) % 6]
            for h in range(2):
                pw = (pw0, pw1)[h]
                b.wait("tensor", c_yn4, c_dmm.n - 5)
                for m in range(MC):
                    b.em("tensor", lambda e, m=m, g=g, h=h, pw=pw, t=t: e.matmul(
                        pw[:, :], oqf2[t % 2][m][:, g * 128:(g + 1) * 128],
                        wdT[m][:, h * TTOK:(h + 1) * TTOK],
                        start=(m == 0), stop=(m == MC - 1)), inc=(c_dmm if m == MC - 1 else None))
            b.wait("vector", c_dmm)
            ln_stats(c_ln4, [[(pw0, 0), (pw1, 0)]], rv4, bb4)
            b.wait("scalar", c_ln4)
            b.wait("vector", c_xload if t % 2 == 0 else c_xload2)
            xob = xout_g[g % 2]
            if t * G + g >= 2:
                b.wait("vector", c_out if g % 2 == 0 else c_out2)
            for h in range(2):
                b.em("scalar", lambda e, h=h, pw=(pw0, pw1)[h]: e.activation(
                    yn_g[h][:, :], pw[:, :], ACT.Identity,
                    bias=bb4[:, 0:1], scale=rv4[:, 0:1]), inc=c_yn4)
                b.wait("vector", c_yn4)
                b.em("vector", lambda e, g=g, h=h, xb=xb, xob=xob: e.tensor_tensor(
                    xob[:, h * 512:(h + 1) * 512], xb[:, g * D + h * 512:g * D + (h + 1) * 512],
                    yn_g[h][:, :], A.add), inc=(c_xo if h == 1 else None))
            if g == G - 1:
                b.em("vector", lambda e: e.tensor_copy(fence_v[0:1, 0:1], fence_v[0:1, 1:2]), inc=c_uxf)
            b.wait("sync", c_xo)
            b.em("sync", lambda e, t=t, g=g, xob=xob: e.dma_start(
                out=out[t * TTOK + g * 128:t * TTOK + (g + 1) * 128, :], in_=xob[:, :]),
                inc=(c_out if g % 2 == 0 else c_out2), amt=16)
            if nxt < NT:
                if g == 0:
                    build_oqf_part(nxt, 8, 12)
                elif g == 1:
                    build_oqf_part(nxt, 12, 16)
                elif g == 2:
                    build_oqf_part(nxt, 16, 20)
                    build_oqf_part(nxt, 20, 24)
        dmm_end[t] = c_dmm.n

    dsnap_finish()
    b.wait("sync", c_out, c_out.n)
    b.wait("sync", c_out2, c_out2.n)
    dstack.close()

    with nc.Block() as block:
        @block.sync
        def _(e):
            for f in b.ops["sync"]:
                f(e)

        @block.scalar
        def _(e):
            for f in b.ops["scalar"]:
                f(e)

        @block.vector
        def _(e):
            for f in b.ops["vector"]:
                f(e)

        @block.tensor
        def _(e):
            for f in b.ops["tensor"]:
                f(e)

        @block.gpsimd
        def _(e):
            for f in b.ops["gpsimd"]:
                f(e)

    gstack.close()
    return nc


_NC_CACHE = {}


def kernel(L0_attn_inputs, L0_attn_W_rkv, L0_attn_b_rkv, L0_attn_W_o, L0_attn_b_o,
           L0_attn_nu_log, L0_attn_gamma_log, L0_ffn_w_gate, L0_ffn_w_up, L0_ffn_w_down):
    if "main" not in _NC_CACHE:
        _NC_CACHE["main"] = build_program(debug=False)
    nc = _NC_CACHE["main"]
    x = np.ascontiguousarray(np.asarray(L0_attn_inputs, dtype=np.float32))
    shared = {
        "w_rkv": np.ascontiguousarray(np.asarray(L0_attn_W_rkv, np.float32)),
        "b_rkv": np.ascontiguousarray(np.asarray(L0_attn_b_rkv, np.float32)),
        "w_o": np.ascontiguousarray(np.asarray(L0_attn_W_o, np.float32)),
        "b_o": np.ascontiguousarray(np.asarray(L0_attn_b_o, np.float32)),
        "nu_log": np.ascontiguousarray(np.asarray(L0_attn_nu_log, np.float32)),
        "gamma_log": np.ascontiguousarray(np.asarray(L0_attn_gamma_log, np.float32)),
        "w_gate": np.ascontiguousarray(np.asarray(L0_ffn_w_gate, np.float32)),
        "w_up": np.ascontiguousarray(np.asarray(L0_ffn_w_up, np.float32)),
        "w_down": np.ascontiguousarray(np.asarray(L0_ffn_w_down, np.float32)),
    }
    in_maps = [dict(shared, x=x[i]) for i in range(NCORES)]
    res = run_bass_kernel_spmd(nc, in_maps, core_ids=list(range(NCORES)))
    return np.stack([np.asarray(res.results[i]["out"], dtype=np.float32)
                     for i in range(NCORES)], axis=0)


# version 15
# speedup vs baseline: 1.0341x; 1.0138x over previous
import numpy as np
import concourse.bass as bass
import concourse.mybir as mybir
from concourse.bass_utils import run_bass_kernel_spmd

F32 = mybir.dt.float32
BF16 = mybir.dt.bfloat16
I32 = mybir.dt.int32
A = mybir.AluOpType
ACT = mybir.ActivationFunctionType
AX = mybir.AxisListType.X

C_RND = float(np.float32(12582912.0))
T = 4096
D = 1024
I3 = 3072
TTOK = 512
NT = T // TTOK
G = TTOK // 128
KC = D // 128
MC = I3 // 128
NCORES = 8
LN_EPS = 1e-5
QA = 32767.0


class Ch:
    def __init__(self, stack, nc, name):
        self.sem = stack.enter_context(nc.semaphore(name))
        self.n = 0
        self.rn = 0
        self.prefix = [(0, 0, -1)]

    def note(self, amt, kind, gvidx):
        self.n += amt
        c, r, g = self.prefix[-1]
        if kind == "r":
            self.rn += amt
            r = self.rn
        else:
            g = max(g, gvidx)
        self.prefix.append((self.n, r, g))

    def targets(self, val):
        import bisect
        i = bisect.bisect_left([p[0] for p in self.prefix], val)
        if i >= len(self.prefix):
            i = len(self.prefix) - 1
        c, r, g = self.prefix[i]
        return r, (g + 1 if g >= 0 else 0)


class B_:
    def __init__(self, nc, stack):
        self.nc = nc
        self.stack = stack
        self.ops = {k: [] for k in ("sync", "scalar", "vector", "tensor", "gpsimd")}
        self.gv = stack.enter_context(nc.semaphore("c_gv"))
        self.gv_n = 0

    def ch(self, name):
        return Ch(self.stack, self.nc, name)

    def em(self, eng, f, inc=None, amt=1):
        if eng == "vector":
            idx = self.gv_n
            self.gv_n += 1
            gsem = self.gv
            if inc is not None:
                inc.note(amt, "v", idx)

            def g(e, f=f, idx=idx, gsem=gsem):
                if idx > 0:
                    e.wait_ge(gsem, idx)
                f(e).then_inc(gsem, 1)
            self.ops[eng].append(g)
            return
        if inc is not None:
            inc.note(amt, "r", None)
            self.ops[eng].append(lambda e, f=f, s=inc.sem, a=amt: f(e).then_inc(s, a))
        else:
            self.ops[eng].append(f)

    def wait(self, eng, ch, val=None):
        v = ch.n if val is None else val
        if v <= 0:
            return
        rt, gt = ch.targets(v)
        if rt > 0:
            self.ops[eng].append(lambda e, s=ch.sem, v=rt: e.wait_ge(s, v))
        if gt > 0:
            self.ops[eng].append(lambda e, s=self.gv, v=gt: e.wait_ge(s, v))


def build_program(debug=False):
    from contextlib import ExitStack
    nc = bass.Bass("TRN2", target_bir_lowering=False, debug=False)
    gstack = ExitStack()
    b = B_(nc, gstack)

    def sb(st, name, shape, dt=F32):
        return st.enter_context(nc.sbuf_tensor(name, shape, dt))

    scratch_kind = {"kind": "ExternalOutput"} if debug else {}

    x_in = nc.dram_tensor("x", [T, D], F32, kind="ExternalInput").ap()
    w_rkv = nc.dram_tensor("w_rkv", [D, 3 * D], F32, kind="ExternalInput").ap()
    b_rkv = nc.dram_tensor("b_rkv", [3 * D], F32, kind="ExternalInput").ap()
    w_o = nc.dram_tensor("w_o", [D, D], F32, kind="ExternalInput").ap()
    b_o = nc.dram_tensor("b_o", [D], F32, kind="ExternalInput").ap()
    nu_log = nc.dram_tensor("nu_log", [D], F32, kind="ExternalInput").ap()
    gamma_log = nc.dram_tensor("gamma_log", [D], F32, kind="ExternalInput").ap()
    w_gate = nc.dram_tensor("w_gate", [I3, D], F32, kind="ExternalInput").ap()
    w_up = nc.dram_tensor("w_up", [I3, D], F32, kind="ExternalInput").ap()
    w_down = nc.dram_tensor("w_down", [D, I3], F32, kind="ExternalInput").ap()
    out = nc.dram_tensor("out", [T, D], F32, kind="ExternalOutput").ap()

    x1_hbm = nc.dram_tensor("x1_hbm", [T, D], F32, **scratch_kind).ap()
    up_hbm = nc.dram_tensor("up_hbm", [I3, T], F32, **scratch_kind).ap()
    gt_hbm = nc.dram_tensor("gt_hbm", [I3, T], F32, **scratch_kind).ap()
    g2_hbm = nc.dram_tensor("g2_hbm", [I3, T], F32, **scratch_kind).ap()
    o_hbm = nc.dram_tensor("o_hbm", [I3, T], F32, **scratch_kind).ap()
    wup_bf = nc.dram_tensor("wup_bf", [I3, D], BF16).ap()
    wgt_bf = nc.dram_tensor("wgt_bf", [I3, D], BF16).ap()
    wdn_bf = nc.dram_tensor("wdn_bf", [D, I3], BF16).ap()
    cc_bufs = {}
    for nm, n in [("cc1", 128), ("cc2", 256), ("cc3", 128), ("cc4", 128)]:
        cc_bufs[nm] = (nc.dram_tensor(nm + "_in", [1, n], F32).ap(),
                       nc.dram_tensor(nm + "_out", [1, n], F32).ap())
    scl_hbm = nc.dram_tensor("scl_hbm", [1, 8], F32).ap()

    ident = sb(gstack, "ident", [128, 128], BF16)
    iotat = sb(gstack, "iotat", [128, 128], F32)
    ones_bf = sb(gstack, "ones_bf", [1, 128], BF16)
    bo_bf = sb(gstack, "bo_bf", [1, D], BF16)
    brkv_t = sb(gstack, "brkv_t", [128, 3 * KC], F32)
    nul_t = sb(gstack, "nul_t", [128, KC], F32)
    gml_t = sb(gstack, "gml_t", [128, KC], F32)
    lam_t = sb(gstack, "lam_t", [128, KC], F32)
    gam_t = sb(gstack, "gam_t", [128, KC], F32)
    brkv_h = sb(gstack, "brkv_h", [128, KC], F32)

    x_tm = [sb(gstack, f"x_tm{i}", [128, G * D], F32) for i in range(2)]
    yn_g = [sb(gstack, f"yn{i}", [128, TTOK], F32) for i in range(2)]
    xn2_g = sb(gstack, "xn2_g", [128, D], F32)
    xn2_g2 = sb(gstack, "xn2_g2", [128, D], F32)

    bnst = sb(gstack, "bnst", [128, 48], F32)
    stat_m = sb(gstack, "stat_m", [128, G], F32)
    stat_t0 = sb(gstack, "stat_t0", [128, G], F32)
    stat_v = sb(gstack, "stat_v", [128, G], F32)
    stat_sq = sb(gstack, "stat_sq", [128, 16], F32)
    stat_ss = sb(gstack, "stat_ss", [128, G], F32)
    stat_std = sb(gstack, "stat_std", [128, G], F32)
    rv1 = sb(gstack, "rv1", [128, G], F32)
    bb1 = sb(gstack, "bb1", [128, G], F32)
    rv2 = sb(gstack, "rv2", [128, 1], F32)
    bb2 = sb(gstack, "bb2", [128, 1], F32)
    rv4 = sb(gstack, "rv4", [128, 1], F32)
    bb4 = sb(gstack, "bb4", [128, 1], F32)
    rv3_all = sb(gstack, "rv3_all", [128, NT * G], F32)
    b3_all = sb(gstack, "b3_all", [128, NT * G], F32)

    sxacc = sb(gstack, "sxacc", [128, 1], F32)
    sx_slots = sb(gstack, "sx_slots", [128, NT * G], F32)
    acc_slots = [sb(gstack, f"accsl{i}", [128, MC * NT], F32) for i in range(2)]
    uacc = sb(gstack, "uacc", [128, 1], F32)
    gacc = sb(gstack, "gacc", [128, 1], F32)
    g2acc = sb(gstack, "g2acc", [128, 1], F32)
    oacc = sb(gstack, "oacc", [128, 1], F32)
    redt = sb(gstack, "redt", [128, 1], F32)
    sx_b = sb(gstack, "sx_b", [128, 2], F32)
    su_b = sb(gstack, "su_b", [128, 2], F32)
    sg_b = sb(gstack, "sg_b", [128, 2], F32)
    sg2_b = sb(gstack, "sg2_b", [128, 2], F32)
    so_b = sb(gstack, "so_b", [128, 2], F32)
    ccs = sb(gstack, "ccs", [1, 256], F32)
    svals = sb(gstack, "svals", [1, 4], F32)
    gx1 = sb(gstack, "gx1", [1, 1], F32)
    gx2 = sb(gstack, "gx2", [1, 1], F32)
    fence_v = sb(gstack, "fence_v", [1, 2], F32)
    fence_s = sb(gstack, "fence_s", [1, 2], F32)
    eps_t = sb(gstack, "eps_t", [128, 1], F32)

    WPC = 1024
    WPD = 3
    wp_in = [sb(gstack, f"wp_in{i}", [128, WPC], F32) for i in range(WPD)]
    wp_bmax = sb(gstack, "wp_bmax", [128, WPC // 32], F32)
    wp_sb = sb(gstack, "wp_sb", [128, WPC // 32], F32)
    wp_isb = sb(gstack, "wp_isb", [128, WPC // 32], F32)
    wp_t1 = sb(gstack, "wp_t1", [128, WPC], F32)
    wp_i32 = sb(gstack, "wp_i32", [128, WPC], I32)
    wp_out = [sb(gstack, f"wp_out{i}", [128, WPC], BF16) for i in range(2)]

    ptp = [gstack.enter_context(nc.psum_tensor("ptp_a", [128, 1024], BF16)),
           gstack.enter_context(nc.psum_tensor("ptp_b", [128, 1024], BF16))]
    pmm = [gstack.enter_context(nc.psum_tensor(f"pmm{i}", [128, TTOK], F32)) for i in range(6)]

    names = ("c_setup c_cast c_tiny c_xload c_xfree c_ln1 c_xn c_tp c_tpe c_rkv "
             "c_kev c_rev c_vev c_xat c_wo c_ln2 c_yn c_x1 c_x1s c_ln3 c_xn2 c_sx "
             "c_cc c_bdma c_bv c_wpl c_wpg c_wpv c_wps c_wt c_xq c_mm c_mev c_mred "
             "c_mst c_fl c_f1 c_f2 c_f3 c_fsg c_uxf c_uq c_oq c_oql c_dmm c_ln4 "
             "c_yn4 c_xo c_out c_vg c_iota c_x1s2 c_mst2 c_f32 c_out2 c_xload2 c_fl2 c_oql2 c_wpl2 c_wps2 c_woh "
             "c_wpl3 c_wpl4 c_wt2 c_mst3 c_mst4 c_oql3 c_oql4 c_oql5 c_oql6 c_oql7 c_oql8").split()
    C = {nm: b.ch(nm) for nm in names}
    (c_setup, c_cast, c_tiny, c_xload, c_xfree, c_ln1, c_xn, c_tp, c_tpe, c_rkv,
     c_kev, c_rev, c_vev, c_xat, c_wo, c_ln2, c_yn, c_x1, c_x1s, c_ln3, c_xn2, c_sx,
     c_cc, c_bdma, c_bv, c_wpl, c_wpg, c_wpv, c_wps, c_wt, c_xq, c_mm, c_mev, c_mred,
     c_mst, c_fl, c_f1, c_f2, c_f3, c_fsg, c_uxf, c_uq, c_oq, c_oql, c_dmm, c_ln4,
     c_yn4, c_xo, c_out, c_vg, c_iota, c_x1s2, c_mst2, c_f32, c_out2, c_xload2, c_fl2, c_oql2, c_wpl2, c_wps2, c_woh,
     c_wpl3, c_wpl4, c_wt2, c_mst3, c_mst4, c_oql3, c_oql4, c_oql5, c_oql6, c_oql7, c_oql8) = (C[nm] for nm in names)

    c_dbgp = b.ch("c_dbgp")
    c_dbgs = b.ch("c_dbgs")
    if debug:
        dbg1 = nc.dram_tensor("dbg1", [128, 4096], F32, kind="ExternalOutput").ap()
        dbgb = nc.dram_tensor("dbgb", [128, 2048], BF16, kind="ExternalOutput").ap()
        dbgst1 = sb(gstack, "dbgst1", [128, 2048], F32)
        dbgstb = sb(gstack, "dbgstb", [128, 2048], BF16)

    def dsnap(eng, src_fn, dstname, col, w, wait_ch=None):
        if not debug:
            return
        if dstname == "dbg1" and col + w > 2048:
            return
        stg = dbgst1 if dstname == "dbg1" else dbgstb
        if wait_ch is not None:
            b.wait("vector", wait_ch)
        b.em("vector", lambda e: e.tensor_copy(stg[:, col:col + w], src_fn()), inc=c_dbgp)

    def dsnap_finish():
        if not debug:
            return
        b.wait("sync", c_dbgp)
        b.em("sync", lambda e: e.dma_start(out=dbg1[:, 0:2048], in_=dbgst1[:, :]), inc=c_dbgs, amt=16)
        b.em("sync", lambda e: e.dma_start(out=dbgb[:, :], in_=dbgstb[:, :]), inc=c_dbgs, amt=16)
        b.wait("sync", c_dbgs)

    b.em("gpsimd", lambda e: e.iota(iotat[:, :], [[1, 128]], base=0, channel_multiplier=-1,
                                    allow_small_or_imprecise_dtypes=True), inc=c_iota, amt=1)
    b.em("sync", lambda e: e.dma_start(out=brkv_t[:, :],
                                       in_=b_rkv.rearrange("(c p) -> p c", p=128), allow_slow_non_contiguous=True), inc=c_setup, amt=16)
    b.em("sync", lambda e: e.dma_start(out=nul_t[:, :],
                                       in_=nu_log.rearrange("(c p) -> p c", p=128), allow_slow_non_contiguous=True), inc=c_setup, amt=16)
    b.em("sync", lambda e: e.dma_start(out=gml_t[:, :],
                                       in_=gamma_log.rearrange("(c p) -> p c", p=128), allow_slow_non_contiguous=True), inc=c_setup, amt=16)
    SETUP_DONE = c_setup.n

    b.wait("vector", c_iota, 1)
    b.em("vector", lambda e: e.tensor_scalar(ident[:, :], iotat[:, :], 0.0, None, A.is_equal))
    b.em("vector", lambda e: e.memset(ones_bf[0:1, :], 1.0))
    b.em("vector", lambda e: e.memset(eps_t[:, :], float(LN_EPS)))
    b.em("vector", lambda e: e.memset(fence_v[0:1, :], 0.0))
    b.em("vector", lambda e: e.memset(fence_s[0:1, :], 0.0))
    if debug:
        b.em("vector", lambda e: e.memset(dbgst1[:, :], 0.0))
        b.em("vector", lambda e: e.memset(dbgstb[:, :], 0.0))
    for z in (sxacc, uacc, gacc, g2acc, oacc):
        b.em("vector", lambda e, z=z: e.memset(z[:, :], 0.0))
    b.em("vector", lambda e: e.memset(redt[:, :], 0.0), inc=c_tiny)
    b.wait("scalar", c_setup, SETUP_DONE)
    b.wait("scalar", c_tiny, 1)
    b.em("scalar", lambda e: e.activation(lam_t[:, :], nul_t[:, :], ACT.Exp), inc=c_tiny)
    b.wait("vector", c_tiny, 2)
    b.em("vector", lambda e: e.tensor_tensor(gam_t[:, :], nul_t[:, :], gml_t[:, :], A.add), inc=c_tiny)
    b.wait("scalar", c_tiny, 3)
    b.em("scalar", lambda e: e.activation(gam_t[:, :], gam_t[:, :], ACT.Exp), inc=c_tiny)
    b.em("scalar", lambda e: e.activation(lam_t[:, :], lam_t[:, :], ACT.Exp, scale=-1.0), inc=c_tiny)
    b.wait("vector", c_tiny, 5)
    b.em("vector", lambda e: e.tensor_scalar(gam_t[:, :], gam_t[:, :], -1.0, None, A.add))
    b.wait("vector", c_setup, SETUP_DONE)
    b.em("vector", lambda e: e.tensor_scalar(brkv_h[:, :], brkv_t[:, 0:KC], 0.5, None, A.mult), inc=c_tiny)
    TINY_DONE = c_tiny.n

    def ln_stats(chan, srcs, rv_dst, b_dst, col0=0):
        ng = len(srcs)
        kk = 4
        for g, halves in enumerate(srcs):
            for h, (tt_, cc_) in enumerate(halves):
                last = (g == ng - 1) and (h == 1)
                b.em("vector", lambda e, g=g, h=h, tt_=tt_, cc_=cc_: e.bn_stats(
                    bnst[:, 12 * g + 6 * h:12 * g + 6 * h + 6], tt_[:, cc_:cc_ + 512]),
                    inc=(chan if last else None))
        b.wait("vector", chan)
        b.em("vector", lambda e, ng=ng: e.tensor_reduce(
            stat_m[:, 0:ng], bnst[:, 1:12 * ng:3].rearrange("p (g k) -> p g k", k=kk), AX, A.add))
        b.em("vector", lambda e, ng=ng: e.tensor_reduce(
            stat_t0[:, 0:ng], bnst[:, 2:12 * ng:3].rearrange("p (g k) -> p g k", k=kk), AX, A.add))
        b.em("vector", lambda e, ng=ng: e.tensor_tensor(
            stat_sq[:, 0:kk * ng], bnst[:, 1:12 * ng:3], bnst[:, 1:12 * ng:3], A.mult))
        b.em("vector", lambda e, ng=ng: e.tensor_reduce(
            stat_ss[:, 0:ng], stat_sq[:, 0:kk * ng].rearrange("p (g k) -> p g k", k=kk), AX, A.add),
            inc=chan)
        b.wait("vector", chan)
        b.em("vector", lambda e, ng=ng: e.tensor_scalar(
            stat_m[:, 0:ng], stat_m[:, 0:ng], 0.25, None, A.mult))
        b.em("vector", lambda e, ng=ng: e.tensor_scalar(
            stat_v[:, 0:ng], stat_t0[:, 0:ng], 1.0 / 1024.0, None, A.mult))
        b.em("vector", lambda e, ng=ng: e.scalar_tensor_tensor(
            stat_v[:, 0:ng], stat_ss[:, 0:ng], 0.25, stat_v[:, 0:ng], A.mult, A.add))
        b.em("vector", lambda e, ng=ng: e.tensor_tensor(
            stat_sq[:, 0:ng], stat_m[:, 0:ng], stat_m[:, 0:ng], A.mult))
        b.em("vector", lambda e, ng=ng: e.tensor_tensor(
            stat_v[:, 0:ng], stat_v[:, 0:ng], stat_sq[:, 0:ng], A.subtract), inc=chan)
        b.wait("scalar", chan)
        b.em("scalar", lambda e, ng=ng: e.activation(
            stat_std[:, 0:ng], stat_v[:, 0:ng], ACT.Sqrt, bias=eps_t[:, 0:1]), inc=chan)
        b.wait("vector", chan)
        b.em("vector", lambda e, ng=ng: e.reciprocal(stat_t0[:, 0:ng], stat_std[:, 0:ng]),
             inc=chan)
        b.wait("vector", chan)
        b.em("vector", lambda e, ng=ng: e.tensor_tensor(
            stat_sq[:, 0:ng], stat_t0[:, 0:ng], stat_t0[:, 0:ng], A.mult))
        b.em("vector", lambda e, ng=ng: e.tensor_scalar(
            stat_ss[:, 0:ng], stat_v[:, 0:ng], 0.5, float(0.5 * LN_EPS), A.mult, A.add))
        b.em("vector", lambda e, ng=ng: e.tensor_tensor(
            stat_ss[:, 0:ng], stat_ss[:, 0:ng], stat_sq[:, 0:ng], A.mult))
        b.em("vector", lambda e, ng=ng: e.tensor_scalar(
            stat_ss[:, 0:ng], stat_ss[:, 0:ng], -1.0, 1.5, A.mult, A.add))
        b.em("vector", lambda e, ng=ng, rv_dst=rv_dst, col0=col0: e.tensor_tensor(
            rv_dst[:, col0:col0 + ng], stat_t0[:, 0:ng], stat_ss[:, 0:ng], A.mult))
        b.em("vector", lambda e, ng=ng, rv_dst=rv_dst, b_dst=b_dst, col0=col0: e.scalar_tensor_tensor(
            b_dst[:, col0:col0 + ng], stat_m[:, 0:ng], -1.0,
            rv_dst[:, col0:col0 + ng], A.mult, A.mult), inc=chan)

    def barrier(cc_name, accs, out_tiles, qmaxs):
        n = len(accs)
        cin, cout = cc_bufs[cc_name]
        for i, acc in enumerate(accs):
            b.em("sync", lambda e, i=i, acc=acc, cin=cin: e.dma_start(
                out=cin[0:1, 128 * i:128 * (i + 1)], in_=acc[:, 0:1]), inc=c_bdma, amt=16)
        b.wait("gpsimd", c_bdma)
        b.em("gpsimd", lambda e, cin=cin, cout=cout: e.collective_compute(
            "AllReduce", A.max, replica_groups=[list(range(NCORES))],
            ins=[cin[:, :]], outs=[cout[:, :]]), inc=c_cc)
        b.wait("sync", c_cc)
        b.em("sync", lambda e, cout=cout, n=n: e.dma_start(
            out=ccs[0:1, 0:128 * n], in_=cout[:, :]), inc=c_bdma, amt=16)
        b.wait("vector", c_bdma)
        for i in range(n):
            q = qmaxs[i]
            b.em("vector", lambda e, i=i: e.tensor_reduce(
                gx1[0:1, 0:1], ccs[0:1, 128 * i:128 * (i + 1)], AX, A.max), inc=c_bv)
            b.wait("vector", c_bv)
            b.em("vector", lambda e, q=q: e.tensor_scalar(
                gx2[0:1, 0:1], gx1[0:1, 0:1], 1.0 / q, 1e-12, A.mult, A.max))
            b.em("vector", lambda e, i=i: e.tensor_copy(svals[0:1, 2 * i:2 * i + 1], gx2[0:1, 0:1]))
            b.em("vector", lambda e, i=i: e.reciprocal(svals[0:1, 2 * i + 1:2 * i + 2], gx2[0:1, 0:1]))
        b.em("vector", lambda e: e.tensor_copy(fence_v[0:1, 0:1], fence_v[0:1, 1:2]), inc=c_bv)
        b.wait("sync", c_bv)
        b.em("sync", lambda e, n=n: e.dma_start(out=scl_hbm[0:1, 0:2 * n],
                                                in_=svals[0:1, 0:2 * n]), inc=c_bdma, amt=16)
        b.wait("sync", c_bdma)
        for i in range(n):
            b.em("sync", lambda e, i=i, ot=out_tiles[i]: e.dma_start(
                out=ot[:, :], in_=scl_hbm[0:1, 2 * i:2 * i + 2].broadcast_to((128, 2))),
                inc=c_bdma, amt=16)

    NB = WPC // 32
    wp_vend = {}

    def make_prep(w_ap, whbm_bf, r, col0, idx):
        wch = (c_wpl, c_wpl2, c_wpl3, c_wpl4)[idx % WPD]

        def load():
            if idx >= WPD:
                b.wait("sync", c_wpv, wp_vend[idx - WPD])
            b.em("sync", lambda e: e.dma_start(
                out=wp_in[idx % WPD][:, :], in_=w_ap[r * 128:(r + 1) * 128, col0:col0 + WPC]),
                inc=wch, amt=16)

        def proc():
            win = wp_in[idx % WPD]
            wout = wp_out[idx % 2]
            b.wait("vector", wch)
            b.em("vector", lambda e, win=win: e.tensor_reduce(
                wp_bmax[:, 0:NB], win[:, :].rearrange("p (n k) -> p n k", k=32),
                AX, A.max, apply_absolute_value=True))
            b.em("vector", lambda e: e.tensor_scalar(
                wp_sb[:, 0:NB], wp_bmax[:, 0:NB], 1.0 / 7.0, 1e-12, A.mult, A.max))
            b.em("vector", lambda e: e.reciprocal(wp_isb[:, 0:NB], wp_sb[:, 0:NB]))
            b.em("vector", lambda e, win=win: e.tensor_tensor(
                wp_t1[:, :].rearrange("p (n k) -> p n k", k=32),
                win[:, :].rearrange("p (n k) -> p n k", k=32),
                wp_isb[:, 0:NB].rearrange("p (n o) -> p n o", o=1).broadcast_to((128, NB, 32)),
                A.mult))
            b.em("vector", lambda e: e.tensor_scalar(
                wp_t1[:, :], wp_t1[:, :], C_RND, -C_RND, A.add, A.add))
            b.em("vector", lambda e: e.tensor_copy(wp_i32[:, :], wp_t1[:, :]))
            b.em("vector", lambda e: e.tensor_copy(wp_t1[:, :], wp_i32[:, :]))
            if idx >= 2:
                b.wait("vector", c_wps if idx % 2 == 0 else c_wps2)
            b.em("vector", lambda e, wout=wout: e.tensor_tensor(
                wout[:, :].rearrange("p (n k) -> p n k", k=32),
                wp_t1[:, :].rearrange("p (n k) -> p n k", k=32),
                wp_sb[:, 0:NB].rearrange("p (n o) -> p n o", o=1).broadcast_to((128, NB, 32)),
                A.mult), inc=c_wpv)
            wp_vend[idx] = c_wpv.n
            b.wait("sync", c_wpv, wp_vend[idx])
            b.em("sync", lambda e, wout=wout: e.dma_start(
                out=whbm_bf[r * 128:(r + 1) * 128, col0:col0 + WPC], in_=wout[:, :]),
                inc=(c_wps if idx % 2 == 0 else c_wps2), amt=16)
        return load, proc

    prep_steps = []
    _idx = 0
    for w_ap, whbm in ((w_up, wup_bf), (w_gate, wgt_bf)):
        for r in range(MC):
            for cchunk in range(D // WPC):
                prep_steps.append(make_prep(w_ap, whbm, r, cchunk * WPC, _idx))
                _idx += 1
    WPS_UP = 16 * (MC * (D // WPC)) // 2
    WPS_GT = 2 * WPS_UP
    for r in range(KC):
        for cchunk in range(I3 // WPC):
            prep_steps.append(make_prep(w_down, wdn_bf, r, cchunk * WPC, _idx))
            _idx += 1
    WPS_DN = 16 * _idx // 2
    N_PREP = _idx
    prep_pos = [0]

    def do_prep(n):
        for _ in range(n):
            s = prep_pos[0]
            if s > N_PREP:
                return
            if s == 0:
                prep_steps[0][0]()
            if s + 1 < N_PREP:
                prep_steps[s + 1][0]()
            if s >= 1:
                prep_steps[s - 1][1]()
            prep_pos[0] = s + 1

    astack = ExitStack()
    wrkv_sb = [sb(astack, f"wrkv{k}", [128, 3 * D], BF16) for k in range(KC)]
    wo_sb = [sb(astack, f"wo{k}", [128, D], BF16) for k in range(KC)]
    xn_tm = [sb(astack, f"xn_tm{g}", [128, D], BF16) for g in range(G)]
    xn_fm = [sb(astack, f"xn_fm{f}", [128, TTOK], BF16) for f in range(KC)]
    k_c = [sb(astack, f"k_c{i}", [128, TTOK], F32) for i in range(2)]
    r_c = [sb(astack, f"r_c{i}", [128, TTOK], F32) for i in range(2)]
    kv_c = [sb(astack, f"kv_c{i}", [128, TTOK], F32) for i in range(2)]
    y1_c = sb(astack, "y1_c", [128, TTOK], F32)
    y2_c = sb(astack, "y2_c", [128, TTOK], F32)
    num_c = sb(astack, "num_c", [128, TTOK], F32)
    den_c = sb(astack, "den_c", [128, TTOK], F32)
    rec_c = sb(astack, "rec_c", [128, TTOK], F32)
    xatt_fm = [sb(astack, f"xatt{f}", [128, TTOK], BF16) for f in range(KC)]
    st1 = sb(astack, "st1", [128, KC], F32)
    st2 = sb(astack, "st2", [128, KC], F32)
    x1_g = [sb(astack, f"x1g{i}", [128, D], F32) for i in range(2)]

    for k in range(KC):
        b.em("gpsimd", lambda e, k=k: e.dma_start(out=wrkv_sb[k][:, :],
                                                  in_=w_rkv[k * 128:(k + 1) * 128, :]), inc=c_cast, amt=16)
        b.em("gpsimd", lambda e, k=k: e.dma_start(out=wo_sb[k][:, :],
                                                  in_=w_o[k * 128:(k + 1) * 128, :]), inc=c_cast, amt=16)
    b.em("gpsimd", lambda e: e.dma_start(out=bo_bf[0:1, :], in_=b_o[None, :]), inc=c_cast, amt=16)
    CAST_DONE = c_cast.n

    b.wait("vector", c_cast, CAST_DONE)
    for k in range(KC):
        b.em("vector", lambda e, k=k: e.tensor_scalar(
            wo_sb[k][:, :], wo_sb[k][:, :], 0.5, None, A.mult),
            inc=(c_woh if k == KC - 1 else None))

    b.wait("vector", c_tiny, 1)
    b.em("vector", lambda e: e.memset(st1[:, :], 0.0))
    b.em("vector", lambda e: e.memset(st2[:, :], 0.0), inc=c_tiny)
    TINY_DONE = c_tiny.n

    b.wait("tensor", c_cast, CAST_DONE)
    b.wait("tensor", c_woh)
    b.wait("tensor", c_tiny, 1)
    b.wait("vector", c_tiny, TINY_DONE)
    b.wait("scalar", c_tiny, TINY_DONE)
    b.wait("scalar", c_setup, SETUP_DONE)

    for t in range(NT):
        xb = x_tm[t % 2]
        if t >= 2:
            b.wait("sync", c_xfree, t - 1)
        for g in range(G):
            b.em("sync", lambda e, t=t, g=g, xb=xb: e.dma_start(
                out=xb[:, g * D:(g + 1) * D],
                in_=x_in[t * TTOK + g * 128: t * TTOK + (g + 1) * 128, :]),
                inc=(c_xload if t % 2 == 0 else c_xload2), amt=16)
        b.wait("vector", c_xload if t % 2 == 0 else c_xload2)
        ln_stats(c_ln1, [[(xb, g * D), (xb, g * D + 512)] for g in range(G)], rv1, bb1)
        do_prep(3)
        if t == 0:
            dsnap("vector", lambda: stat_v[:, 0:4], "dbg1", 0, 4)
            dsnap("vector", lambda: rv1[:, 0:4], "dbg1", 4, 4)
            dsnap("vector", lambda: bb1[:, 0:4], "dbg1", 8, 4)
        b.wait("scalar", c_ln1)
        if t >= 1:
            b.wait("scalar", c_tp)
        for g in range(G):
            b.em("scalar", lambda e, g=g, xb=xb: e.activation(
                xn_tm[g][:, :], xb[:, g * D:(g + 1) * D], ACT.Identity,
                bias=bb1[:, g:g + 1], scale=rv1[:, g:g + 1]), inc=c_xn)
        if t == 0:
            dsnap("vector", lambda: xn_tm[0][:, :], "dbgb", 0, 1024, wait_ch=c_xn)
        b.wait("tensor", c_xn)
        for f in range(KC):
            if c_tp.n >= 2:
                b.wait("tensor", c_tpe, c_tp.n - 1)
            for g in range(G):
                b.em("tensor", lambda e, f=f, g=g: e.transpose(
                    ptp[f % 2][:, g * 128:(g + 1) * 128], xn_tm[g][:, f * 128:(f + 1) * 128],
                    ident[:, :]), inc=(c_tp if g == G - 1 else None))
            b.wait("scalar", c_tp)
            if t >= 1 and f == 0:
                b.wait("scalar", c_rkv)
            b.em("scalar", lambda e, f=f: e.activation(
                xn_fm[f][:, :], ptp[f % 2][:, 0:TTOK], ACT.Identity), inc=c_tpe)
        if t == 0:
            dsnap("vector", lambda: xn_fm[0][:, :], "dbgb", 1024, 512, wait_ch=c_tpe)
        b.wait("tensor", c_tpe)
        if t >= 1:
            b.wait("vector", c_wo)
            b.wait("tensor", c_yn)
        for c in range(KC):
            kb, rb, vb = k_c[c % 2], r_c[c % 2], kv_c[c % 2]
            pk, pv, pr = pmm[c % 2], pmm[2 + c % 2], pmm[4 + c % 2]
            b.wait("tensor", c_kev, c_kev.n - 1)
            b.wait("tensor", c_vev, c_vev.n - 1)
            b.wait("tensor", c_rev, c_rev.n - 1)
            for kk in range(KC):
                b.em("tensor", lambda e, c=c, kk=kk, pk=pk: e.matmul(
                    pk[:, :], wrkv_sb[kk][:, (8 + c) * 128:(9 + c) * 128], xn_fm[kk][:, :],
                    start=(kk == 0), stop=(kk == KC - 1)), inc=(c_rkv if kk == KC - 1 else None))
            for kk in range(KC):
                b.em("tensor", lambda e, c=c, kk=kk, pv=pv: e.matmul(
                    pv[:, :], wrkv_sb[kk][:, (16 + c) * 128:(17 + c) * 128], xn_fm[kk][:, :],
                    start=(kk == 0), stop=(kk == KC - 1)), inc=(c_rkv if kk == KC - 1 else None))
            for kk in range(KC):
                b.em("tensor", lambda e, c=c, kk=kk, pr=pr: e.matmul(
                    pr[:, :], wrkv_sb[kk][:, c * 128:(c + 1) * 128], xn_fm[kk][:, :],
                    start=(kk == 0), stop=(kk == KC - 1)), inc=(c_rkv if kk == KC - 1 else None))
            b.wait("scalar", c_rkv, c_rkv.n - 2)
            if t * KC + c >= 2:
                b.wait("scalar", c_xat, c_xat.n - 1)
            b.em("scalar", lambda e, c=c, kb=kb, pk=pk: e.activation(
                kb[:, :], pk[:, :], ACT.Exp, bias=brkv_t[:, 8 + c:9 + c]), inc=c_kev)
            b.wait("scalar", c_rkv, c_rkv.n)
            b.em("scalar", lambda e, c=c, rb=rb, pr=pr: e.activation(
                rb[:, :], pr[:, :], ACT.Tanh, scale=0.5, bias=brkv_h[:, c:c + 1]), inc=c_rev)
            if t == 0 and c == 0:
                dsnap("vector", lambda: k_c[0][:, :], "dbg1", 512, 512, wait_ch=c_kev)
                dsnap("vector", lambda: r_c[0][:, :], "dbg1", 1024, 512, wait_ch=c_rev)
            b.wait("vector", c_rkv, c_rkv.n - 1)
            b.wait("vector", c_kev)
            b.em("vector", lambda e, c=c, kb=kb, vb=vb, pv=pv: e.scalar_tensor_tensor(
                vb[:, :], pv[:, :], brkv_t[:, 16 + c:17 + c], kb[:, :], A.add, A.mult), inc=c_vev)
            if t == 0 and c == 0:
                dsnap("vector", lambda: kv_c[0][:, :], "dbg1", 1536, 512)
            b.em("vector", lambda e, c=c, vb=vb: e.tensor_tensor_scan(
                y1_c[:, :], lam_t[:, c:c + 1].broadcast_to((128, TTOK)), vb[:, :],
                st1[:, c:c + 1], A.mult, A.add))
            b.em("vector", lambda e, c=c, kb=kb: e.tensor_tensor_scan(
                y2_c[:, :], lam_t[:, c:c + 1].broadcast_to((128, TTOK)), kb[:, :],
                st2[:, c:c + 1], A.mult, A.add), inc=c_vg)
            b.wait("vector", c_vg)
            b.em("vector", lambda e, c=c: e.tensor_copy(st1[:, c:c + 1], y1_c[:, TTOK - 1:TTOK]))
            b.em("vector", lambda e, c=c: e.tensor_copy(st2[:, c:c + 1], y2_c[:, TTOK - 1:TTOK]))
            if t == 0 and c == 0:
                dsnap("vector", lambda: y1_c[:, :], "dbg1", 2048, 512)
                dsnap("vector", lambda: y2_c[:, :], "dbg1", 2560, 512)
                dsnap("vector", lambda: lam_t[:, 0:8], "dbg1", 3584, 8)
                dsnap("vector", lambda: gam_t[:, 0:8], "dbg1", 3592, 8)
                dsnap("vector", lambda: brkv_t[:, 0:24], "dbg1", 3600, 24, wait_ch=c_setup)
            b.em("vector", lambda e, c=c, vb=vb: e.scalar_tensor_tensor(
                num_c[:, :], vb[:, :], gam_t[:, c:c + 1], y1_c[:, :], A.mult, A.add))
            b.em("vector", lambda e, c=c, kb=kb: e.scalar_tensor_tensor(
                den_c[:, :], kb[:, :], gam_t[:, c:c + 1], y2_c[:, :], A.mult, A.add))
            b.em("vector", lambda e: e.tensor_scalar(den_c[:, :], den_c[:, :], 1e-6, None, A.add))
            b.em("vector", lambda e: e.reciprocal(rec_c[:, :], den_c[:, :]))
            if t == 0 and c == 0:
                dsnap("vector", lambda: den_c[:, :], "dbg1", 3072, 512)
            b.em("vector", lambda e: e.tensor_tensor(num_c[:, :], num_c[:, :], rec_c[:, :], A.mult))
            b.wait("vector", c_rev)
            b.em("vector", lambda e, c=c, rb=rb: e.scalar_tensor_tensor(
                xatt_fm[c][:, :], rb[:, :], 1.0, num_c[:, :], A.add, A.mult), inc=c_xat)
            if t == 0 and c == 0:
                dsnap("vector", lambda: xatt_fm[0][:, :], "dbgb", 1536, 512)
            if 2 <= c <= 4:
                do_prep(1)
        b.wait("tensor", c_xat)
        for g in range(G):
            pw0, pw1 = pmm[(2 * g) % 4], pmm[(2 * g + 1) % 4]
            for h in range(2):
                pw = (pw0, pw1)[h]
                b.wait("tensor", c_yn, c_wo.n - 3)
                for c in range(KC):
                    b.em("tensor", lambda e, c=c, g=g, h=h, pw=pw: e.matmul(
                        pw[:, :], xatt_fm[c][:, g * 128:(g + 1) * 128],
                        wo_sb[c][:, h * TTOK:(h + 1) * TTOK], start=(c == 0), stop=False))
                b.em("tensor", lambda e, g=g, h=h, pw=pw: e.matmul(
                    pw[:, :], ones_bf[0:1, :], bo_bf[0:1, h * TTOK:(h + 1) * TTOK],
                    start=False, stop=True), inc=c_wo)
            b.wait("vector", c_wo)
            ln_stats(c_ln2, [[(pw0, 0), (pw1, 0)]], rv2, bb2)
            b.wait("scalar", c_ln2)
            x1b = x1_g[g % 2]
            if t * G + g >= 2:
                b.wait("vector", c_xn2, c_xn2.n - 1)
                b.wait("vector", c_x1s if g % 2 == 0 else c_x1s2)
            for h in range(2):
                b.em("scalar", lambda e, h=h, pw=(pw0, pw1)[h]: e.activation(
                    yn_g[h][:, :], pw[:, :], ACT.Identity,
                    bias=bb2[:, 0:1], scale=rv2[:, 0:1]), inc=c_yn)
                if t == 0 and g == 0 and h == 0:
                    dsnap("vector", lambda: yn_g[0][:, 0:400], "dbg1", 3624, 400, wait_ch=c_yn)
                    dsnap("vector", lambda: rv2[:, 0:1], "dbg1", 4094, 1)
                    dsnap("vector", lambda: bb2[:, 0:1], "dbg1", 4095, 1)
                b.wait("vector", c_yn)
                b.em("vector", lambda e, g=g, h=h, xb=xb, x1b=x1b: e.tensor_tensor(
                    x1b[:, h * 512:(h + 1) * 512], xb[:, g * D + h * 512:g * D + (h + 1) * 512],
                    yn_g[h][:, :], A.add), inc=(c_x1 if h == 1 else None))
            if g == G - 1:
                b.em("vector", lambda e: e.tensor_copy(fence_v[0:1, 0:1], fence_v[0:1, 1:2]), inc=c_xfree)
            b.wait("sync", c_x1)
            b.em("sync", lambda e, t=t, g=g, x1b=x1b: e.dma_start(
                out=x1_hbm[t * TTOK + g * 128:t * TTOK + (g + 1) * 128, :], in_=x1b[:, :]),
                inc=(c_x1s if g % 2 == 0 else c_x1s2), amt=16)
            ln_stats(c_ln3, [[(x1b, 0), (x1b, 512)]], rv3_all, b3_all, col0=t * G + g)
            b.wait("scalar", c_ln3)
            if t * G + g >= 1:
                b.wait("scalar", c_sx)
            b.em("scalar", lambda e, t=t, g=g, x1b=x1b: e.activation(
                xn2_g[:, :], x1b[:, :], ACT.Identity,
                bias=b3_all[:, t * G + g:t * G + g + 1],
                scale=rv3_all[:, t * G + g:t * G + g + 1]), inc=c_xn2)
            b.wait("vector", c_xn2)
            b.em("vector", lambda e, t=t, g=g: e.tensor_reduce(
                sx_slots[:, t * G + g:t * G + g + 1], xn2_g[:, :], AX, A.max,
                apply_absolute_value=True), inc=c_sx)

    b.wait("vector", c_sx)
    b.em("vector", lambda e: e.tensor_reduce(sxacc[:, 0:1], sx_slots[:, :], AX, A.max), inc=c_sx)
    b.wait("sync", c_sx)
    barrier("cc1", [sxacc], [sx_b], [QA])
    B1_DMA = c_bdma.n
    astack.close()

    ustack = ExitStack()
    xq_tm = [sb(ustack, f"xq_tm{g}", [128, D], BF16) for g in range(G)]
    qt_g = sb(ustack, "qt_g", [128, D], F32)
    xqf = [sb(ustack, f"xqf{f}", [128, T // 2], BF16) for f in range(KC)]
    wT = [sb(ustack, f"wT{k}", [128, I3], BF16) for k in range(KC)]
    mm_stage = [sb(ustack, f"mmst{i}", [128, TTOK], F32) for i in range(8)]
    uq_hist = []

    b.wait("vector", c_bdma, B1_DMA)
    b.wait("scalar", c_ln3)
    b.wait("sync", c_x1s)
    b.wait("sync", c_x1s2)

    def xq_front(t, tloc):
        xb = x_tm[t % 2]
        if t >= 2:
            b.wait("sync", c_uxf, t - 1)
        for g in range(G):
            b.em("sync", lambda e, t=t, g=g, xb=xb: e.dma_start(
                out=xb[:, g * D:(g + 1) * D],
                in_=x1_hbm[t * TTOK + g * 128:t * TTOK + (g + 1) * 128, :]),
                inc=(c_xload if t % 2 == 0 else c_xload2), amt=16)
        b.wait("scalar", c_xload if t % 2 == 0 else c_xload2)
        b.wait("vector", c_tp)
        for g in range(G):
            xng = (xn2_g, xn2_g2)[g % 2]
            if len(uq_hist) >= 2:
                b.wait("scalar", c_uq, uq_hist[-2])
            b.em("scalar", lambda e, t=t, g=g, xb=xb, xng=xng: e.activation(
                xng[:, :], xb[:, g * D:(g + 1) * D], ACT.Identity,
                bias=b3_all[:, t * G + g:t * G + g + 1],
                scale=rv3_all[:, t * G + g:t * G + g + 1]), inc=c_xn2)
            b.wait("vector", c_xn2)
            b.em("vector", lambda e, xng=xng: e.tensor_scalar(
                qt_g[:, :], xng[:, :], sx_b[:, 1:2], C_RND, A.mult, A.add))
            b.em("vector", lambda e, g=g: e.tensor_scalar(
                xq_tm[g][:, :], qt_g[:, :], -C_RND, sx_b[:, 0:1], A.add, A.mult), inc=c_uq)
            uq_hist.append(c_uq.n)
        b.em("vector", lambda e: e.tensor_copy(fence_v[0:1, 0:1], fence_v[0:1, 1:2]), inc=c_uxf)
        do_prep(6)

    def xq_back(t, tloc, mm_wait=True):
        if mm_wait:
            b.wait("scalar", c_mm)
        b.wait("tensor", c_uq)
        for f in range(KC):
            if c_tp.n >= 2:
                b.wait("tensor", c_tpe, c_tp.n - 1)
            for g in range(G):
                b.em("tensor", lambda e, f=f, g=g: e.transpose(
                    ptp[f % 2][:, g * 128:(g + 1) * 128], xq_tm[g][:, f * 128:(f + 1) * 128],
                    ident[:, :]), inc=(c_tp if g == G - 1 else None))
            b.wait("scalar", c_tp)
            b.em("scalar", lambda e, f=f, tloc=tloc: e.activation(
                xqf[f][:, tloc * TTOK:(tloc + 1) * TTOK], ptp[f % 2][:, 0:TTOK], ACT.Identity),
                inc=c_tpe)
        b.em("scalar", lambda e: e.activation(fence_s[0:1, 0:1], fence_s[0:1, 1:2], ACT.Identity), inc=c_xq)

    mst_chs = (c_mst, c_mst2, c_mst3, c_mst4)

    def emit_ffn_mm(tglobal, tloc, dest_hbm, slots, mrange=None):
        for m in (mrange if mrange is not None else range(MC)):
            b.wait("tensor", c_mev, c_mm.n - 5)
            for kk in range(KC):
                b.em("tensor", lambda e, m=m, kk=kk, tloc=tloc: e.matmul(
                    pmm[m % 6][:, :], wT[kk][:, m * 128:(m + 1) * 128],
                    xqf[kk][:, tloc * TTOK:(tloc + 1) * TTOK],
                    start=(kk == 0), stop=(kk == KC - 1)), inc=(c_mm if kk == KC - 1 else None))
            b.wait("scalar", c_mm)
            stg = mm_stage[m % 8]
            b.wait("scalar", c_mred, c_mred.n - 7)
            b.wait("scalar", mst_chs[m % 4])
            b.em("scalar", lambda e, m=m, stg=stg: e.activation(
                stg[:, :], pmm[m % 6][:, :], ACT.Identity), inc=c_mev)
            b.wait("vector", c_mev)
            b.em("vector", lambda e, stg=stg, m=m, tglobal=tglobal: e.tensor_reduce(
                slots[:, tglobal * MC + m:tglobal * MC + m + 1], stg[:, :], AX, A.max,
                apply_absolute_value=True), inc=c_mred)
            b.wait("sync", c_mev)
            b.em("sync", lambda e, m=m, tglobal=tglobal, stg=stg, dest_hbm=dest_hbm: e.dma_start(
                out=dest_hbm[m * 128:(m + 1) * 128, tglobal * TTOK:(tglobal + 1) * TTOK],
                in_=stg[:, :]), inc=mst_chs[m % 4], amt=16)

    def load_wT(whbm, wps_val):
        b.wait("sync", c_wps, wps_val)
        b.wait("sync", c_wps2, wps_val)
        b.wait("sync", c_mm)
        for kk in range(KC):
            b.em("sync", lambda e, kk=kk, whbm=whbm: e.dma_start_transpose(
                wT[kk][:, 0:512], whbm[0:512, kk * 128:(kk + 1) * 128]), inc=c_wt, amt=16)
        for kk in range(KC):
            b.em("sync", lambda e, kk=kk, whbm=whbm: e.dma_start_transpose(
                wT[kk][:, 512:I3], whbm[512:I3, kk * 128:(kk + 1) * 128]), inc=c_wt2, amt=16)

    xq_front(0, 0)
    load_wT(wup_bf, WPS_UP)
    xq_back(0, 0, mm_wait=False)
    for half in range(2):
        for tloc in range(4):
            t = half * 4 + tloc
            b.wait("tensor", c_wt)
            b.wait("tensor", c_xq)
            emit_ffn_mm(t, tloc, up_hbm, acc_slots[0], range(0, 4))
            b.wait("tensor", c_wt2)
            if half == 0 and t + 1 <= 3:
                xq_front(t + 1, t + 1)
            emit_ffn_mm(t, tloc, up_hbm, acc_slots[0], range(4, MC))
            if half == 0 and t + 1 <= 3:
                xq_back(t + 1, t + 1, mm_wait=False)
        load_wT(wgt_bf, WPS_GT)
        for tloc in range(4):
            t = half * 4 + tloc
            b.wait("tensor", c_wt)
            emit_ffn_mm(t, tloc, gt_hbm, acc_slots[1], range(0, 4))
            b.wait("tensor", c_wt2)
            if half == 0:
                xq_front(4 + tloc, tloc)
            emit_ffn_mm(t, tloc, gt_hbm, acc_slots[1], range(4, MC))
            if half == 0:
                xq_back(4 + tloc, tloc, mm_wait=True)
        if half == 0:
            load_wT(wup_bf, WPS_UP)

    b.wait("vector", c_mred)
    b.em("vector", lambda e: e.tensor_reduce(uacc[:, 0:1], acc_slots[0][:, :], AX, A.max))
    b.em("vector", lambda e: e.tensor_reduce(gacc[:, 0:1], acc_slots[1][:, :], AX, A.max), inc=c_mred)
    MEV_FINAL = c_mev.n
    ustack.close()

    fstack = ExitStack()
    FW = 2048
    NFW = (I3 * T) // (128 * FW)
    f_in = [sb(fstack, f"fin{i}", [128, FW], F32) for i in range(2)]
    f_in2 = [sb(fstack, f"fin2{i}", [128, FW], F32) for i in range(2)]
    f_t = sb(fstack, "f_t", [128, FW], F32)
    f_qp = [sb(fstack, f"f_qp{i}", [128, FW], F32) for i in range(2)]
    f_sigp = [sb(fstack, f"f_sigp{i}", [128, FW], F32) for i in range(2)]
    f_t2 = sb(fstack, "f_t2", [128, FW], F32)
    f_u16 = sb(fstack, "f_u16", [128, FW], F32)
    f_q2 = sb(fstack, "f_q2", [128, FW], F32)
    f_out = [sb(fstack, f"fout{i}", [128, FW], F32) for i in range(2)]
    f1_at, f2_at, fsg_at = {}, {}, {}

    def s2_load(i):
        m, tq = i // (T // FW), i % (T // FW)
        fin = f_in[i % 2]
        if i >= 2:
            b.wait("sync", c_f1, f1_at[i - 2])
        b.em("sync", lambda e, m=m, tq=tq, fin=fin: e.dma_start(
            out=fin[:, :], in_=gt_hbm[m * 128:(m + 1) * 128, tq * FW:(tq + 1) * FW]),
            inc=(c_fl if i % 2 == 0 else c_fl2), amt=16)

    def s2_front(i):
        fin, fq = f_in[i % 2], f_qp[i % 2]
        b.wait("vector", c_fl if i % 2 == 0 else c_fl2)
        b.em("vector", lambda e, fin=fin: e.tensor_scalar(
            f_t[:, :], fin[:, :], sg_b[:, 1:2], C_RND, A.mult, A.add))
        b.em("vector", lambda e, fq=fq: e.tensor_scalar(
            fq[:, :], f_t[:, :], -C_RND, sg_b[:, 0:1], A.add, A.mult), inc=c_f1)
        f1_at[i] = c_f1.n

    def s2_sig(i):
        b.wait("scalar", c_f1, f1_at[i])
        if i >= 2:
            b.wait("scalar", c_f2, f2_at[i - 2])
        b.em("scalar", lambda e, i=i: e.activation(
            f_sigp[i % 2][:, :], f_qp[i % 2][:, :], ACT.Sigmoid), inc=c_fsg)
        fsg_at[i] = c_fsg.n

    def s2_tail(i):
        fo = f_out[i % 2]
        b.wait("vector", c_fsg, fsg_at[i])
        b.em("vector", lambda e, i=i: e.tensor_scalar(
            f_t2[:, :], f_sigp[i % 2][:, :], 65536.0, C_RND, A.mult, A.add))
        b.em("vector", lambda e: e.tensor_scalar(
            f_u16[:, :], f_t2[:, :], -C_RND, 65535.0, A.add, A.min))
        if i >= 2:
            b.wait("vector", c_f3 if i % 2 == 0 else c_f32)
        b.em("vector", lambda e, fo=fo, i=i: e.scalar_tensor_tensor(
            fo[:, :], f_u16[:, :], float(2.0 ** -16), f_qp[i % 2][:, :], A.mult, A.mult))
        b.em("vector", lambda e, fo=fo, i=i: e.tensor_reduce(
            acc_slots[0][:, i:i + 1], fo[:, :], AX, A.max, apply_absolute_value=True),
            inc=c_f2)
        f2_at[i] = c_f2.n
        m, tq = i // (T // FW), i % (T // FW)
        b.wait("sync", c_f2, f2_at[i])
        b.em("sync", lambda e, m=m, tq=tq, fo=fo: e.dma_start(
            out=g2_hbm[m * 128:(m + 1) * 128, tq * FW:(tq + 1) * FW], in_=fo[:, :]),
            inc=(c_f3 if i % 2 == 0 else c_f32), amt=16)

    b.wait("sync", c_mst)
    b.wait("sync", c_mst2)
    b.wait("sync", c_mst3)
    b.wait("sync", c_mst4)
    s2_load(0)
    s2_load(1)
    b.wait("sync", c_mred)
    barrier("cc2", [uacc, gacc], [su_b, sg_b], [QA, QA])
    B2_DMA = c_bdma.n
    b.wait("vector", c_bdma, B2_DMA)
    for i in range(NFW):
        if i % 6 == 0:
            do_prep(1)
        s2_front(i)
        if i + 2 < NFW:
            s2_load(i + 2)
        s2_sig(i)
        if i >= 1:
            s2_tail(i - 1)
    s2_tail(NFW - 1)

    b.wait("vector", c_f2)
    b.em("vector", lambda e: e.tensor_reduce(g2acc[:, 0:1], acc_slots[0][:, 0:NFW], AX, A.max), inc=c_f2)
    f1o_at, f2o_at = {}, {}

    def o_load(i):
        m, tq = i // (T // FW), i % (T // FW)
        fin, fin2 = f_in[i % 2], f_in2[i % 2]
        if i >= 2:
            b.wait("sync", c_f1, f1o_at[i - 2])
        b.em("sync", lambda e, m=m, tq=tq, fin=fin: e.dma_start(
            out=fin[:, :], in_=g2_hbm[m * 128:(m + 1) * 128, tq * FW:(tq + 1) * FW]),
            inc=(c_fl if i % 2 == 0 else c_fl2), amt=16)
        b.em("sync", lambda e, m=m, tq=tq, fin2=fin2: e.dma_start(
            out=fin2[:, :], in_=up_hbm[m * 128:(m + 1) * 128, tq * FW:(tq + 1) * FW]),
            inc=(c_fl if i % 2 == 0 else c_fl2), amt=16)

    def o_body(i):
        m, tq = i // (T // FW), i % (T // FW)
        fin, fin2 = f_in[i % 2], f_in2[i % 2]
        fq = f_qp[0]
        b.wait("vector", c_fl if i % 2 == 0 else c_fl2)
        b.em("vector", lambda e, fin=fin: e.tensor_scalar(
            f_t[:, :], fin[:, :], sg2_b[:, 1:2], C_RND, A.mult, A.add))
        b.em("vector", lambda e, fq=fq: e.tensor_scalar(
            fq[:, :], f_t[:, :], -C_RND, sg2_b[:, 0:1], A.add, A.mult))
        b.em("vector", lambda e, fin2=fin2: e.tensor_scalar(
            f_t2[:, :], fin2[:, :], su_b[:, 1:2], C_RND, A.mult, A.add), inc=c_f1)
        f1o_at[i] = c_f1.n
        b.em("vector", lambda e: e.tensor_scalar(
            f_q2[:, :], f_t2[:, :], -C_RND, su_b[:, 0:1], A.add, A.mult))
        fo = f_out[i % 2]
        if i >= 2:
            b.wait("vector", c_f3 if i % 2 == 0 else c_f32)
        b.em("vector", lambda e, fo=fo, fq=fq: e.tensor_tensor(
            fo[:, :], fq[:, :], f_q2[:, :], A.mult))
        b.em("vector", lambda e, fo=fo, i=i: e.tensor_reduce(
            acc_slots[1][:, i:i + 1], fo[:, :], AX, A.max, apply_absolute_value=True),
            inc=c_f2)
        f2o_at[i] = c_f2.n
        b.wait("sync", c_f2, f2o_at[i])
        b.em("sync", lambda e, m=m, tq=tq, fo=fo: e.dma_start(
            out=o_hbm[m * 128:(m + 1) * 128, tq * FW:(tq + 1) * FW], in_=fo[:, :]),
            inc=(c_f3 if i % 2 == 0 else c_f32), amt=16)

    b.wait("sync", c_f3)
    b.wait("sync", c_f32)
    o_load(0)
    o_load(1)
    b.wait("sync", c_f2)
    barrier("cc3", [g2acc], [sg2_b], [QA])
    B3_DMA = c_bdma.n
    b.wait("vector", c_bdma, B3_DMA)
    for i in range(NFW):
        o_body(i)
        if i + 2 < NFW:
            o_load(i + 2)

    b.wait("vector", c_f2)
    b.em("vector", lambda e: e.tensor_reduce(oacc[:, 0:1], acc_slots[1][:, 0:NFW], AX, A.max), inc=c_f2)
    b.wait("sync", c_f2)
    barrier("cc4", [oacc], [so_b], [QA])
    B4_DMA = c_bdma.n

    fstack.close()
    dstack = ExitStack()
    d_in = [sb(dstack, f"d_in{i}", [128, 512], F32) for i in range(6)]
    d_t = sb(dstack, "d_t", [128, 512], F32)
    oqf2 = [[sb(dstack, f"oqf{p}_{m}", [128, 512], BF16) for m in range(MC)]
            for p in range(2)]
    wdT = [sb(dstack, f"wdT{m}", [128, D], BF16) for m in range(MC)]
    xout_g = [sb(dstack, f"xoutg{i}", [128, D], F32) for i in range(2)]

    b.wait("vector", c_bdma, B4_DMA)
    b.wait("sync", c_f3)
    b.wait("sync", c_f32)
    b.wait("sync", c_x1s)
    b.wait("sync", c_x1s2)
    b.wait("sync", c_wps, WPS_DN)
    b.wait("sync", c_wps2, WPS_DN)
    for m in range(MC):
        b.em("sync", lambda e, m=m: e.dma_start_transpose(
            wdT[m][:, :], wdn_bf[:, m * 128:(m + 1) * 128]), inc=c_wt, amt=16)
    WT_DN = c_wt.n
    b.wait("tensor", c_mev, MEV_FINAL)

    dq_at, dl_at, dmm_end = {}, {}, {}

    def build_oqf_part(t, mlo, mhi):
        buf = oqf2[t % 2]
        oql_chs = (c_oql, c_oql2, c_oql3, c_oql4, c_oql5, c_oql6, c_oql7, c_oql8)
        if mlo == 0 and t >= 2:
            b.wait("vector", c_dmm, dmm_end[t - 2])
        for m in range(mlo, mhi):
            mi = t * MC + m
            if mi >= 6:
                b.wait("sync", c_oq, dl_at[mi - 6])
            b.em("sync", lambda e, m=m, t=t, fin=d_in[m % 6]: e.dma_start(
                out=fin[:, :], in_=o_hbm[m * 128:(m + 1) * 128, t * TTOK:(t + 1) * TTOK]),
                inc=oql_chs[m % 6], amt=16)
        for m in range(mlo, mhi):
            fin = d_in[m % 6]
            mi = t * MC + m
            b.wait("vector", oql_chs[m % 6])
            b.em("vector", lambda e, fin=fin: e.tensor_scalar(
                d_t[:, :], fin[:, :], so_b[:, 1:2], C_RND, A.mult, A.add))
            b.em("vector", lambda e, m=m, buf=buf: e.tensor_scalar(
                buf[m][:, :], d_t[:, :], -C_RND, so_b[:, 0:1], A.add, A.mult), inc=c_oq)
            dl_at[mi] = c_oq.n
        if mhi == MC:
            dq_at[t] = c_oq.n

    for p in range(6):
        build_oqf_part(0, 4 * p, 4 * p + 4)
    for t in range(NT):
        xb = x_tm[t % 2]
        nxt = t + 1
        if t >= 2:
            b.wait("sync", c_uxf, c_uxf.n - 1)
        for g in range(G):
            b.em("sync", lambda e, t=t, g=g, xb=xb: e.dma_start(
                out=xb[:, g * D:(g + 1) * D],
                in_=x1_hbm[t * TTOK + g * 128:t * TTOK + (g + 1) * 128, :]),
                inc=(c_xload if t % 2 == 0 else c_xload2), amt=16)
        if t == 0:
            b.wait("tensor", c_wt, WT_DN)
        b.wait("tensor", c_oq, dq_at[t])
        if nxt < NT:
            build_oqf_part(nxt, 0, 4)
            build_oqf_part(nxt, 4, 8)
        for g in range(G):
            s0 = 8 * t + 2 * g
            pw0, pw1 = pmm[s0 % 6], pmm[(s0 + 1) % 6]
            for h in range(2):
                pw = (pw0, pw1)[h]
                b.wait("tensor", c_yn4, c_dmm.n - 5)
                for m in range(MC):
                    b.em("tensor", lambda e, m=m, g=g, h=h, pw=pw, t=t: e.matmul(
                        pw[:, :], oqf2[t % 2][m][:, g * 128:(g + 1) * 128],
                        wdT[m][:, h * TTOK:(h + 1) * TTOK],
                        start=(m == 0), stop=(m == MC - 1)), inc=(c_dmm if m == MC - 1 else None))
            b.wait("vector", c_dmm)
            ln_stats(c_ln4, [[(pw0, 0), (pw1, 0)]], rv4, bb4)
            b.wait("scalar", c_ln4)
            b.wait("vector", c_xload if t % 2 == 0 else c_xload2)
            xob = xout_g[g % 2]
            if t * G + g >= 2:
                b.wait("vector", c_out if g % 2 == 0 else c_out2)
            for h in range(2):
                b.em("scalar", lambda e, h=h, pw=(pw0, pw1)[h]: e.activation(
                    yn_g[h][:, :], pw[:, :], ACT.Identity,
                    bias=bb4[:, 0:1], scale=rv4[:, 0:1]), inc=c_yn4)
                b.wait("vector", c_yn4)
                b.em("vector", lambda e, g=g, h=h, xb=xb, xob=xob: e.tensor_tensor(
                    xob[:, h * 512:(h + 1) * 512], xb[:, g * D + h * 512:g * D + (h + 1) * 512],
                    yn_g[h][:, :], A.add), inc=(c_xo if h == 1 else None))
            if g == G - 1:
                b.em("vector", lambda e: e.tensor_copy(fence_v[0:1, 0:1], fence_v[0:1, 1:2]), inc=c_uxf)
            b.wait("sync", c_xo)
            b.em("sync", lambda e, t=t, g=g, xob=xob: e.dma_start(
                out=out[t * TTOK + g * 128:t * TTOK + (g + 1) * 128, :], in_=xob[:, :]),
                inc=(c_out if g % 2 == 0 else c_out2), amt=16)
            if nxt < NT:
                if g == 0:
                    build_oqf_part(nxt, 8, 12)
                elif g == 1:
                    build_oqf_part(nxt, 12, 16)
                elif g == 2:
                    build_oqf_part(nxt, 16, 20)
                    build_oqf_part(nxt, 20, 24)
        dmm_end[t] = c_dmm.n

    dsnap_finish()
    b.wait("sync", c_out, c_out.n)
    b.wait("sync", c_out2, c_out2.n)
    dstack.close()

    with nc.Block() as block:
        @block.sync
        def _(e):
            for f in b.ops["sync"]:
                f(e)

        @block.scalar
        def _(e):
            for f in b.ops["scalar"]:
                f(e)

        @block.vector
        def _(e):
            for f in b.ops["vector"]:
                f(e)

        @block.tensor
        def _(e):
            for f in b.ops["tensor"]:
                f(e)

        @block.gpsimd
        def _(e):
            for f in b.ops["gpsimd"]:
                f(e)

    gstack.close()
    return nc


_NC_CACHE = {}


def kernel(L0_attn_inputs, L0_attn_W_rkv, L0_attn_b_rkv, L0_attn_W_o, L0_attn_b_o,
           L0_attn_nu_log, L0_attn_gamma_log, L0_ffn_w_gate, L0_ffn_w_up, L0_ffn_w_down):
    if "main" not in _NC_CACHE:
        _NC_CACHE["main"] = build_program(debug=False)
    nc = _NC_CACHE["main"]
    x = np.ascontiguousarray(np.asarray(L0_attn_inputs, dtype=np.float32))
    shared = {
        "w_rkv": np.ascontiguousarray(np.asarray(L0_attn_W_rkv, np.float32)),
        "b_rkv": np.ascontiguousarray(np.asarray(L0_attn_b_rkv, np.float32)),
        "w_o": np.ascontiguousarray(np.asarray(L0_attn_W_o, np.float32)),
        "b_o": np.ascontiguousarray(np.asarray(L0_attn_b_o, np.float32)),
        "nu_log": np.ascontiguousarray(np.asarray(L0_attn_nu_log, np.float32)),
        "gamma_log": np.ascontiguousarray(np.asarray(L0_attn_gamma_log, np.float32)),
        "w_gate": np.ascontiguousarray(np.asarray(L0_ffn_w_gate, np.float32)),
        "w_up": np.ascontiguousarray(np.asarray(L0_ffn_w_up, np.float32)),
        "w_down": np.ascontiguousarray(np.asarray(L0_ffn_w_down, np.float32)),
    }
    in_maps = [dict(shared, x=x[i]) for i in range(NCORES)]
    res = run_bass_kernel_spmd(nc, in_maps, core_ids=list(range(NCORES)))
    return np.stack([np.asarray(res.results[i]["out"], dtype=np.float32)
                     for i in range(NCORES)], axis=0)


# version 16 (final)
# speedup vs baseline: 1.0471x; 1.0126x over previous
import numpy as np
import concourse.bass as bass
import concourse.mybir as mybir
from concourse.bass_utils import run_bass_kernel_spmd

F32 = mybir.dt.float32
BF16 = mybir.dt.bfloat16
I32 = mybir.dt.int32
A = mybir.AluOpType
ACT = mybir.ActivationFunctionType
AX = mybir.AxisListType.X

C_RND = float(np.float32(12582912.0))
T = 4096
D = 1024
I3 = 3072
TTOK = 512
NT = T // TTOK
G = TTOK // 128
KC = D // 128
MC = I3 // 128
NCORES = 8
LN_EPS = 1e-5
QA = 32767.0


class Ch:
    def __init__(self, stack, nc, name):
        self.sem = stack.enter_context(nc.semaphore(name))
        self.n = 0
        self.rn = 0
        self.prefix = [(0, 0, -1)]

    def note(self, amt, kind, gvidx):
        self.n += amt
        c, r, g = self.prefix[-1]
        if kind == "r":
            self.rn += amt
            r = self.rn
        else:
            g = max(g, gvidx)
        self.prefix.append((self.n, r, g))

    def targets(self, val):
        import bisect
        i = bisect.bisect_left([p[0] for p in self.prefix], val)
        if i >= len(self.prefix):
            i = len(self.prefix) - 1
        c, r, g = self.prefix[i]
        return r, (g + 1 if g >= 0 else 0)


class B_:
    def __init__(self, nc, stack):
        self.nc = nc
        self.stack = stack
        self.ops = {k: [] for k in ("sync", "scalar", "vector", "tensor", "gpsimd")}
        self.gv = stack.enter_context(nc.semaphore("c_gv"))
        self.gv_n = 0

    def ch(self, name):
        return Ch(self.stack, self.nc, name)

    def em(self, eng, f, inc=None, amt=1):
        if eng == "vector":
            idx = self.gv_n
            self.gv_n += 1
            gsem = self.gv
            if inc is not None:
                inc.note(amt, "v", idx)

            def g(e, f=f, idx=idx, gsem=gsem):
                if idx > 0:
                    e.wait_ge(gsem, idx)
                f(e).then_inc(gsem, 1)
            self.ops[eng].append(g)
            return
        if inc is not None:
            inc.note(amt, "r", None)
            self.ops[eng].append(lambda e, f=f, s=inc.sem, a=amt: f(e).then_inc(s, a))
        else:
            self.ops[eng].append(f)

    def wait(self, eng, ch, val=None):
        v = ch.n if val is None else val
        if v <= 0:
            return
        rt, gt = ch.targets(v)
        if rt > 0:
            self.ops[eng].append(lambda e, s=ch.sem, v=rt: e.wait_ge(s, v))
        if gt > 0:
            self.ops[eng].append(lambda e, s=self.gv, v=gt: e.wait_ge(s, v))


def build_program(debug=False):
    from contextlib import ExitStack
    nc = bass.Bass("TRN2", target_bir_lowering=False, debug=False)
    gstack = ExitStack()
    b = B_(nc, gstack)

    def sb(st, name, shape, dt=F32):
        return st.enter_context(nc.sbuf_tensor(name, shape, dt))

    scratch_kind = {"kind": "ExternalOutput"} if debug else {}

    x_in = nc.dram_tensor("x", [T, D], F32, kind="ExternalInput").ap()
    w_rkv = nc.dram_tensor("w_rkv", [D, 3 * D], F32, kind="ExternalInput").ap()
    b_rkv = nc.dram_tensor("b_rkv", [3 * D], F32, kind="ExternalInput").ap()
    w_o = nc.dram_tensor("w_o", [D, D], F32, kind="ExternalInput").ap()
    b_o = nc.dram_tensor("b_o", [D], F32, kind="ExternalInput").ap()
    nu_log = nc.dram_tensor("nu_log", [D], F32, kind="ExternalInput").ap()
    gamma_log = nc.dram_tensor("gamma_log", [D], F32, kind="ExternalInput").ap()
    w_gate = nc.dram_tensor("w_gate", [I3, D], F32, kind="ExternalInput").ap()
    w_up = nc.dram_tensor("w_up", [I3, D], F32, kind="ExternalInput").ap()
    w_down = nc.dram_tensor("w_down", [D, I3], F32, kind="ExternalInput").ap()
    out = nc.dram_tensor("out", [T, D], F32, kind="ExternalOutput").ap()

    x1_hbm = nc.dram_tensor("x1_hbm", [T, D], F32, **scratch_kind).ap()
    up_hbm = nc.dram_tensor("up_hbm", [I3, T], F32, **scratch_kind).ap()
    gt_hbm = nc.dram_tensor("gt_hbm", [I3, T], F32, **scratch_kind).ap()
    g2_hbm = nc.dram_tensor("g2_hbm", [I3, T], F32, **scratch_kind).ap()
    o_hbm = nc.dram_tensor("o_hbm", [I3, T], F32, **scratch_kind).ap()
    wup_bf = nc.dram_tensor("wup_bf", [I3, D], BF16).ap()
    wgt_bf = nc.dram_tensor("wgt_bf", [I3, D], BF16).ap()
    wdn_bf = nc.dram_tensor("wdn_bf", [D, I3], BF16).ap()
    cc_bufs = {}
    for nm, n in [("cc1", 128), ("cc2", 256), ("cc3", 128), ("cc4", 128)]:
        cc_bufs[nm] = (nc.dram_tensor(nm + "_in", [1, n], F32).ap(),
                       nc.dram_tensor(nm + "_out", [1, n], F32).ap())
    scl_hbm = nc.dram_tensor("scl_hbm", [1, 8], F32).ap()

    ident = sb(gstack, "ident", [128, 128], BF16)
    iotat = sb(gstack, "iotat", [128, 128], F32)
    ones_bf = sb(gstack, "ones_bf", [1, 128], BF16)
    bo_bf = sb(gstack, "bo_bf", [1, D], BF16)
    brkv_t = sb(gstack, "brkv_t", [128, 3 * KC], F32)
    nul_t = sb(gstack, "nul_t", [128, KC], F32)
    gml_t = sb(gstack, "gml_t", [128, KC], F32)
    lam_t = sb(gstack, "lam_t", [128, KC], F32)
    gam_t = sb(gstack, "gam_t", [128, KC], F32)
    brkv_h = sb(gstack, "brkv_h", [128, KC], F32)

    x_tm = [sb(gstack, f"x_tm{i}", [128, G * D], F32) for i in range(2)]
    yn_g = [sb(gstack, f"yn{i}", [128, TTOK], F32) for i in range(2)]
    xn2_g = sb(gstack, "xn2_g", [128, D], F32)
    xn2_g2 = sb(gstack, "xn2_g2", [128, D], F32)

    bnst = sb(gstack, "bnst", [128, 48], F32)
    stat_m = sb(gstack, "stat_m", [128, G], F32)
    stat_t0 = sb(gstack, "stat_t0", [128, G], F32)
    stat_v = sb(gstack, "stat_v", [128, G], F32)
    stat_sq = sb(gstack, "stat_sq", [128, 16], F32)
    stat_ss = sb(gstack, "stat_ss", [128, G], F32)
    stat_std = sb(gstack, "stat_std", [128, G], F32)
    rv1 = sb(gstack, "rv1", [128, G], F32)
    bb1 = sb(gstack, "bb1", [128, G], F32)
    rv2 = sb(gstack, "rv2", [128, 1], F32)
    bb2 = sb(gstack, "bb2", [128, 1], F32)
    rv4 = sb(gstack, "rv4", [128, 1], F32)
    bb4 = sb(gstack, "bb4", [128, 1], F32)
    rv3_all = sb(gstack, "rv3_all", [128, NT * G], F32)
    b3_all = sb(gstack, "b3_all", [128, NT * G], F32)

    sxacc = sb(gstack, "sxacc", [128, 1], F32)
    sx_slots = sb(gstack, "sx_slots", [128, NT * G], F32)
    acc_slots = [sb(gstack, f"accsl{i}", [128, MC * NT], F32) for i in range(2)]
    uacc = sb(gstack, "uacc", [128, 1], F32)
    gacc = sb(gstack, "gacc", [128, 1], F32)
    g2acc = sb(gstack, "g2acc", [128, 1], F32)
    oacc = sb(gstack, "oacc", [128, 1], F32)
    redt = sb(gstack, "redt", [128, 1], F32)
    sx_b = sb(gstack, "sx_b", [128, 2], F32)
    su_b = sb(gstack, "su_b", [128, 2], F32)
    sg_b = sb(gstack, "sg_b", [128, 2], F32)
    sg2_b = sb(gstack, "sg2_b", [128, 2], F32)
    so_b = sb(gstack, "so_b", [128, 2], F32)
    ccs = sb(gstack, "ccs", [1, 256], F32)
    svals = sb(gstack, "svals", [1, 4], F32)
    gx1 = sb(gstack, "gx1", [1, 1], F32)
    gx2 = sb(gstack, "gx2", [1, 1], F32)
    fence_v = sb(gstack, "fence_v", [1, 2], F32)
    fence_s = sb(gstack, "fence_s", [1, 2], F32)
    eps_t = sb(gstack, "eps_t", [128, 1], F32)

    WPC = 1024
    WPD = 3
    wp_in = [sb(gstack, f"wp_in{i}", [128, WPC], F32) for i in range(WPD)]
    wp_bmax = sb(gstack, "wp_bmax", [128, WPC // 32], F32)
    wp_sb = sb(gstack, "wp_sb", [128, WPC // 32], F32)
    wp_isb = sb(gstack, "wp_isb", [128, WPC // 32], F32)
    wp_t1 = sb(gstack, "wp_t1", [128, WPC], F32)
    wp_i32 = sb(gstack, "wp_i32", [128, WPC], I32)
    wp_out = [sb(gstack, f"wp_out{i}", [128, WPC], BF16) for i in range(2)]

    ptp = [gstack.enter_context(nc.psum_tensor("ptp_a", [128, 1024], BF16)),
           gstack.enter_context(nc.psum_tensor("ptp_b", [128, 1024], BF16))]
    pmm = [gstack.enter_context(nc.psum_tensor(f"pmm{i}", [128, TTOK], F32)) for i in range(6)]

    names = ("c_setup c_cast c_tiny c_xload c_xfree c_ln1 c_xn c_tp c_tpe c_rkv "
             "c_kev c_rev c_vev c_xat c_wo c_ln2 c_yn c_x1 c_x1s c_ln3 c_xn2 c_sx "
             "c_cc c_bdma c_bv c_wpl c_wpg c_wpv c_wps c_wt c_xq c_mm c_mev c_mred "
             "c_mst c_fl c_f1 c_f2 c_f3 c_fsg c_uxf c_uq c_oq c_oql c_dmm c_ln4 "
             "c_yn4 c_xo c_out c_vg c_iota c_x1s2 c_mst2 c_f32 c_out2 c_xload2 c_fl2 c_oql2 c_wpl2 c_wps2 c_woh "
             "c_wpl3 c_wpl4 c_wt2 c_mst3 c_mst4 c_oql3 c_oql4 c_oql5 c_oql6 c_oql7 c_oql8").split()
    C = {nm: b.ch(nm) for nm in names}
    (c_setup, c_cast, c_tiny, c_xload, c_xfree, c_ln1, c_xn, c_tp, c_tpe, c_rkv,
     c_kev, c_rev, c_vev, c_xat, c_wo, c_ln2, c_yn, c_x1, c_x1s, c_ln3, c_xn2, c_sx,
     c_cc, c_bdma, c_bv, c_wpl, c_wpg, c_wpv, c_wps, c_wt, c_xq, c_mm, c_mev, c_mred,
     c_mst, c_fl, c_f1, c_f2, c_f3, c_fsg, c_uxf, c_uq, c_oq, c_oql, c_dmm, c_ln4,
     c_yn4, c_xo, c_out, c_vg, c_iota, c_x1s2, c_mst2, c_f32, c_out2, c_xload2, c_fl2, c_oql2, c_wpl2, c_wps2, c_woh,
     c_wpl3, c_wpl4, c_wt2, c_mst3, c_mst4, c_oql3, c_oql4, c_oql5, c_oql6, c_oql7, c_oql8) = (C[nm] for nm in names)

    c_dbgp = b.ch("c_dbgp")
    c_dbgs = b.ch("c_dbgs")
    if debug:
        dbg1 = nc.dram_tensor("dbg1", [128, 4096], F32, kind="ExternalOutput").ap()
        dbgb = nc.dram_tensor("dbgb", [128, 2048], BF16, kind="ExternalOutput").ap()
        dbgst1 = sb(gstack, "dbgst1", [128, 2048], F32)
        dbgstb = sb(gstack, "dbgstb", [128, 2048], BF16)

    def dsnap(eng, src_fn, dstname, col, w, wait_ch=None):
        if not debug:
            return
        if dstname == "dbg1" and col + w > 2048:
            return
        stg = dbgst1 if dstname == "dbg1" else dbgstb
        if wait_ch is not None:
            b.wait("vector", wait_ch)
        b.em("vector", lambda e: e.tensor_copy(stg[:, col:col + w], src_fn()), inc=c_dbgp)

    def dsnap_finish():
        if not debug:
            return
        b.wait("sync", c_dbgp)
        b.em("sync", lambda e: e.dma_start(out=dbg1[:, 0:2048], in_=dbgst1[:, :]), inc=c_dbgs, amt=16)
        b.em("sync", lambda e: e.dma_start(out=dbgb[:, :], in_=dbgstb[:, :]), inc=c_dbgs, amt=16)
        b.wait("sync", c_dbgs)

    b.em("gpsimd", lambda e: e.iota(iotat[:, :], [[1, 128]], base=0, channel_multiplier=-1,
                                    allow_small_or_imprecise_dtypes=True), inc=c_iota, amt=1)
    b.em("sync", lambda e: e.dma_start(out=brkv_t[:, :],
                                       in_=b_rkv.rearrange("(c p) -> p c", p=128), allow_slow_non_contiguous=True), inc=c_setup, amt=16)
    b.em("sync", lambda e: e.dma_start(out=nul_t[:, :],
                                       in_=nu_log.rearrange("(c p) -> p c", p=128), allow_slow_non_contiguous=True), inc=c_setup, amt=16)
    b.em("sync", lambda e: e.dma_start(out=gml_t[:, :],
                                       in_=gamma_log.rearrange("(c p) -> p c", p=128), allow_slow_non_contiguous=True), inc=c_setup, amt=16)
    SETUP_DONE = c_setup.n

    b.wait("vector", c_iota, 1)
    b.em("vector", lambda e: e.tensor_scalar(ident[:, :], iotat[:, :], 0.0, None, A.is_equal))
    b.em("vector", lambda e: e.memset(ones_bf[0:1, :], 1.0))
    b.em("vector", lambda e: e.memset(eps_t[:, :], float(LN_EPS)))
    b.em("vector", lambda e: e.memset(fence_v[0:1, :], 0.0))
    b.em("vector", lambda e: e.memset(fence_s[0:1, :], 0.0))
    if debug:
        b.em("vector", lambda e: e.memset(dbgst1[:, :], 0.0))
        b.em("vector", lambda e: e.memset(dbgstb[:, :], 0.0))
    for z in (sxacc, uacc, gacc, g2acc, oacc):
        b.em("vector", lambda e, z=z: e.memset(z[:, :], 0.0))
    b.em("vector", lambda e: e.memset(redt[:, :], 0.0), inc=c_tiny)
    b.wait("scalar", c_setup, SETUP_DONE)
    b.wait("scalar", c_tiny, 1)
    b.em("scalar", lambda e: e.activation(lam_t[:, :], nul_t[:, :], ACT.Exp), inc=c_tiny)
    b.wait("vector", c_tiny, 2)
    b.em("vector", lambda e: e.tensor_tensor(gam_t[:, :], nul_t[:, :], gml_t[:, :], A.add), inc=c_tiny)
    b.wait("scalar", c_tiny, 3)
    b.em("scalar", lambda e: e.activation(gam_t[:, :], gam_t[:, :], ACT.Exp), inc=c_tiny)
    b.em("scalar", lambda e: e.activation(lam_t[:, :], lam_t[:, :], ACT.Exp, scale=-1.0), inc=c_tiny)
    b.wait("vector", c_tiny, 5)
    b.em("vector", lambda e: e.tensor_scalar(gam_t[:, :], gam_t[:, :], -1.0, None, A.add))
    b.wait("vector", c_setup, SETUP_DONE)
    b.em("vector", lambda e: e.tensor_scalar(brkv_h[:, :], brkv_t[:, 0:KC], 0.5, None, A.mult), inc=c_tiny)
    TINY_DONE = c_tiny.n

    def ln_stats(chan, srcs, rv_dst, b_dst, col0=0):
        ng = len(srcs)
        kk = 4
        for g, halves in enumerate(srcs):
            for h, (tt_, cc_) in enumerate(halves):
                last = (g == ng - 1) and (h == 1)
                b.em("vector", lambda e, g=g, h=h, tt_=tt_, cc_=cc_: e.bn_stats(
                    bnst[:, 12 * g + 6 * h:12 * g + 6 * h + 6], tt_[:, cc_:cc_ + 512]),
                    inc=(chan if last else None))
        b.wait("vector", chan)
        b.em("vector", lambda e, ng=ng: e.tensor_reduce(
            stat_m[:, 0:ng], bnst[:, 1:12 * ng:3].rearrange("p (g k) -> p g k", k=kk), AX, A.add))
        b.em("vector", lambda e, ng=ng: e.tensor_reduce(
            stat_t0[:, 0:ng], bnst[:, 2:12 * ng:3].rearrange("p (g k) -> p g k", k=kk), AX, A.add))
        b.em("vector", lambda e, ng=ng: e.tensor_tensor(
            stat_sq[:, 0:kk * ng], bnst[:, 1:12 * ng:3], bnst[:, 1:12 * ng:3], A.mult))
        b.em("vector", lambda e, ng=ng: e.tensor_reduce(
            stat_ss[:, 0:ng], stat_sq[:, 0:kk * ng].rearrange("p (g k) -> p g k", k=kk), AX, A.add),
            inc=chan)
        b.wait("vector", chan)
        b.em("vector", lambda e, ng=ng: e.tensor_scalar(
            stat_m[:, 0:ng], stat_m[:, 0:ng], 0.25, None, A.mult))
        b.em("vector", lambda e, ng=ng: e.tensor_scalar(
            stat_v[:, 0:ng], stat_t0[:, 0:ng], 1.0 / 1024.0, None, A.mult))
        b.em("vector", lambda e, ng=ng: e.scalar_tensor_tensor(
            stat_v[:, 0:ng], stat_ss[:, 0:ng], 0.25, stat_v[:, 0:ng], A.mult, A.add))
        b.em("vector", lambda e, ng=ng: e.tensor_tensor(
            stat_sq[:, 0:ng], stat_m[:, 0:ng], stat_m[:, 0:ng], A.mult))
        b.em("vector", lambda e, ng=ng: e.tensor_tensor(
            stat_v[:, 0:ng], stat_v[:, 0:ng], stat_sq[:, 0:ng], A.subtract), inc=chan)
        b.wait("scalar", chan)
        b.em("scalar", lambda e, ng=ng: e.activation(
            stat_std[:, 0:ng], stat_v[:, 0:ng], ACT.Sqrt, bias=eps_t[:, 0:1]), inc=chan)
        b.wait("vector", chan)
        b.em("vector", lambda e, ng=ng: e.reciprocal(stat_t0[:, 0:ng], stat_std[:, 0:ng]),
             inc=chan)
        b.wait("vector", chan)
        b.em("vector", lambda e, ng=ng: e.tensor_tensor(
            stat_sq[:, 0:ng], stat_t0[:, 0:ng], stat_t0[:, 0:ng], A.mult))
        b.em("vector", lambda e, ng=ng: e.tensor_scalar(
            stat_ss[:, 0:ng], stat_v[:, 0:ng], 0.5, float(0.5 * LN_EPS), A.mult, A.add))
        b.em("vector", lambda e, ng=ng: e.tensor_tensor(
            stat_ss[:, 0:ng], stat_ss[:, 0:ng], stat_sq[:, 0:ng], A.mult))
        b.em("vector", lambda e, ng=ng: e.tensor_scalar(
            stat_ss[:, 0:ng], stat_ss[:, 0:ng], -1.0, 1.5, A.mult, A.add))
        b.em("vector", lambda e, ng=ng, rv_dst=rv_dst, col0=col0: e.tensor_tensor(
            rv_dst[:, col0:col0 + ng], stat_t0[:, 0:ng], stat_ss[:, 0:ng], A.mult))
        b.em("vector", lambda e, ng=ng, rv_dst=rv_dst, b_dst=b_dst, col0=col0: e.scalar_tensor_tensor(
            b_dst[:, col0:col0 + ng], stat_m[:, 0:ng], -1.0,
            rv_dst[:, col0:col0 + ng], A.mult, A.mult), inc=chan)

    def barrier(cc_name, accs, out_tiles, qmaxs):
        n = len(accs)
        cin, cout = cc_bufs[cc_name]
        for i, acc in enumerate(accs):
            b.em("sync", lambda e, i=i, acc=acc, cin=cin: e.dma_start(
                out=cin[0:1, 128 * i:128 * (i + 1)], in_=acc[:, 0:1]), inc=c_bdma, amt=16)
        b.wait("gpsimd", c_bdma)
        b.em("gpsimd", lambda e, cin=cin, cout=cout: e.collective_compute(
            "AllReduce", A.max, replica_groups=[list(range(NCORES))],
            ins=[cin[:, :]], outs=[cout[:, :]]), inc=c_cc)
        b.wait("sync", c_cc)
        b.em("sync", lambda e, cout=cout, n=n: e.dma_start(
            out=ccs[0:1, 0:128 * n], in_=cout[:, :]), inc=c_bdma, amt=16)
        b.wait("vector", c_bdma)
        for i in range(n):
            q = qmaxs[i]
            b.em("vector", lambda e, i=i: e.tensor_reduce(
                gx1[0:1, 0:1], ccs[0:1, 128 * i:128 * (i + 1)], AX, A.max), inc=c_bv)
            b.wait("vector", c_bv)
            b.em("vector", lambda e, q=q: e.tensor_scalar(
                gx2[0:1, 0:1], gx1[0:1, 0:1], 1.0 / q, 1e-12, A.mult, A.max))
            b.em("vector", lambda e, i=i: e.tensor_copy(svals[0:1, 2 * i:2 * i + 1], gx2[0:1, 0:1]))
            b.em("vector", lambda e, i=i: e.reciprocal(svals[0:1, 2 * i + 1:2 * i + 2], gx2[0:1, 0:1]))
        b.em("vector", lambda e: e.tensor_copy(fence_v[0:1, 0:1], fence_v[0:1, 1:2]), inc=c_bv)
        b.wait("sync", c_bv)
        b.em("sync", lambda e, n=n: e.dma_start(out=scl_hbm[0:1, 0:2 * n],
                                                in_=svals[0:1, 0:2 * n]), inc=c_bdma, amt=16)
        b.wait("sync", c_bdma)
        for i in range(n):
            b.em("sync", lambda e, i=i, ot=out_tiles[i]: e.dma_start(
                out=ot[:, :], in_=scl_hbm[0:1, 2 * i:2 * i + 2].broadcast_to((128, 2))),
                inc=c_bdma, amt=16)

    NB = WPC // 32
    wp_vend = {}

    def make_prep(w_ap, whbm_bf, r, col0, idx):
        wch = (c_wpl, c_wpl2, c_wpl3, c_wpl4)[idx % WPD]

        def load():
            if idx >= WPD:
                b.wait("sync", c_wpv, wp_vend[idx - WPD])
            b.em("sync", lambda e: e.dma_start(
                out=wp_in[idx % WPD][:, :], in_=w_ap[r * 128:(r + 1) * 128, col0:col0 + WPC]),
                inc=wch, amt=16)

        def proc():
            win = wp_in[idx % WPD]
            wout = wp_out[idx % 2]
            b.wait("vector", wch)
            b.em("vector", lambda e, win=win: e.tensor_reduce(
                wp_bmax[:, 0:NB], win[:, :].rearrange("p (n k) -> p n k", k=32),
                AX, A.max, apply_absolute_value=True))
            b.em("vector", lambda e: e.tensor_scalar(
                wp_sb[:, 0:NB], wp_bmax[:, 0:NB], 1.0 / 7.0, 1e-12, A.mult, A.max))
            b.em("vector", lambda e: e.reciprocal(wp_isb[:, 0:NB], wp_sb[:, 0:NB]))
            b.em("vector", lambda e, win=win: e.tensor_tensor(
                wp_t1[:, :].rearrange("p (n k) -> p n k", k=32),
                win[:, :].rearrange("p (n k) -> p n k", k=32),
                wp_isb[:, 0:NB].rearrange("p (n o) -> p n o", o=1).broadcast_to((128, NB, 32)),
                A.mult))
            b.em("vector", lambda e: e.tensor_scalar(
                wp_t1[:, :], wp_t1[:, :], C_RND, -C_RND, A.add, A.add))
            b.em("vector", lambda e: e.tensor_copy(wp_i32[:, :], wp_t1[:, :]))
            b.em("vector", lambda e: e.tensor_copy(wp_t1[:, :], wp_i32[:, :]))
            if idx >= 2:
                b.wait("vector", c_wps if idx % 2 == 0 else c_wps2)
            b.em("vector", lambda e, wout=wout: e.tensor_tensor(
                wout[:, :].rearrange("p (n k) -> p n k", k=32),
                wp_t1[:, :].rearrange("p (n k) -> p n k", k=32),
                wp_sb[:, 0:NB].rearrange("p (n o) -> p n o", o=1).broadcast_to((128, NB, 32)),
                A.mult), inc=c_wpv)
            wp_vend[idx] = c_wpv.n
            b.wait("sync", c_wpv, wp_vend[idx])
            b.em("sync", lambda e, wout=wout: e.dma_start(
                out=whbm_bf[r * 128:(r + 1) * 128, col0:col0 + WPC], in_=wout[:, :]),
                inc=(c_wps if idx % 2 == 0 else c_wps2), amt=16)
        return load, proc

    prep_steps = []
    _idx = 0
    for w_ap, whbm in ((w_up, wup_bf), (w_gate, wgt_bf)):
        for r in range(MC):
            for cchunk in range(D // WPC):
                prep_steps.append(make_prep(w_ap, whbm, r, cchunk * WPC, _idx))
                _idx += 1
    WPS_UP = 16 * (MC * (D // WPC)) // 2
    WPS_GT = 2 * WPS_UP
    for r in range(KC):
        for cchunk in range(I3 // WPC):
            prep_steps.append(make_prep(w_down, wdn_bf, r, cchunk * WPC, _idx))
            _idx += 1
    WPS_DN = 16 * _idx // 2
    N_PREP = _idx
    prep_pos = [0]

    def do_prep(n):
        for _ in range(n):
            s = prep_pos[0]
            if s > N_PREP:
                return
            if s == 0:
                prep_steps[0][0]()
            if s + 1 < N_PREP:
                prep_steps[s + 1][0]()
            if s >= 1:
                prep_steps[s - 1][1]()
            prep_pos[0] = s + 1

    astack = ExitStack()
    wrkv_sb = [sb(astack, f"wrkv{k}", [128, 3 * D], BF16) for k in range(KC)]
    wo_sb = [sb(astack, f"wo{k}", [128, D], BF16) for k in range(KC)]
    xn_tm = [sb(astack, f"xn_tm{g}", [128, D], BF16) for g in range(G)]
    xn_fm = [sb(astack, f"xn_fm{f}", [128, TTOK], BF16) for f in range(KC)]
    k_c = [sb(astack, f"k_c{i}", [128, TTOK], F32) for i in range(2)]
    r_c = [sb(astack, f"r_c{i}", [128, TTOK], F32) for i in range(2)]
    kv_c = [sb(astack, f"kv_c{i}", [128, TTOK], F32) for i in range(2)]
    y1_c = sb(astack, "y1_c", [128, TTOK], F32)
    y2_c = sb(astack, "y2_c", [128, TTOK], F32)
    num_c = sb(astack, "num_c", [128, TTOK], F32)
    den_c = sb(astack, "den_c", [128, TTOK], F32)
    rec_c = sb(astack, "rec_c", [128, TTOK], F32)
    xatt_fm = [sb(astack, f"xatt{f}", [128, TTOK], BF16) for f in range(KC)]
    st1 = sb(astack, "st1", [128, KC], F32)
    st2 = sb(astack, "st2", [128, KC], F32)
    x1_g = [sb(astack, f"x1g{i}", [128, D], F32) for i in range(2)]

    for k in range(KC):
        b.em("gpsimd", lambda e, k=k: e.dma_start(out=wrkv_sb[k][:, :],
                                                  in_=w_rkv[k * 128:(k + 1) * 128, :]), inc=c_cast, amt=16)
        b.em("gpsimd", lambda e, k=k: e.dma_start(out=wo_sb[k][:, :],
                                                  in_=w_o[k * 128:(k + 1) * 128, :]), inc=c_cast, amt=16)
    b.em("gpsimd", lambda e: e.dma_start(out=bo_bf[0:1, :], in_=b_o[None, :]), inc=c_cast, amt=16)
    CAST_DONE = c_cast.n

    b.wait("vector", c_cast, CAST_DONE)
    for k in range(KC):
        b.em("vector", lambda e, k=k: e.tensor_scalar(
            wo_sb[k][:, :], wo_sb[k][:, :], 0.5, None, A.mult),
            inc=(c_woh if k == KC - 1 else None))

    b.wait("vector", c_tiny, 1)
    b.em("vector", lambda e: e.memset(st1[:, :], 0.0))
    b.em("vector", lambda e: e.memset(st2[:, :], 0.0), inc=c_tiny)
    TINY_DONE = c_tiny.n

    b.wait("tensor", c_cast, CAST_DONE)
    b.wait("tensor", c_woh)
    b.wait("tensor", c_tiny, 1)
    b.wait("vector", c_tiny, TINY_DONE)
    b.wait("scalar", c_tiny, TINY_DONE)
    b.wait("scalar", c_setup, SETUP_DONE)

    for t in range(NT):
        xb = x_tm[t % 2]
        if t >= 2:
            b.wait("sync", c_xfree, t - 1)
        for g in range(G):
            b.em("sync", lambda e, t=t, g=g, xb=xb: e.dma_start(
                out=xb[:, g * D:(g + 1) * D],
                in_=x_in[t * TTOK + g * 128: t * TTOK + (g + 1) * 128, :]),
                inc=(c_xload if t % 2 == 0 else c_xload2), amt=16)
        b.wait("vector", c_xload if t % 2 == 0 else c_xload2)
        ln_stats(c_ln1, [[(xb, g * D), (xb, g * D + 512)] for g in range(G)], rv1, bb1)
        do_prep(3)
        if t == 0:
            dsnap("vector", lambda: stat_v[:, 0:4], "dbg1", 0, 4)
            dsnap("vector", lambda: rv1[:, 0:4], "dbg1", 4, 4)
            dsnap("vector", lambda: bb1[:, 0:4], "dbg1", 8, 4)
        b.wait("scalar", c_ln1)
        if t >= 1:
            b.wait("scalar", c_tp)
        for g in range(G):
            b.em("scalar", lambda e, g=g, xb=xb: e.activation(
                xn_tm[g][:, :], xb[:, g * D:(g + 1) * D], ACT.Identity,
                bias=bb1[:, g:g + 1], scale=rv1[:, g:g + 1]), inc=c_xn)
        if t == 0:
            dsnap("vector", lambda: xn_tm[0][:, :], "dbgb", 0, 1024, wait_ch=c_xn)
        b.wait("tensor", c_xn)
        for f in range(KC):
            if c_tp.n >= 2:
                b.wait("tensor", c_tpe, c_tp.n - 1)
            for g in range(G):
                b.em("tensor", lambda e, f=f, g=g: e.transpose(
                    ptp[f % 2][:, g * 128:(g + 1) * 128], xn_tm[g][:, f * 128:(f + 1) * 128],
                    ident[:, :]), inc=(c_tp if g == G - 1 else None))
            b.wait("scalar", c_tp)
            if t >= 1 and f == 0:
                b.wait("scalar", c_rkv)
            b.em("scalar", lambda e, f=f: e.activation(
                xn_fm[f][:, :], ptp[f % 2][:, 0:TTOK], ACT.Identity), inc=c_tpe)
        if t == 0:
            dsnap("vector", lambda: xn_fm[0][:, :], "dbgb", 1024, 512, wait_ch=c_tpe)
        b.wait("tensor", c_tpe)
        if t >= 1:
            b.wait("vector", c_wo)
            b.wait("tensor", c_yn)
        for c in range(KC):
            kb, rb, vb = k_c[c % 2], r_c[c % 2], kv_c[c % 2]
            pk, pv, pr = pmm[c % 2], pmm[2 + c % 2], pmm[4 + c % 2]
            b.wait("tensor", c_kev, c_kev.n - 1)
            b.wait("tensor", c_vev, c_vev.n - 1)
            b.wait("tensor", c_rev, c_rev.n - 1)
            for kk in range(KC):
                b.em("tensor", lambda e, c=c, kk=kk, pk=pk: e.matmul(
                    pk[:, :], wrkv_sb[kk][:, (8 + c) * 128:(9 + c) * 128], xn_fm[kk][:, :],
                    start=(kk == 0), stop=(kk == KC - 1)), inc=(c_rkv if kk == KC - 1 else None))
            for kk in range(KC):
                b.em("tensor", lambda e, c=c, kk=kk, pv=pv: e.matmul(
                    pv[:, :], wrkv_sb[kk][:, (16 + c) * 128:(17 + c) * 128], xn_fm[kk][:, :],
                    start=(kk == 0), stop=(kk == KC - 1)), inc=(c_rkv if kk == KC - 1 else None))
            for kk in range(KC):
                b.em("tensor", lambda e, c=c, kk=kk, pr=pr: e.matmul(
                    pr[:, :], wrkv_sb[kk][:, c * 128:(c + 1) * 128], xn_fm[kk][:, :],
                    start=(kk == 0), stop=(kk == KC - 1)), inc=(c_rkv if kk == KC - 1 else None))
            b.wait("scalar", c_rkv, c_rkv.n - 2)
            if t * KC + c >= 2:
                b.wait("scalar", c_xat, c_xat.n - 1)
            b.em("scalar", lambda e, c=c, kb=kb, pk=pk: e.activation(
                kb[:, :], pk[:, :], ACT.Exp, bias=brkv_t[:, 8 + c:9 + c]), inc=c_kev)
            b.wait("scalar", c_rkv, c_rkv.n)
            b.em("scalar", lambda e, c=c, rb=rb, pr=pr: e.activation(
                rb[:, :], pr[:, :], ACT.Tanh, scale=0.5, bias=brkv_h[:, c:c + 1]), inc=c_rev)
            if t == 0 and c == 0:
                dsnap("vector", lambda: k_c[0][:, :], "dbg1", 512, 512, wait_ch=c_kev)
                dsnap("vector", lambda: r_c[0][:, :], "dbg1", 1024, 512, wait_ch=c_rev)
            b.wait("vector", c_rkv, c_rkv.n - 1)
            b.wait("vector", c_kev)
            b.em("vector", lambda e, c=c, kb=kb, vb=vb, pv=pv: e.scalar_tensor_tensor(
                vb[:, :], pv[:, :], brkv_t[:, 16 + c:17 + c], kb[:, :], A.add, A.mult), inc=c_vev)
            if t == 0 and c == 0:
                dsnap("vector", lambda: kv_c[0][:, :], "dbg1", 1536, 512)
            b.em("vector", lambda e, c=c, vb=vb: e.tensor_tensor_scan(
                y1_c[:, :], lam_t[:, c:c + 1].broadcast_to((128, TTOK)), vb[:, :],
                st1[:, c:c + 1], A.mult, A.add))
            b.em("vector", lambda e, c=c, kb=kb: e.tensor_tensor_scan(
                y2_c[:, :], lam_t[:, c:c + 1].broadcast_to((128, TTOK)), kb[:, :],
                st2[:, c:c + 1], A.mult, A.add), inc=c_vg)
            b.wait("vector", c_vg)
            b.em("vector", lambda e, c=c: e.tensor_copy(st1[:, c:c + 1], y1_c[:, TTOK - 1:TTOK]))
            b.em("vector", lambda e, c=c: e.tensor_copy(st2[:, c:c + 1], y2_c[:, TTOK - 1:TTOK]))
            if t == 0 and c == 0:
                dsnap("vector", lambda: y1_c[:, :], "dbg1", 2048, 512)
                dsnap("vector", lambda: y2_c[:, :], "dbg1", 2560, 512)
                dsnap("vector", lambda: lam_t[:, 0:8], "dbg1", 3584, 8)
                dsnap("vector", lambda: gam_t[:, 0:8], "dbg1", 3592, 8)
                dsnap("vector", lambda: brkv_t[:, 0:24], "dbg1", 3600, 24, wait_ch=c_setup)
            b.em("vector", lambda e, c=c, vb=vb: e.scalar_tensor_tensor(
                num_c[:, :], vb[:, :], gam_t[:, c:c + 1], y1_c[:, :], A.mult, A.add))
            b.em("vector", lambda e, c=c, kb=kb: e.scalar_tensor_tensor(
                den_c[:, :], kb[:, :], gam_t[:, c:c + 1], y2_c[:, :], A.mult, A.add))
            b.em("vector", lambda e: e.tensor_scalar(den_c[:, :], den_c[:, :], 1e-6, None, A.add))
            b.em("vector", lambda e: e.reciprocal(rec_c[:, :], den_c[:, :]))
            if t == 0 and c == 0:
                dsnap("vector", lambda: den_c[:, :], "dbg1", 3072, 512)
            b.em("vector", lambda e: e.tensor_tensor(num_c[:, :], num_c[:, :], rec_c[:, :], A.mult))
            b.wait("vector", c_rev)
            b.em("vector", lambda e, c=c, rb=rb: e.scalar_tensor_tensor(
                xatt_fm[c][:, :], rb[:, :], 1.0, num_c[:, :], A.add, A.mult), inc=c_xat)
            if t == 0 and c == 0:
                dsnap("vector", lambda: xatt_fm[0][:, :], "dbgb", 1536, 512)
            if 2 <= c <= 4:
                do_prep(1)
        b.wait("tensor", c_xat)
        for g in range(G):
            pw0, pw1 = pmm[(2 * g) % 4], pmm[(2 * g + 1) % 4]
            for h in range(2):
                pw = (pw0, pw1)[h]
                b.wait("tensor", c_yn, c_wo.n - 3)
                for c in range(KC):
                    b.em("tensor", lambda e, c=c, g=g, h=h, pw=pw: e.matmul(
                        pw[:, :], xatt_fm[c][:, g * 128:(g + 1) * 128],
                        wo_sb[c][:, h * TTOK:(h + 1) * TTOK], start=(c == 0), stop=False))
                b.em("tensor", lambda e, g=g, h=h, pw=pw: e.matmul(
                    pw[:, :], ones_bf[0:1, :], bo_bf[0:1, h * TTOK:(h + 1) * TTOK],
                    start=False, stop=True), inc=c_wo)
            b.wait("vector", c_wo)
            ln_stats(c_ln2, [[(pw0, 0), (pw1, 0)]], rv2, bb2)
            b.wait("scalar", c_ln2)
            x1b = x1_g[g % 2]
            if t * G + g >= 2:
                b.wait("vector", c_xn2, c_xn2.n - 1)
                b.wait("vector", c_x1s if g % 2 == 0 else c_x1s2)
            for h in range(2):
                b.em("scalar", lambda e, h=h, pw=(pw0, pw1)[h]: e.activation(
                    yn_g[h][:, :], pw[:, :], ACT.Identity,
                    bias=bb2[:, 0:1], scale=rv2[:, 0:1]), inc=c_yn)
                if t == 0 and g == 0 and h == 0:
                    dsnap("vector", lambda: yn_g[0][:, 0:400], "dbg1", 3624, 400, wait_ch=c_yn)
                    dsnap("vector", lambda: rv2[:, 0:1], "dbg1", 4094, 1)
                    dsnap("vector", lambda: bb2[:, 0:1], "dbg1", 4095, 1)
                b.wait("vector", c_yn)
                b.em("vector", lambda e, g=g, h=h, xb=xb, x1b=x1b: e.tensor_tensor(
                    x1b[:, h * 512:(h + 1) * 512], xb[:, g * D + h * 512:g * D + (h + 1) * 512],
                    yn_g[h][:, :], A.add), inc=(c_x1 if h == 1 else None))
            if g == G - 1:
                b.em("vector", lambda e: e.tensor_copy(fence_v[0:1, 0:1], fence_v[0:1, 1:2]), inc=c_xfree)
            b.wait("sync", c_x1)
            b.em("sync", lambda e, t=t, g=g, x1b=x1b: e.dma_start(
                out=x1_hbm[t * TTOK + g * 128:t * TTOK + (g + 1) * 128, :], in_=x1b[:, :]),
                inc=(c_x1s if g % 2 == 0 else c_x1s2), amt=16)
            ln_stats(c_ln3, [[(x1b, 0), (x1b, 512)]], rv3_all, b3_all, col0=t * G + g)
            b.wait("scalar", c_ln3)
            if t * G + g >= 1:
                b.wait("scalar", c_sx)
            b.em("scalar", lambda e, t=t, g=g, x1b=x1b: e.activation(
                xn2_g[:, :], x1b[:, :], ACT.Identity,
                bias=b3_all[:, t * G + g:t * G + g + 1],
                scale=rv3_all[:, t * G + g:t * G + g + 1]), inc=c_xn2)
            b.wait("vector", c_xn2)
            b.em("vector", lambda e, t=t, g=g: e.tensor_reduce(
                sx_slots[:, t * G + g:t * G + g + 1], xn2_g[:, :], AX, A.max,
                apply_absolute_value=True), inc=c_sx)

    b.wait("vector", c_sx)
    b.em("vector", lambda e: e.tensor_reduce(sxacc[:, 0:1], sx_slots[:, :], AX, A.max), inc=c_sx)
    b.wait("sync", c_sx)
    barrier("cc1", [sxacc], [sx_b], [QA])
    B1_DMA = c_bdma.n
    astack.close()

    ustack = ExitStack()
    xq_tm = [sb(ustack, f"xq_tm{g}", [128, D], BF16) for g in range(G)]
    qt_g = sb(ustack, "qt_g", [128, D], F32)
    xqf = [sb(ustack, f"xqf{f}", [128, T // 2], BF16) for f in range(KC)]
    wT = [sb(ustack, f"wT{k}", [128, I3], BF16) for k in range(KC)]
    mm_stage = [sb(ustack, f"mmst{i}", [128, TTOK], F32) for i in range(8)]
    uq_hist = []

    b.wait("vector", c_bdma, B1_DMA)
    b.wait("scalar", c_ln3)
    b.wait("sync", c_x1s)
    b.wait("sync", c_x1s2)

    def xq_front(t, tloc):
        xb = x_tm[t % 2]
        if t >= 2:
            b.wait("sync", c_uxf, t - 1)
        for g in range(G):
            b.em("sync", lambda e, t=t, g=g, xb=xb: e.dma_start(
                out=xb[:, g * D:(g + 1) * D],
                in_=x1_hbm[t * TTOK + g * 128:t * TTOK + (g + 1) * 128, :]),
                inc=(c_xload if t % 2 == 0 else c_xload2), amt=16)
        b.wait("scalar", c_xload if t % 2 == 0 else c_xload2)
        b.wait("vector", c_tp)
        for g in range(G):
            xng = (xn2_g, xn2_g2)[g % 2]
            if len(uq_hist) >= 2:
                b.wait("scalar", c_uq, uq_hist[-2])
            b.em("scalar", lambda e, t=t, g=g, xb=xb, xng=xng: e.activation(
                xng[:, :], xb[:, g * D:(g + 1) * D], ACT.Identity,
                bias=b3_all[:, t * G + g:t * G + g + 1],
                scale=rv3_all[:, t * G + g:t * G + g + 1]), inc=c_xn2)
            b.wait("vector", c_xn2)
            b.em("vector", lambda e, xng=xng: e.tensor_scalar(
                qt_g[:, :], xng[:, :], sx_b[:, 1:2], C_RND, A.mult, A.add))
            b.em("vector", lambda e, g=g: e.tensor_scalar(
                xq_tm[g][:, :], qt_g[:, :], -C_RND, sx_b[:, 0:1], A.add, A.mult), inc=c_uq)
            uq_hist.append(c_uq.n)
        b.em("vector", lambda e: e.tensor_copy(fence_v[0:1, 0:1], fence_v[0:1, 1:2]), inc=c_uxf)
        do_prep(6)

    def xq_back(t, tloc, mm_wait=True):
        if mm_wait:
            b.wait("scalar", c_mm)
        b.wait("tensor", c_uq)
        for f in range(KC):
            if c_tp.n >= 2:
                b.wait("tensor", c_tpe, c_tp.n - 1)
            for g in range(G):
                b.em("tensor", lambda e, f=f, g=g: e.transpose(
                    ptp[f % 2][:, g * 128:(g + 1) * 128], xq_tm[g][:, f * 128:(f + 1) * 128],
                    ident[:, :]), inc=(c_tp if g == G - 1 else None))
            b.wait("scalar", c_tp)
            b.em("scalar", lambda e, f=f, tloc=tloc: e.activation(
                xqf[f][:, tloc * TTOK:(tloc + 1) * TTOK], ptp[f % 2][:, 0:TTOK], ACT.Identity),
                inc=c_tpe)
        b.em("scalar", lambda e: e.activation(fence_s[0:1, 0:1], fence_s[0:1, 1:2], ACT.Identity), inc=c_xq)

    mst_chs = (c_mst, c_mst2, c_mst3, c_mst4)

    def emit_ffn_mm(tglobal, tloc, dest_hbm, slots, mrange=None):
        for m in (mrange if mrange is not None else range(MC)):
            b.wait("tensor", c_mev, c_mm.n - 5)
            for kk in range(KC):
                b.em("tensor", lambda e, m=m, kk=kk, tloc=tloc: e.matmul(
                    pmm[m % 6][:, :], wT[kk][:, m * 128:(m + 1) * 128],
                    xqf[kk][:, tloc * TTOK:(tloc + 1) * TTOK],
                    start=(kk == 0), stop=(kk == KC - 1)), inc=(c_mm if kk == KC - 1 else None))
            b.wait("scalar", c_mm)
            stg = mm_stage[m % 8]
            b.wait("scalar", c_mred, c_mred.n - 7)
            b.wait("scalar", mst_chs[m % 4])
            b.em("scalar", lambda e, m=m, stg=stg: e.activation(
                stg[:, :], pmm[m % 6][:, :], ACT.Identity), inc=c_mev)
            b.wait("vector", c_mev)
            b.em("vector", lambda e, stg=stg, m=m, tglobal=tglobal: e.tensor_reduce(
                slots[:, tglobal * MC + m:tglobal * MC + m + 1], stg[:, :], AX, A.max,
                apply_absolute_value=True), inc=c_mred)
            b.wait("sync", c_mev)
            b.em("sync", lambda e, m=m, tglobal=tglobal, stg=stg, dest_hbm=dest_hbm: e.dma_start(
                out=dest_hbm[m * 128:(m + 1) * 128, tglobal * TTOK:(tglobal + 1) * TTOK],
                in_=stg[:, :]), inc=mst_chs[m % 4], amt=16)

    def load_wT(whbm, wps_val):
        b.wait("sync", c_wps, wps_val)
        b.wait("sync", c_wps2, wps_val)
        b.wait("sync", c_mm)
        for kk in range(KC):
            b.em("sync", lambda e, kk=kk, whbm=whbm: e.dma_start_transpose(
                wT[kk][:, 0:512], whbm[0:512, kk * 128:(kk + 1) * 128]), inc=c_wt, amt=16)
        for kk in range(KC):
            b.em("sync", lambda e, kk=kk, whbm=whbm: e.dma_start_transpose(
                wT[kk][:, 512:I3], whbm[512:I3, kk * 128:(kk + 1) * 128]), inc=c_wt2, amt=16)

    xq_front(0, 0)
    load_wT(wup_bf, WPS_UP)
    xq_back(0, 0, mm_wait=False)
    for half in range(2):
        for tloc in range(4):
            t = half * 4 + tloc
            b.wait("tensor", c_wt)
            b.wait("tensor", c_xq)
            emit_ffn_mm(t, tloc, up_hbm, acc_slots[0], range(0, 4))
            b.wait("tensor", c_wt2)
            if half == 0 and t + 1 <= 3:
                xq_front(t + 1, t + 1)
            emit_ffn_mm(t, tloc, up_hbm, acc_slots[0], range(4, MC))
            if half == 0 and t + 1 <= 3:
                xq_back(t + 1, t + 1, mm_wait=False)
        load_wT(wgt_bf, WPS_GT)
        for tloc in range(4):
            t = half * 4 + tloc
            b.wait("tensor", c_wt)
            emit_ffn_mm(t, tloc, gt_hbm, acc_slots[1], range(0, 4))
            b.wait("tensor", c_wt2)
            if half == 0:
                xq_front(4 + tloc, tloc)
            emit_ffn_mm(t, tloc, gt_hbm, acc_slots[1], range(4, MC))
            if half == 0:
                xq_back(4 + tloc, tloc, mm_wait=True)
        if half == 0:
            load_wT(wup_bf, WPS_UP)

    b.wait("vector", c_mred)
    b.em("vector", lambda e: e.tensor_reduce(uacc[:, 0:1], acc_slots[0][:, :], AX, A.max))
    b.em("vector", lambda e: e.tensor_reduce(gacc[:, 0:1], acc_slots[1][:, :], AX, A.max), inc=c_mred)
    MEV_FINAL = c_mev.n
    ustack.close()

    fstack = ExitStack()
    FW = 2048
    NFW = (I3 * T) // (128 * FW)
    f_in = [sb(fstack, f"fin{i}", [128, FW], F32) for i in range(2)]
    f_in2 = [sb(fstack, f"fin2{i}", [128, FW], F32) for i in range(2)]
    f_t = sb(fstack, "f_t", [128, FW], F32)
    f_qp = [sb(fstack, f"f_qp{i}", [128, FW], F32) for i in range(2)]
    f_sigp = [sb(fstack, f"f_sigp{i}", [128, FW], F32) for i in range(2)]
    f_t2 = sb(fstack, "f_t2", [128, FW], F32)
    f_u16 = sb(fstack, "f_u16", [128, FW], F32)
    f_q2 = sb(fstack, "f_q2", [128, FW], F32)
    f_out = [sb(fstack, f"fout{i}", [128, FW], F32) for i in range(2)]
    f1_at, f2_at, fsg_at = {}, {}, {}

    def s2_load(i):
        m, tq = i // (T // FW), i % (T // FW)
        fin = f_in[i % 2]
        if i >= 2:
            b.wait("sync", c_f1, f1_at[i - 2])
        b.em("sync", lambda e, m=m, tq=tq, fin=fin: e.dma_start(
            out=fin[:, :], in_=gt_hbm[m * 128:(m + 1) * 128, tq * FW:(tq + 1) * FW]),
            inc=(c_fl if i % 2 == 0 else c_fl2), amt=16)

    def s2_front(i):
        fin, fq = f_in[i % 2], f_qp[i % 2]
        b.wait("vector", c_fl if i % 2 == 0 else c_fl2)
        b.em("vector", lambda e, fin=fin: e.tensor_scalar(
            f_t[:, :], fin[:, :], sg_b[:, 1:2], C_RND, A.mult, A.add))
        b.em("vector", lambda e, fq=fq: e.tensor_scalar(
            fq[:, :], f_t[:, :], -C_RND, sg_b[:, 0:1], A.add, A.mult), inc=c_f1)
        f1_at[i] = c_f1.n

    def s2_sig(i):
        b.wait("scalar", c_f1, f1_at[i])
        if i >= 2:
            b.wait("scalar", c_f2, f2_at[i - 2])
        b.em("scalar", lambda e, i=i: e.activation(
            f_sigp[i % 2][:, :], f_qp[i % 2][:, :], ACT.Sigmoid), inc=c_fsg)
        fsg_at[i] = c_fsg.n

    def s2_tail(i):
        fo = f_out[i % 2]
        b.wait("vector", c_fsg, fsg_at[i])
        b.em("vector", lambda e, i=i: e.tensor_scalar(
            f_t2[:, :], f_sigp[i % 2][:, :], 65536.0, C_RND, A.mult, A.add))
        b.em("vector", lambda e: e.tensor_scalar(
            f_u16[:, :], f_t2[:, :], -C_RND, 65535.0, A.add, A.min))
        if i >= 2:
            b.wait("vector", c_f3 if i % 2 == 0 else c_f32)
        b.em("vector", lambda e, fo=fo, i=i: e.scalar_tensor_tensor(
            fo[:, :], f_u16[:, :], float(2.0 ** -16), f_qp[i % 2][:, :], A.mult, A.mult))
        b.em("vector", lambda e, fo=fo, i=i: e.tensor_reduce(
            acc_slots[0][:, i:i + 1], fo[:, :], AX, A.max, apply_absolute_value=True),
            inc=c_f2)
        f2_at[i] = c_f2.n
        m, tq = i // (T // FW), i % (T // FW)
        b.wait("sync", c_f2, f2_at[i])
        b.em("sync", lambda e, m=m, tq=tq, fo=fo: e.dma_start(
            out=g2_hbm[m * 128:(m + 1) * 128, tq * FW:(tq + 1) * FW], in_=fo[:, :]),
            inc=(c_f3 if i % 2 == 0 else c_f32), amt=16)

    b.wait("sync", c_mst)
    b.wait("sync", c_mst2)
    b.wait("sync", c_mst3)
    b.wait("sync", c_mst4)
    s2_load(0)
    s2_load(1)
    b.wait("sync", c_mred)
    barrier("cc2", [uacc, gacc], [su_b, sg_b], [QA, QA])
    B2_DMA = c_bdma.n
    b.wait("vector", c_bdma, B2_DMA)
    for i in range(NFW):
        if i % 6 == 0:
            do_prep(1)
        s2_front(i)
        if i + 2 < NFW:
            s2_load(i + 2)
        s2_sig(i)
        if i >= 1:
            s2_tail(i - 1)
    s2_tail(NFW - 1)

    b.wait("vector", c_f2)
    b.em("vector", lambda e: e.tensor_reduce(g2acc[:, 0:1], acc_slots[0][:, 0:NFW], AX, A.max), inc=c_f2)
    f1o_at, f2o_at = {}, {}

    def o_load(i):
        m, tq = i // (T // FW), i % (T // FW)
        fin, fin2 = f_in[i % 2], f_in2[i % 2]
        if i >= 2:
            b.wait("sync", c_f1, f1o_at[i - 2])
        b.em("sync", lambda e, m=m, tq=tq, fin=fin: e.dma_start(
            out=fin[:, :], in_=g2_hbm[m * 128:(m + 1) * 128, tq * FW:(tq + 1) * FW]),
            inc=(c_fl if i % 2 == 0 else c_fl2), amt=16)
        b.em("sync", lambda e, m=m, tq=tq, fin2=fin2: e.dma_start(
            out=fin2[:, :], in_=up_hbm[m * 128:(m + 1) * 128, tq * FW:(tq + 1) * FW]),
            inc=(c_fl if i % 2 == 0 else c_fl2), amt=16)

    def o_body(i):
        m, tq = i // (T // FW), i % (T // FW)
        fin, fin2 = f_in[i % 2], f_in2[i % 2]
        fq = f_qp[0]
        b.wait("vector", c_fl if i % 2 == 0 else c_fl2)
        b.em("vector", lambda e, fin=fin: e.tensor_scalar(
            f_t[:, :], fin[:, :], sg2_b[:, 1:2], C_RND, A.mult, A.add))
        b.em("vector", lambda e, fq=fq: e.tensor_scalar(
            fq[:, :], f_t[:, :], -C_RND, sg2_b[:, 0:1], A.add, A.mult))
        b.em("vector", lambda e, fin2=fin2: e.tensor_scalar(
            f_t2[:, :], fin2[:, :], su_b[:, 1:2], C_RND, A.mult, A.add), inc=c_f1)
        f1o_at[i] = c_f1.n
        b.em("vector", lambda e: e.tensor_scalar(
            f_q2[:, :], f_t2[:, :], -C_RND, su_b[:, 0:1], A.add, A.mult))
        fo = f_out[i % 2]
        if i >= 2:
            b.wait("vector", c_f3 if i % 2 == 0 else c_f32)
        b.em("vector", lambda e, fo=fo, fq=fq: e.tensor_tensor(
            fo[:, :], fq[:, :], f_q2[:, :], A.mult))
        b.em("vector", lambda e, fo=fo, i=i: e.tensor_reduce(
            acc_slots[1][:, i:i + 1], fo[:, :], AX, A.max, apply_absolute_value=True),
            inc=c_f2)
        f2o_at[i] = c_f2.n
        b.wait("sync", c_f2, f2o_at[i])
        b.em("sync", lambda e, m=m, tq=tq, fo=fo: e.dma_start(
            out=o_hbm[m * 128:(m + 1) * 128, tq * FW:(tq + 1) * FW], in_=fo[:, :]),
            inc=(c_f3 if i % 2 == 0 else c_f32), amt=16)

    b.wait("sync", c_f3)
    b.wait("sync", c_f32)
    o_load(0)
    o_load(1)
    b.wait("sync", c_f2)
    barrier("cc3", [g2acc], [sg2_b], [QA])
    B3_DMA = c_bdma.n
    b.wait("vector", c_bdma, B3_DMA)
    for i in range(NFW):
        o_body(i)
        if i + 2 < NFW:
            o_load(i + 2)

    b.wait("vector", c_f2)
    b.em("vector", lambda e: e.tensor_reduce(oacc[:, 0:1], acc_slots[1][:, 0:NFW], AX, A.max), inc=c_f2)

    fstack.close()
    dstack = ExitStack()
    d_in = [sb(dstack, f"d_in{i}", [128, 512], F32) for i in range(6)]
    d_t = sb(dstack, "d_t", [128, 512], F32)
    oqf2 = [[sb(dstack, f"oqf{p}_{m}", [128, 512], BF16) for m in range(MC)]
            for p in range(2)]
    wdT = [sb(dstack, f"wdT{m}", [128, D], BF16) for m in range(MC)]
    xout_g = [sb(dstack, f"xoutg{i}", [128, D], F32) for i in range(2)]

    b.wait("sync", c_f3)
    b.wait("sync", c_f32)
    b.wait("sync", c_x1s)
    b.wait("sync", c_x1s2)
    b.wait("sync", c_wps, WPS_DN)
    b.wait("sync", c_wps2, WPS_DN)
    for m in range(MC):
        b.em("sync", lambda e, m=m: e.dma_start_transpose(
            wdT[m][:, :], wdn_bf[:, m * 128:(m + 1) * 128]), inc=c_wt, amt=16)
    WT_DN = c_wt.n
    b.wait("sync", c_f2)
    barrier("cc4", [oacc], [so_b], [QA])
    B4_DMA = c_bdma.n
    b.wait("vector", c_bdma, B4_DMA)
    b.wait("tensor", c_mev, MEV_FINAL)

    dq_at, dl_at, dmm_end = {}, {}, {}

    def build_oqf_part(t, mlo, mhi):
        buf = oqf2[t % 2]
        oql_chs = (c_oql, c_oql2, c_oql3, c_oql4, c_oql5, c_oql6, c_oql7, c_oql8)
        if mlo == 0 and t >= 2:
            b.wait("vector", c_dmm, dmm_end[t - 2])
        for m in range(mlo, mhi):
            mi = t * MC + m
            if mi >= 6:
                b.wait("sync", c_oq, dl_at[mi - 6])
            b.em("sync", lambda e, m=m, t=t, fin=d_in[m % 6]: e.dma_start(
                out=fin[:, :], in_=o_hbm[m * 128:(m + 1) * 128, t * TTOK:(t + 1) * TTOK]),
                inc=oql_chs[m % 6], amt=16)
        for m in range(mlo, mhi):
            fin = d_in[m % 6]
            mi = t * MC + m
            b.wait("vector", oql_chs[m % 6])
            b.em("vector", lambda e, fin=fin: e.tensor_scalar(
                d_t[:, :], fin[:, :], so_b[:, 1:2], C_RND, A.mult, A.add))
            b.em("vector", lambda e, m=m, buf=buf: e.tensor_scalar(
                buf[m][:, :], d_t[:, :], -C_RND, so_b[:, 0:1], A.add, A.mult), inc=c_oq)
            dl_at[mi] = c_oq.n
        if mhi == MC:
            dq_at[t] = c_oq.n

    for p in range(6):
        build_oqf_part(0, 4 * p, 4 * p + 4)
    for t in range(NT):
        xb = x_tm[t % 2]
        nxt = t + 1
        if t >= 2:
            b.wait("sync", c_uxf, c_uxf.n - 1)
        for g in range(G):
            b.em("sync", lambda e, t=t, g=g, xb=xb: e.dma_start(
                out=xb[:, g * D:(g + 1) * D],
                in_=x1_hbm[t * TTOK + g * 128:t * TTOK + (g + 1) * 128, :]),
                inc=(c_xload if t % 2 == 0 else c_xload2), amt=16)
        if t == 0:
            b.wait("tensor", c_wt, WT_DN)
        b.wait("tensor", c_oq, dq_at[t])
        if nxt < NT:
            build_oqf_part(nxt, 0, 4)
            build_oqf_part(nxt, 4, 8)
            build_oqf_part(nxt, 8, 12)
        for g in range(G):
            s0 = 8 * t + 2 * g
            pw0, pw1 = pmm[s0 % 6], pmm[(s0 + 1) % 6]
            for h in range(2):
                pw = (pw0, pw1)[h]
                b.wait("tensor", c_yn4, c_dmm.n - 5)
                for m in range(MC):
                    b.em("tensor", lambda e, m=m, g=g, h=h, pw=pw, t=t: e.matmul(
                        pw[:, :], oqf2[t % 2][m][:, g * 128:(g + 1) * 128],
                        wdT[m][:, h * TTOK:(h + 1) * TTOK],
                        start=(m == 0), stop=(m == MC - 1)), inc=(c_dmm if m == MC - 1 else None))
            b.wait("vector", c_dmm)
            ln_stats(c_ln4, [[(pw0, 0), (pw1, 0)]], rv4, bb4)
            b.wait("scalar", c_ln4)
            b.wait("vector", c_xload if t % 2 == 0 else c_xload2)
            xob = xout_g[g % 2]
            if t * G + g >= 2:
                b.wait("vector", c_out if g % 2 == 0 else c_out2)
            for h in range(2):
                b.em("scalar", lambda e, h=h, pw=(pw0, pw1)[h]: e.activation(
                    yn_g[h][:, :], pw[:, :], ACT.Identity,
                    bias=bb4[:, 0:1], scale=rv4[:, 0:1]), inc=c_yn4)
                b.wait("vector", c_yn4)
                b.em("vector", lambda e, g=g, h=h, xb=xb, xob=xob: e.tensor_tensor(
                    xob[:, h * 512:(h + 1) * 512], xb[:, g * D + h * 512:g * D + (h + 1) * 512],
                    yn_g[h][:, :], A.add), inc=(c_xo if h == 1 else None))
            if g == G - 1:
                b.em("vector", lambda e: e.tensor_copy(fence_v[0:1, 0:1], fence_v[0:1, 1:2]), inc=c_uxf)
            b.wait("sync", c_xo)
            b.em("sync", lambda e, t=t, g=g, xob=xob: e.dma_start(
                out=out[t * TTOK + g * 128:t * TTOK + (g + 1) * 128, :], in_=xob[:, :]),
                inc=(c_out if g % 2 == 0 else c_out2), amt=16)
            if nxt < NT:
                if g == 0:
                    build_oqf_part(nxt, 12, 16)
                elif g == 1:
                    build_oqf_part(nxt, 16, 20)
                    build_oqf_part(nxt, 20, 24)
        dmm_end[t] = c_dmm.n

    dsnap_finish()
    b.wait("sync", c_out, c_out.n)
    b.wait("sync", c_out2, c_out2.n)
    dstack.close()

    with nc.Block() as block:
        @block.sync
        def _(e):
            for f in b.ops["sync"]:
                f(e)

        @block.scalar
        def _(e):
            for f in b.ops["scalar"]:
                f(e)

        @block.vector
        def _(e):
            for f in b.ops["vector"]:
                f(e)

        @block.tensor
        def _(e):
            for f in b.ops["tensor"]:
                f(e)

        @block.gpsimd
        def _(e):
            for f in b.ops["gpsimd"]:
                f(e)

    gstack.close()
    return nc


_NC_CACHE = {}


def kernel(L0_attn_inputs, L0_attn_W_rkv, L0_attn_b_rkv, L0_attn_W_o, L0_attn_b_o,
           L0_attn_nu_log, L0_attn_gamma_log, L0_ffn_w_gate, L0_ffn_w_up, L0_ffn_w_down):
    if "main" not in _NC_CACHE:
        _NC_CACHE["main"] = build_program(debug=False)
    nc = _NC_CACHE["main"]
    x = np.ascontiguousarray(np.asarray(L0_attn_inputs, dtype=np.float32))
    shared = {
        "w_rkv": np.ascontiguousarray(np.asarray(L0_attn_W_rkv, np.float32)),
        "b_rkv": np.ascontiguousarray(np.asarray(L0_attn_b_rkv, np.float32)),
        "w_o": np.ascontiguousarray(np.asarray(L0_attn_W_o, np.float32)),
        "b_o": np.ascontiguousarray(np.asarray(L0_attn_b_o, np.float32)),
        "nu_log": np.ascontiguousarray(np.asarray(L0_attn_nu_log, np.float32)),
        "gamma_log": np.ascontiguousarray(np.asarray(L0_attn_gamma_log, np.float32)),
        "w_gate": np.ascontiguousarray(np.asarray(L0_ffn_w_gate, np.float32)),
        "w_up": np.ascontiguousarray(np.asarray(L0_ffn_w_up, np.float32)),
        "w_down": np.ascontiguousarray(np.asarray(L0_ffn_w_down, np.float32)),
    }
    in_maps = [dict(shared, x=x[i]) for i in range(NCORES)]
    res = run_bass_kernel_spmd(nc, in_maps, core_ids=list(range(NCORES)))
    return np.stack([np.asarray(res.results[i]["out"], dtype=np.float32)
                     for i in range(NCORES)], axis=0)
